# version 20
# baseline (speedup 1.0000x reference)
import numpy as np
import concourse.bass as bass
import concourse.mybir as mybir
from concourse.bass_utils import run_bass_kernel_spmd

F32 = mybir.dt.float32
I32 = mybir.dt.int32
AF = mybir.ActivationFunctionType
ALU = mybir.AluOpType
AX = mybir.AxisListType

P = 128
TT = 512
NT = 11
NG = 8
D = 1024
E = 8
FH = 2048
NGH = 16
HALO = 1536
LC = 4096
TFULL = HALO + LC
EPS = 1e-5
REAL0 = 3
ATT0 = 2

DEBUG_TAPS = False

PV = {}
_pv_names = ["ln1_g", "ln1_b", "ln2_g", "ln2_b", "mu_k", "mu_v", "mu_r",
             "td", "tf", "ffn_bias"]
for _i, _n in enumerate(_pv_names):
    PV[_n] = _i * NG
PV["cm_k"] = 10 * NG
PV["cm_r"] = 10 * NG + 64
PCOLS = 10 * NG + 128


def build():
    nc = bass.Bass(target_bir_lowering=False)

    d_xT = nc.dram_tensor("xT", [D, TFULL], F32, kind="ExternalInput").ap()
    d_wk = nc.dram_tensor("Wk", [D, D], F32, kind="ExternalInput").ap()
    d_wv = nc.dram_tensor("Wv", [D, D], F32, kind="ExternalInput").ap()
    d_wr = nc.dram_tensor("Wr", [D, D], F32, kind="ExternalInput").ap()
    d_wo = nc.dram_tensor("Wo", [D, D], F32, kind="ExternalInput").ap()
    d_rw = nc.dram_tensor("router_w_s", [P, NG * E], F32, kind="ExternalInput").ap()
    d_ewk = nc.dram_tensor("eWk", [E, D, FH], F32, kind="ExternalInput").ap()
    d_ewv = nc.dram_tensor("eWv", [E, FH, D], F32, kind="ExternalInput").ap()
    d_ewr = nc.dram_tensor("eWr", [E, D, D], F32, kind="ExternalInput").ap()
    d_ffk = nc.dram_tensor("ffk", [D, D], F32, kind="ExternalInput").ap()
    d_ffa = nc.dram_tensor("ffa_s", [P, NG * 16], F32, kind="ExternalInput").ap()
    d_ffb = nc.dram_tensor("ffb", [16, D], F32, kind="ExternalInput").ap()
    d_pvec = nc.dram_tensor("pvec", [P, PCOLS], F32, kind="ExternalInput").ap()
    d_ident = nc.dram_tensor("cident", [P, P], F32, kind="ExternalInput").ap()
    d_onescol = nc.dram_tensor("conescol", [P, 1], F32, kind="ExternalInput").ap()
    d_onesrow = nc.dram_tensor("conesrow", [1, P], F32, kind="ExternalInput").ap()
    d_inv1024 = nc.dram_tensor("cinv1024", [1, P], F32, kind="ExternalInput").ap()
    d_onehot = nc.dram_tensor("conehot", [E, D], F32, kind="ExternalInput").ap()
    d_zcol = nc.dram_tensor("czcol", [P, 1], F32, kind="ExternalInput").ap()
    d_zc16 = nc.dram_tensor("czc16", [P, 16], F32, kind="ExternalInput").ap()
    d_me = nc.dram_tensor("cme", [P, 1], F32, kind="ExternalInput").ap()
    d_eps = nc.dram_tensor("ceps", [P, 1], F32, kind="ExternalInput").ap()
    d_z01 = nc.dram_tensor("cz01", [P, 1], F32, kind="ExternalInput").ap()
    d_out = nc.dram_tensor("outT", [D, LC], F32, kind="ExternalOutput").ap()
    d_wq = nc.dram_tensor("wq_scratch", [D, D], F32, kind="Internal").ap()

    taps = {}
    if DEBUG_TAPS:
        for nm, shp in [("tap_wkv", [D, TFULL]), ("tap_x2", [D, TFULL]),
                        ("tap_ei", [D, TFULL]), ("tap_xn", [D, TFULL]),
                        ("tap_tw", [P, 32 * NT]), ("tap_acc2", [D, TFULL])]:
            taps[nm] = nc.dram_tensor(nm, shp, F32, kind="ExternalOutput").ap()

    t_xn = nc.alloc_sbuf_tensor("t_xn", [P, NG * TT], F32)
    t_xn1 = nc.alloc_sbuf_tensor("t_xn1", [P, NG * TT], F32)
    t_acc = nc.alloc_sbuf_tensor("t_acc", [P, NG * TT], F32)
    t_ei = nc.alloc_sbuf_tensor("t_ei", [P, NG * TT], F32)
    t_A = nc.alloc_sbuf_tensor("t_A", [P, NG * TT], F32)
    t_B = nc.alloc_sbuf_tensor("t_B", [P, NG * TT], F32)
    t_h = nc.alloc_sbuf_tensor("t_h", [P, NGH * TT], F32)

    s_EK = nc.alloc_sbuf_tensor("s_EK", [P, TT], F32)
    s_EV = nc.alloc_sbuf_tensor("s_EV", [P, TT], F32)
    s_BsN = nc.alloc_sbuf_tensor("s_BsN", [P, TT], F32)
    s_BsD = nc.alloc_sbuf_tensor("s_BsD", [P, TT], F32)
    s_t1 = nc.alloc_sbuf_tensor("s_t1", [P, TT], F32)
    s_pl = nc.alloc_sbuf_tensor("s_pl", [P, TT], F32)
    s_pl2 = nc.alloc_sbuf_tensor("s_pl2", [P, TT], F32)
    s_t2 = nc.alloc_sbuf_tensor("s_t2", [P, TT], F32)
    b_mean = nc.alloc_sbuf_tensor("b_mean", [P, TT], F32)
    b_var = nc.alloc_sbuf_tensor("b_var", [P, TT], F32)
    b_rstd = nc.alloc_sbuf_tensor("b_rstd", [P, TT], F32)
    b_rstd1 = nc.alloc_sbuf_tensor("b_rstd1", [P, TT], F32)

    t_acc2 = nc.alloc_sbuf_tensor("t_acc2", [P, NG * TT], F32)
    wS = [nc.alloc_sbuf_tensor(f"wS{i}", [P, 4 * P], F32) for i in range(2)]
    ewkS = [nc.alloc_sbuf_tensor(f"ewkS{i}", [P, 4 * P], F32) for i in range(2)]
    ewvS = [nc.alloc_sbuf_tensor(f"ewvS{i}", [P, NG * P], F32) for i in range(2)]
    ewrS = [nc.alloc_sbuf_tensor(f"ewrS{i}", [P, NG * P], F32) for i in range(2)]

    t_la = b_rstd1[0:16, :]
    t_sums = b_mean[0:1, :]
    t_sumsq = b_var[0:1, :]
    t_pv = nc.alloc_sbuf_tensor("t_pv", [P, PCOLS], F32)
    t_gd = nc.alloc_sbuf_tensor("t_gd", [P, NG], F32)
    t_eu = nc.alloc_sbuf_tensor("t_eu", [P, NG], F32)
    t_nk = nc.alloc_sbuf_tensor("t_nk", [P, 64], F32)
    t_nr = nc.alloc_sbuf_tensor("t_nr", [P, 64], F32)
    t_e1 = nc.alloc_sbuf_tensor("t_e1", [P, NG], F32)
    t_cN = nc.alloc_sbuf_tensor("t_cN", [P, NG], F32)
    t_cD = nc.alloc_sbuf_tensor("t_cD", [P, NG], F32)
    t_tailA = nc.alloc_sbuf_tensor("t_tailA", [P, NG], F32)
    t_tailE = nc.alloc_sbuf_tensor("t_tailE", [P, NG], F32)
    t_mtail = nc.alloc_sbuf_tensor("t_mtail", [P, NG], F32)
    t_id = nc.alloc_sbuf_tensor("t_id", [P, P], F32)
    t_oc = nc.alloc_sbuf_tensor("t_oc", [P, 1], F32)
    t_or = nc.alloc_sbuf_tensor("t_or", [1, P], F32)
    t_iv = nc.alloc_sbuf_tensor("t_iv", [1, P], F32)
    t_oh = nc.alloc_sbuf_tensor("t_oh", [E, D], F32)
    t_zc = nc.alloc_sbuf_tensor("t_zc", [P, 1], F32)
    t_me = nc.alloc_sbuf_tensor("t_me", [P, 1], F32)
    t_eps = nc.alloc_sbuf_tensor("t_eps", [P, 1], F32)
    t_z01 = nc.alloc_sbuf_tensor("t_z01", [P, 1], F32)
    t_rws = nc.alloc_sbuf_tensor("t_rws", [P, NG * E], F32)
    t_as = nc.alloc_sbuf_tensor("t_as", [P, NG * 16], F32)
    t_bs = nc.alloc_sbuf_tensor("t_bs", [16, D], F32)
    t_lg = nc.alloc_sbuf_tensor("t_lg", [P, 32], F32)
    t_m1 = nc.alloc_sbuf_tensor("t_m1", [P, 4], F32)
    t_m2 = nc.alloc_sbuf_tensor("t_m2", [P, 4], F32)
    t_d12 = nc.alloc_sbuf_tensor("t_d12", [P, 4], F32)
    t_w1 = nc.alloc_sbuf_tensor("t_w1", [P, 4], F32)
    t_mk1 = nc.alloc_sbuf_tensor("t_mk1", [P, 32], F32)
    t_mk2 = nc.alloc_sbuf_tensor("t_mk2", [P, 32], F32)
    t_twm = nc.alloc_sbuf_tensor("t_twm", [P, 32], F32)
    t_twT = nc.alloc_sbuf_tensor("t_twT", [E, TT], F32)
    t_mx8 = nc.alloc_sbuf_tensor("t_mx8", [P, NG], F32)
    t_mx1 = nc.alloc_sbuf_tensor("t_mx1", [P, 1], F32)
    t_mrow = nc.alloc_sbuf_tensor("t_mrow", [1, P], F32)
    t_M = nc.alloc_sbuf_tensor("t_M", [1, 1], F32)
    t_scB = nc.alloc_sbuf_tensor("t_scB", [P, 1], F32)
    t_isB = nc.alloc_sbuf_tensor("t_isB", [P, 1], F32)

    PSN = 5
    ps_f = [nc.alloc_psum_tensor(f"ps_f{i}", [P, TT], F32) for i in range(PSN)]
    ps_s1 = nc.alloc_psum_tensor("ps_s1", [P, TT], F32)
    ps_s2 = nc.alloc_psum_tensor("ps_s2", [P, TT], F32)
    ps_s3 = nc.alloc_psum_tensor("ps_s3", [P, TT], F32)

    prog = {k: [] for k in ["pe", "dve", "act", "sp", "pl"]}
    WCH = ["w0", "w1", "ek0", "ek1", "ev0", "ev1", "er0", "er1"]
    cnt = {"pe": 0, "dve": 0, "act": 0, "ld": 0, "q": 0, "x": 0, "st": 0, "plc": 0}
    for _c in WCH:
        cnt[_c] = 0

    def emit(q, fn):
        prog[q].append(("op", fn, None, 0))

    def emiti(q, fn, ch, amt=1):
        if (q in ("dve", "act") and ch == q) or (q == "pl" and ch == "plc"):
            prog[q].append(("wait", ch, cnt[ch], 0))
        cnt[ch] += amt
        prog[q].append(("op", fn, ch, amt))

    def wait(q, ch, val):
        if val > 0:
            prog[q].append(("wait", ch, val, 0))

    bank_free = [("dve", 0)] * PSN
    bank_i = [0]

    def next_bank():
        i = bank_i[0] % PSN
        bank_i[0] += 1
        ch, v = bank_free[i]
        wait("pe", ch, v)
        return i

    def bank_consumed(i, ch):
        bank_free[i] = (ch, cnt[ch])

    def dma(dst, src):
        return lambda e: e.dma_start(dst, src)

    def fw(tens, g):
        return tens[:, g * TT:(g + 1) * TT]

    def fw3(tens, ngroups=NG):
        return tens.ap().rearrange("p (g n) -> p g n", n=TT)[:, 0:ngroups, :]

    def pcol(name, g):
        c = PV[name] + g
        return t_pv[:, c:c + 1]

    def wchunk(buf, kc):
        return buf[:, kc * P:(kc + 1) * P]

    def mblock_src(w_ap, og, kc0=0, nkc=None):
        if nkc is None:
            sl = w_ap[:, og * P:(og + 1) * P]
        else:
            sl = w_ap[kc0 * P:(kc0 + nkc) * P, og * P:(og + 1) * P]
        return sl.rearrange("(c p) m -> p c m", p=P)

    def wblock_dst(buf, nkc):
        return buf.ap().rearrange("p (c m) -> p c m", m=P)[:, 0:nkc, :]

    for dst, src in [(t_pv[:, :], d_pvec), (t_id[:, :], d_ident),
                     (t_oc[:, :], d_onescol), (t_or[:, :], d_onesrow),
                     (t_iv[:, :], d_inv1024), (t_oh[:, :], d_onehot),
                     (t_zc[:, :], d_zcol), (t_eps[:, :], d_eps),
                     (t_cN[:, :], d_zc16[:, 0:8]),
                     (t_cD[:, :], d_zc16[:, 8:16]), (t_me[:, :], d_me),
                     (t_z01[:, :], d_z01),
                     (t_rws[:, :], d_rw),
                     (t_as[:, :], d_ffa),
                     (t_bs[:, :], d_ffb)]:
        emiti("sp", dma(dst, src), "ld", 16)
    SETUP_LD = cnt["ld"]

    wait("act", "ld", SETUP_LD)
    emiti("act", lambda e: e.activation(t_e1[:, :], t_pv[:, PV["td"]:PV["td"] + NG], AF.Exp), "act")
    wait("act", "act", cnt["act"])
    emiti("act", lambda e: e.activation(t_gd[:, :], t_e1[:, :], AF.Exp, scale=-1.0), "act")
    emiti("act", lambda e: e.activation(t_eu[:, :], t_pv[:, PV["tf"]:PV["tf"] + NG], AF.Exp), "act")
    SETUP_ACT = cnt["act"]
    wait("dve", "ld", SETUP_LD)
    emiti("dve", lambda e: e.tensor_scalar(t_nk[:, :], t_pv[:, PV["cm_k"]:PV["cm_k"] + 64], 1.0, None, ALU.subtract), "dve")
    emiti("dve", lambda e: e.tensor_scalar(t_nr[:, :], t_pv[:, PV["cm_r"]:PV["cm_r"] + 64], 1.0, None, ALU.subtract), "dve")
    wait("dve", "dve", cnt["dve"])

    QLOAD = ewrS[1][:, 0:D]
    for kc in range(NG):
        emiti("sp", dma(QLOAD, d_ffk[kc * P:(kc + 1) * P, :]), "q", 16)
        wait("dve", "q", cnt["q"])
        emiti("dve", (lambda kc: lambda e: e.reduce_max(
            t_mx8[:, kc:kc + 1], QLOAD, axis=AX.X, apply_absolute_value=True))(kc), "dve")
        wait("sp", "dve", cnt["dve"])
    emiti("dve", lambda e: e.reduce_max(t_mx1[:, :], t_mx8[:, :], axis=AX.X), "dve")
    wait("pe", "ld", SETUP_LD)
    wait("pe", "dve", cnt["dve"])
    emiti("pe", lambda e: e.matmul(ps_s1[0:1, 0:P], t_mx1[:, :], t_id[:, :], start=True, stop=True), "pe")
    wait("dve", "pe", cnt["pe"])
    emiti("dve", lambda e: e.tensor_copy(t_mrow[:, :], ps_s1[0:1, 0:P]), "dve")
    wait("dve", "dve", cnt["dve"])
    emiti("dve", lambda e: e.reduce_max(t_M[:, :], t_mrow[:, :], axis=AX.X), "dve")
    wait("pe", "dve", cnt["dve"])
    emiti("pe", lambda e: e.matmul(ps_s3[0:P, 0:1], t_or[:, :], t_M[:, :], start=True, stop=True), "pe")
    wait("dve", "pe", cnt["pe"])
    emiti("dve", lambda e: e.tensor_scalar(t_scB[:, :], ps_s3[0:P, 0:1], 1.0 / 7.0, None, ALU.mult), "dve")
    wait("dve", "dve", cnt["dve"])
    emiti("dve", lambda e: e.reciprocal(t_isB[:, :], t_scB[:, :]), "dve")
    wait("pl", "dve", cnt["dve"])
    QU = ewvS[0][:, 0:D]
    QP = ewvS[1][:, 0:D]
    QQ = ewrS[0][:, 0:D]
    for kc in range(NG):
        wait("sp", "q", cnt["q"])
        emiti("sp", dma(QLOAD, d_ffk[kc * P:(kc + 1) * P, :]), "q", 16)
        wait("dve", "q", cnt["q"])
        emiti("dve", lambda e: e.tensor_scalar(QU, QLOAD, t_isB[:, 0:1], None, ALU.mult), "dve")
        emiti("dve", lambda e: e.tensor_scalar(QP, QU, -7.5, None, ALU.is_ge), "dve")
        cur, nxt = QP, QQ
        for j in range(-7, 7):
            thr = j + 0.5
            emiti("dve", (lambda thr, cur, nxt: lambda e: e.scalar_tensor_tensor(
                nxt, QU, thr, cur, ALU.is_ge, ALU.add))(thr, cur, nxt), "dve")
            cur, nxt = nxt, cur
        emiti("dve", (lambda cur, nxt: lambda e: e.tensor_scalar(
            nxt, cur, 8.0, None, ALU.subtract))(cur, nxt), "dve")
        q8 = nxt
        emiti("dve", (lambda q8: lambda e: e.tensor_scalar(
            QU, q8, t_scB[:, 0:1], None, ALU.mult))(q8), "dve")
        wait("sp", "dve", cnt["dve"])
        emiti("sp", dma(d_wq[kc * P:(kc + 1) * P, :], QU), "q", 16)
        wait("dve", "q", cnt["q"])
    WQ_LD = cnt["q"]

    wchan = {id(wS[0]): "w0", id(wS[1]): "w1",
             id(ewkS[0]): "ek0", id(ewkS[1]): "ek1",
             id(ewvS[0]): "ev0", id(ewvS[1]): "ev1",
             id(ewrS[0]): "er0", id(ewrS[1]): "er1"}
    wfree = {id(wS[0]): ("pe", 0), id(wS[1]): ("pe", 0),
             id(ewkS[0]): ("pe", 0), id(ewkS[1]): ("pe", 0),
             id(ewvS[0]): ("q", WQ_LD), id(ewvS[1]): ("q", WQ_LD),
             id(ewrS[0]): ("q", WQ_LD), id(ewrS[1]): ("q", WQ_LD)}

    def stream_load(buf, dst3, src_ap):
        ch, v = wfree[id(buf)]
        wait("sp", ch, v)
        wc = wchan[id(buf)]
        emiti("sp", dma(dst3, src_ap), wc, 16)
        return (wc, cnt[wc])

    def consumed(buf, ch):
        wfree[id(buf)] = (ch, cnt[ch])

    def half_proj(w_dram, og, rhs_fw, bi, first_extra_wait=None):
        for h in range(2):
            buf = wS[wslot[0] % 2]; wslot[0] += 1
            ldv = stream_load(buf, wblock_dst(buf, 4), mblock_src(w_dram, og, h * 4, 4))
            wait("pe", ldv[0], ldv[1])
            if h == 0 and first_extra_wait is not None:
                wait("pe", "dve", first_extra_wait)
            for kc in range(4):
                emiti("pe", (lambda buf, kc, bi, h: lambda e: e.matmul(
                    ps_f[bi][:, :], wchunk(buf, kc), rhs_fw(h * 4 + kc),
                    start=(h == 0 and kc == 0), stop=(h == 1 and kc == 3)))(buf, kc, bi, h), "pe")
            consumed(buf, "pe")

    wait("dve", "act", SETUP_ACT)
    wait("act", "ld", SETUP_LD)
    wait("sp", "q", WQ_LD)

    xload_ld = [0]
    store_done = [0]
    stat_free = [0]
    ps2_free = [0]
    ps3_free = [0]
    bfree_pe = [0]
    pe_sq = [0] * NG
    tile_end_dve = [0]
    wslot = [0]
    ei_done_s = [0]

    emiti("pl", dma(fw3(t_xn1), d_xT.rearrange("(g p) n -> p g n", p=P)[:, :, 0:TT]), "x", 16)
    xload_ld[0] = cnt["x"]

    def ln_stats(src_fw, prev_dve_guard):
        wait("pe", "dve", stat_free[0])
        wait("pe", "dve", prev_dve_guard)
        for g in range(NG):
            emiti("pe", (lambda g: lambda e: e.matmul(
                ps_s1[0:1, :], t_oc[:, :], src_fw(g), start=(g == 0), stop=(g == NG - 1)))(g), "pe")
        wait("act", "dve", prev_dve_guard)
        for g in range(NG):
            sq = s_t1 if g % 2 == 0 else s_t2
            if g >= 2:
                wait("act", "pe", pe_sq[g - 2])
            emiti("act", (lambda g, sq: lambda e: e.activation(sq[:, :], src_fw(g), AF.Square))(g, sq), "act")
            wait("pe", "act", cnt["act"])
            emiti("pe", (lambda g, sq: lambda e: e.matmul(
                ps_s2[0:1, :], t_oc[:, :], sq[:, :], start=(g == 0), stop=(g == NG - 1)))(g, sq), "pe")
            pe_sq[g] = cnt["pe"]
        wait("dve", "pe", cnt["pe"])
        emiti("dve", lambda e: e.tensor_copy(t_sums, ps_s1[0:1, :]), "dve")
        emiti("dve", lambda e: e.tensor_copy(t_sumsq, ps_s2[0:1, :]), "dve")
        wait("dve", "dve", cnt["dve"])
        wait("pe", "dve", cnt["dve"])
        emiti("pe", lambda e: e.matmul(ps_s1[:, :], t_iv[:, :], t_sums, start=True, stop=True), "pe")
        emiti("pe", lambda e: e.matmul(ps_s2[:, :], t_iv[:, :], t_sumsq, start=True, stop=True), "pe")
        wait("dve", "pe", cnt["pe"])
        emiti("dve", lambda e: e.tensor_copy(b_mean[:, :], ps_s1[:, :]), "dve")
        emiti("dve", lambda e: e.tensor_copy(b_var[:, :], ps_s2[:, :]), "dve")
        stat_free[0] = cnt["dve"]
        emiti("dve", lambda e: e.tensor_mul(s_t1[:, :], b_mean[:, :], b_mean[:, :]), "dve")
        emiti("dve", lambda e: e.tensor_sub(s_t2[:, :], b_var[:, :], s_t1[:, :]), "dve")
        wait("act", "dve", cnt["dve"])
        emiti("act", lambda e: e.activation(s_t1[:, :], s_t2[:, :], AF.Sqrt, bias=t_eps[:, 0:1]), "act")
        wait("dve", "act", cnt["act"])
        emiti("dve", lambda e: e.reciprocal(b_rstd[:, :], s_t1[:, :]), "dve")

    def emit_tile(t):
        is_att = t >= ATT0
        is_real = t >= REAL0
        x_t = t_xn1
        xld = xload_ld[0]

        wait("pe", "x", xld)
        wait("act", "x", xld)
        ln_stats(lambda g: fw(x_t, g), tile_end_dve[0])
        emiti("dve", lambda e: e.tensor_mul(s_t1[:, :], b_rstd[:, :], b_rstd[:, :]), "dve")
        emiti("dve", lambda e: e.tensor_mul(b_rstd1[:, :], s_t1[:, :], s_t2[:, :]), "dve")
        wait("act", "dve", cnt["dve"])
        emiti("act", lambda e: e.activation(s_t2[:, :], b_rstd1[:, :], AF.Sqrt, bias=t_eps[:, 0:1]), "act")
        wait("dve", "act", cnt["act"])
        emiti("dve", lambda e: e.reciprocal(b_rstd1[:, :], s_t2[:, :]), "dve")
        for g in range(NG):
            emiti("dve", (lambda g: lambda e: e.tensor_sub(s_t1[:, :], fw(x_t, g), b_mean[:, :]))(g), "dve")
            emiti("dve", (lambda g: lambda e: e.tensor_mul(fw(t_xn, g), s_t1[:, :], b_rstd[:, :]))(g), "dve")
        if is_real:
            wait("dve", "st", store_done[0])
            for g in range(NG):
                emiti("dve", (lambda g: lambda e: e.scalar_tensor_tensor(
                    fw(t_acc, g), fw(x_t, g), pcol("ffn_bias", g), fw(t_xn, g),
                    ALU.add, ALU.add))(g), "dve")
        acc_init = cnt["dve"]
        for g in range(NG):
            emiti("dve", (lambda g: lambda e: e.tensor_mul(s_t1[:, :], fw(t_xn, g), b_rstd1[:, :]))(g), "dve")
            emiti("dve", (lambda g: lambda e: e.tensor_scalar(
                fw(t_xn1, g), s_t1[:, :], pcol("ln1_g", g), pcol("ln1_b", g),
                ALU.mult, ALU.add))(g), "dve")
        xn1_done = cnt["dve"]
        xn_done = cnt["dve"]
        if DEBUG_TAPS:
            wait("pl", "dve", xn_done)
            emiti("pl", (lambda t: lambda e: e.dma_start(
                taps["tap_xn"].rearrange("(g p) n -> p g n", p=P)[:, :, t * TT:(t + 1) * TT],
                fw3(t_xn)))(t), "st", 16)

        if is_real:
            wait("pe", "dve", xn_done)
            wait("pe", "dve", ps3_free[0])
            for g in range(NG):
                emiti("pe", (lambda g: lambda e: e.matmul(
                    ps_s3[0:16, :], t_as[:, g * 16:(g + 1) * 16], fw(t_xn, g),
                    start=(g == 0), stop=(g == NG - 1)))(g), "pe")
            wait("dve", "pe", cnt["pe"])
            emiti("dve", lambda e: e.tensor_copy(t_la, ps_s3[0:16, :]), "dve")
            wait("dve", "dve", cnt["dve"])
            ps3_free[0] = cnt["dve"]
            la_done = cnt["dve"]
            for og in range(NG):
                bi = next_bank()
                for h in range(2):
                    buf = wS[wslot[0] % 2]; wslot[0] += 1
                    ldw = stream_load(buf, wblock_dst(buf, 4), mblock_src(d_wq, og, h * 4, 4))
                    wait("pe", ldw[0], ldw[1])
                    if h == 0 and og == 0:
                        wait("pe", "dve", la_done)
                    for kc in range(4):
                        emiti("pe", (lambda buf, kc, bi, h: lambda e: e.matmul(
                            ps_f[bi][:, :], wchunk(buf, kc), fw(t_xn, h * 4 + kc),
                            start=(h == 0 and kc == 0), stop=False))(buf, kc, bi, h), "pe")
                    consumed(buf, "pe")
                emiti("pe", (lambda og, bi: lambda e: e.matmul(
                    ps_f[bi][:, :], t_bs[:, og * P:(og + 1) * P], t_la,
                    start=False, stop=True))(og, bi), "pe")
                wait("dve", "pe", cnt["pe"])
                wait("dve", "dve", acc_init)
                emiti("dve", (lambda og, bi: lambda e: e.tensor_add(
                    fw(t_acc2, og), fw(t_acc, og), ps_f[bi][:, :]))(og, bi), "dve")
                bank_consumed(bi, "dve")

        tail_src = t_mtail if t == REAL0 else t_tailA
        if t == REAL0:
            wait("dve", "dve", xn1_done)
            emiti("dve", lambda e: e.tensor_scalar(
                t_mtail[:, :], t_tailA[:, :], t_z01[:, 0:1], None, ALU.mult), "dve")
            wait("dve", "dve", cnt["dve"])
        wait("dve", "dve", xn1_done)
        for g in range(NG):
            xx0 = t_zc[:, 0:1] if t == 0 else tail_src[:, g:g + 1]
            emiti("dve", (lambda g, xx0: lambda e: e.tensor_sub(
                s_t1[:, 0:1], fw(t_xn1, g)[:, 0:1], xx0))(g, xx0), "dve")
            emiti("dve", (lambda g: lambda e: e.tensor_sub(
                s_t1[:, 1:TT], fw(t_xn1, g)[:, 1:TT], fw(t_xn1, g)[:, 0:TT - 1]))(g), "dve")
            for name, dst in (("mu_k", t_A), ("mu_v", t_B)):
                emiti("dve", (lambda g, name, dst, xx0: lambda e: e.scalar_tensor_tensor(
                    fw(dst, g)[:, 0:1], s_t1[:, 0:1], pcol(name, g), xx0,
                    ALU.mult, ALU.add))(g, name, dst, xx0), "dve")
                emiti("dve", (lambda g, name, dst: lambda e: e.scalar_tensor_tensor(
                    fw(dst, g)[:, 1:TT], s_t1[:, 1:TT], pcol(name, g),
                    fw(t_xn1, g)[:, 0:TT - 1], ALU.mult, ALU.add))(g, name, dst), "dve")
        mix_kv = cnt["dve"]

        ek_guard = [mix_kv]
        wait("pe", "dve", mix_kv)
        for og in range(NG):
            bik = next_bank()
            half_proj(d_wk, og, lambda kc: fw(t_A, kc), bik)
            kdone = cnt["pe"]
            biv = next_bank()
            half_proj(d_wv, og, lambda kc: fw(t_B, kc), biv)
            vdone = cnt["pe"]
            wait("act", "pe", kdone)
            wait("act", "dve", ek_guard[0])
            emiti("act", (lambda bik: lambda e: e.activation(s_EK[:, :], ps_f[bik][:, :], AF.Exp))(bik), "act")
            bank_consumed(bik, "act")
            wait("dve", "act", cnt["act"])
            wait("dve", "pe", vdone)
            emiti("dve", (lambda biv: lambda e: e.tensor_mul(s_EV[:, :], s_EK[:, :], ps_f[biv][:, :]))(biv), "dve")
            bank_consumed(biv, "dve")
            gbb = t_gd[:, og:og + 1].broadcast_to([P, TT])
            if t < REAL0:
                emiti("dve", lambda e: e.tensor_scalar(s_t2[:, :], s_EV[:, :], t_me[:, 0:1], None, ALU.mult), "dve")
                emiti("dve", lambda e: e.tensor_scalar(s_t1[:, :], s_EK[:, :], t_me[:, 0:1], None, ALU.mult), "dve")
                dsrc = s_t1
                nsrc = s_t2
            else:
                dsrc = s_EK
                nsrc = s_EV
            emiti("dve", (lambda og, gbb, nsrc: lambda e: e.tensor_tensor_scan(
                s_BsN[:, :], gbb, nsrc[:, :], t_cN[:, og:og + 1], ALU.mult, ALU.add))(og, gbb, nsrc), "dve")
            emiti("dve", (lambda og, gbb, dsrc: lambda e: e.tensor_tensor_scan(
                s_BsD[:, :], gbb, dsrc[:, :], t_cD[:, og:og + 1], ALU.mult, ALU.add))(og, gbb, dsrc), "dve")
            if is_att:
                emiti("dve", (lambda og: lambda e: e.scalar_tensor_tensor(
                    s_t1[:, 0:1], s_EV[:, 0:1], t_eu[:, og:og + 1], t_cN[:, og:og + 1],
                    ALU.mult, ALU.add))(og), "dve")
                emiti("dve", (lambda og: lambda e: e.scalar_tensor_tensor(
                    s_t1[:, 1:TT], s_EV[:, 1:TT], t_eu[:, og:og + 1], s_BsN[:, 0:TT - 1],
                    ALU.mult, ALU.add))(og), "dve")
                emiti("dve", (lambda og: lambda e: e.scalar_tensor_tensor(
                    s_t2[:, 0:1], s_EK[:, 0:1], t_eu[:, og:og + 1], t_cD[:, og:og + 1],
                    ALU.mult, ALU.add))(og), "dve")
                emiti("dve", (lambda og: lambda e: e.scalar_tensor_tensor(
                    s_t2[:, 1:TT], s_EK[:, 1:TT], t_eu[:, og:og + 1], s_BsD[:, 0:TT - 1],
                    ALU.mult, ALU.add))(og), "dve")
                emiti("dve", lambda e: e.reciprocal(s_EK[:, :], s_t2[:, :]), "dve")
                emiti("dve", (lambda og: lambda e: e.tensor_mul(
                    fw(t_h, og), s_t1[:, :], s_EK[:, :]))(og), "dve")
            ek_guard[0] = cnt["dve"]
            emiti("dve", (lambda og: lambda e: e.tensor_copy(
                t_cN[:, og:og + 1], s_BsN[:, TT - 1:TT]))(og), "dve")
            emiti("dve", (lambda og: lambda e: e.tensor_copy(
                t_cD[:, og:og + 1], s_BsD[:, TT - 1:TT]))(og), "dve")
            wait("dve", "dve", cnt["dve"])
        scan_done = cnt["dve"]

        kv_pe_end = cnt["pe"]
        if is_att:
            wait("dve", "pe", kv_pe_end)
            for g in range(NG):
                xx0 = t_zc[:, 0:1] if t == 0 else tail_src[:, g:g + 1]
                emiti("dve", (lambda g, xx0: lambda e: e.tensor_sub(
                    s_pl[:, 0:1], fw(t_xn1, g)[:, 0:1], xx0))(g, xx0), "dve")
                emiti("dve", (lambda g: lambda e: e.tensor_sub(
                    s_pl[:, 1:TT], fw(t_xn1, g)[:, 1:TT], fw(t_xn1, g)[:, 0:TT - 1]))(g), "dve")
                emiti("dve", (lambda g, xx0: lambda e: e.scalar_tensor_tensor(
                    fw(t_A, g)[:, 0:1], s_pl[:, 0:1], pcol("mu_r", g), xx0,
                    ALU.mult, ALU.add))(g, xx0), "dve")
                emiti("dve", (lambda g: lambda e: e.scalar_tensor_tensor(
                    fw(t_A, g)[:, 1:TT], s_pl[:, 1:TT], pcol("mu_r", g),
                    fw(t_xn1, g)[:, 0:TT - 1], ALU.mult, ALU.add))(g), "dve")
        xr_done = cnt["dve"]
        emiti("dve", lambda e: e.tensor_copy(
            t_tailA[:, :], fw3(t_xn1)[:, :, TT - 1:TT]), "dve")
        tailA_done = cnt["dve"]
        if t + 1 < NT:
            wait("pl", "dve", tailA_done)
            emiti("pl", (lambda t: lambda e: e.dma_start(
                fw3(t_xn1), d_xT.rearrange("(g p) n -> p g n", p=P)[:, :, (t + 1) * TT:(t + 2) * TT]))(t), "x", 16)
            xload_ld[0] = cnt["x"]

        if is_att:
            if DEBUG_TAPS:
                wait("pl", "dve", scan_done)
                emiti("pl", (lambda t: lambda e: e.dma_start(
                    taps["tap_wkv"].rearrange("(g p) n -> p g n", p=P)[:, :, t * TT:(t + 1) * TT],
                    fw3(t_h)))(t), "st", 16)
            rw_guard = [scan_done, scan_done]
            wait("pe", "dve", xr_done)
            for og in range(NG):
                bir = next_bank()
                half_proj(d_wr, og, lambda kc: fw(t_A, kc), bir)
                rbuf = s_t1 if og % 2 == 0 else s_t2
                wait("act", "pe", cnt["pe"])
                wait("act", "dve", rw_guard[og % 2])
                emiti("act", (lambda bir, rbuf: lambda e: e.activation(
                    rbuf[:, :], ps_f[bir][:, :], AF.Sigmoid))(bir, rbuf), "act")
                bank_consumed(bir, "act")
                wait("dve", "act", cnt["act"])
                if DEBUG_TAPS:
                    wait("dve", "st", cnt["st"])
                emiti("dve", (lambda og, rbuf: lambda e: e.tensor_mul(
                    fw(t_h, NG + og), fw(t_h, og), rbuf[:, :]))(og, rbuf), "dve")
                rw_guard[og % 2] = cnt["dve"]
            rw_done = cnt["dve"]

            wait("pe", "dve", rw_done)
            for og in range(NG):
                bio = next_bank()
                half_proj(d_wo, og, lambda kc: fw(t_h, NG + kc), bio)
                wait("dve", "pe", cnt["pe"])
                if is_real:
                    emiti("dve", (lambda og, bio: lambda e: e.tensor_add(
                        fw(t_acc, og), fw(t_acc2, og), ps_f[bio][:, :]))(og, bio), "dve")
                emiti("dve", (lambda og, bio: lambda e: e.tensor_add(
                    fw(t_acc2, og), fw(t_xn, og), ps_f[bio][:, :]))(og, bio), "dve")
                bank_consumed(bio, "dve")
            x2_done = cnt["dve"]
            if DEBUG_TAPS and is_real:
                wait("pl", "dve", x2_done)
                emiti("pl", (lambda t: lambda e: e.dma_start(
                    taps["tap_acc2"].rearrange("(g p) n -> p g n", p=P)[:, :, t * TT:(t + 1) * TT],
                    fw3(t_acc)))(t), "st", 16)
            if DEBUG_TAPS:
                wait("pl", "dve", x2_done)
                emiti("pl", (lambda t: lambda e: e.dma_start(
                    taps["tap_x2"].rearrange("(g p) n -> p g n", p=P)[:, :, t * TT:(t + 1) * TT],
                    fw3(t_acc2)))(t), "st", 16)

            ln_stats(lambda g: fw(t_acc2, g), x2_done)
            for g in range(NG):
                emiti("dve", (lambda g: lambda e: e.tensor_sub(
                    fw(t_ei, g), fw(t_acc2, g), b_mean[:, :]))(g), "dve")
                emiti("dve", (lambda g: lambda e: e.tensor_mul(
                    s_t1[:, :], fw(t_ei, g), b_rstd[:, :]))(g), "dve")
                emiti("dve", (lambda g: lambda e: e.tensor_scalar(
                    fw(t_ei, g), s_t1[:, :], pcol("ln2_g", g), pcol("ln2_b", g),
                    ALU.mult, ALU.add))(g), "dve")
            ei_done_s[0] = cnt["dve"]
            if DEBUG_TAPS:
                wait("pl", "dve", ei_done_s[0])
                emiti("pl", (lambda t: lambda e: e.dma_start(
                    taps["tap_ei"].rearrange("(g p) n -> p g n", p=P)[:, :, t * TT:(t + 1) * TT],
                    fw3(t_ei)))(t), "st", 16)

        if is_real:
            ei_done = ei_done_s[0]
            wait("pe", "dve", ei_done)
            wait("pe", "dve", stat_free[0])
            for sb in range(4):
                for g in range(NG):
                    emiti("pe", (lambda sb, g: lambda e: e.matmul(
                        ps_s1[0:P, sb * 8:(sb + 1) * 8],
                        fw(t_ei, g)[:, sb * P:(sb + 1) * P],
                        t_rws[:, g * 8:(g + 1) * 8],
                        start=(g == 0), stop=(g == NG - 1)))(sb, g), "pe")
            wait("dve", "pe", cnt["pe"])
            emiti("dve", lambda e: e.tensor_copy(t_lg[:, :], ps_s1[0:P, 0:32]), "dve")
            wait("dve", "dve", cnt["dve"])
            lg3 = t_lg.ap().rearrange("p (c e) -> p c e", e=E)
            mk13 = t_mk1.ap().rearrange("p (c e) -> p c e", e=E)
            mk23 = t_mk2.ap().rearrange("p (c e) -> p c e", e=E)
            tw3 = t_twm.ap().rearrange("p (c e) -> p c e", e=E)
            emiti("dve", lambda e: e.reduce_max(t_m1[:, :], lg3, axis=AX.X), "dve")
            wait("dve", "dve", cnt["dve"])
            emiti("dve", lambda e: e.tensor_tensor(
                mk13, lg3, t_m1.ap().unsqueeze(2).broadcast_to([P, 4, E]), ALU.is_equal), "dve")
            emiti("dve", lambda e: e.scalar_tensor_tensor(
                t_mk2[:, :], t_mk1[:, :], -1e30, t_lg[:, :], ALU.mult, ALU.add), "dve")
            wait("dve", "dve", cnt["dve"])
            emiti("dve", lambda e: e.reduce_max(t_m2[:, :], mk23, axis=AX.X), "dve")
            wait("dve", "dve", cnt["dve"])
            emiti("dve", lambda e: e.tensor_tensor(
                mk23, lg3, t_m2.ap().unsqueeze(2).broadcast_to([P, 4, E]), ALU.is_equal), "dve")
            emiti("dve", lambda e: e.tensor_sub(t_d12[:, :], t_m1[:, :], t_m2[:, :]), "dve")
            wait("act", "dve", cnt["dve"])
            emiti("act", lambda e: e.activation(t_w1[:, :], t_d12[:, :], AF.Sigmoid), "act")
            wait("dve", "act", cnt["act"])
            emiti("dve", lambda e: e.tensor_sub(t_lg[:, :], t_mk1[:, :], t_mk2[:, :]), "dve")
            emiti("dve", lambda e: e.tensor_tensor(
                t_twm.ap().rearrange("p (c e) -> p c e", e=E), t_lg.ap().rearrange("p (c e) -> p c e", e=E),
                t_w1.ap().unsqueeze(2).broadcast_to([P, 4, E]), ALU.mult), "dve")
            emiti("dve", lambda e: e.tensor_add(t_lg[:, :], t_twm[:, :], t_mk2[:, :]), "dve")
            emiti("dve", lambda e: e.tensor_copy(t_twm[:, :], t_lg[:, :]), "dve")
            wait("dve", "dve", cnt["dve"])
            tw_done = cnt["dve"]
            if DEBUG_TAPS:
                wait("pl", "dve", tw_done)
                emiti("pl", (lambda t: lambda e: e.dma_start(
                    taps["tap_tw"][:, t * 32:(t + 1) * 32], t_twm[:, :]))(t), "st", 16)
            wait("pe", "dve", tw_done)
            for sb in range(4):
                emiti("pe", (lambda sb: lambda e: e.transpose(
                    ps_s1[0:8, sb * P:(sb + 1) * P], t_twm[:, sb * 8:(sb + 1) * 8],
                    t_id[:, :]))(sb), "pe")
            wait("dve", "pe", cnt["pe"])
            emiti("dve", lambda e: e.tensor_copy(t_twT[:, :], ps_s1[0:8, :]), "dve")
            wait("dve", "dve", cnt["dve"])
            stat_free[0] = cnt["dve"]
            twT_done = cnt["dve"]

            etail = t_mtail if t == REAL0 else t_tailE
            if t == REAL0:
                emiti("dve", lambda e: e.tensor_scalar(
                    t_mtail[:, :], t_tailE[:, :], t_z01[:, 0:1], None, ALU.mult), "dve")
                wait("dve", "dve", cnt["dve"])
            for g in range(NG):
                emiti("dve", (lambda g, etail: lambda e: e.tensor_sub(
                    fw(t_A, g)[:, 0:1], fw(t_ei, g)[:, 0:1], etail[:, g:g + 1]))(g, etail), "dve")
                emiti("dve", (lambda g: lambda e: e.tensor_sub(
                    fw(t_A, g)[:, 1:TT], fw(t_ei, g)[:, 1:TT], fw(t_ei, g)[:, 0:TT - 1]))(g), "dve")
            d2_done = cnt["dve"]

            for ex in range(E):
                wait("pe", "dve", twT_done)
                wait("pe", "dve", ps2_free[0])
                emiti("pe", (lambda ex: lambda e: e.matmul(
                    ps_s2[:, :], t_oh[:, ex * P:(ex + 1) * P], t_twT[:, :],
                    start=True, stop=True))(ex), "pe")
                wait("dve", "dve", d2_done)
                wait("dve", "pe", bfree_pe[0])
                for g in range(NG):
                    emiti("dve", (lambda ex, g: lambda e: e.scalar_tensor_tensor(
                        fw(t_B, g), fw(t_A, g), t_nk[:, g * 8 + ex:g * 8 + ex + 1],
                        fw(t_ei, g), ALU.mult, ALU.add))(ex, g), "dve")
                xk_pl = cnt["dve"]
                wait("pe", "dve", xk_pl)
                hsq_guard = [0, 0]
                for og in range(NGH):
                    bi = next_bank()
                    for hh in range(2):
                        buf = ewkS[hh]
                        ldh = stream_load(buf, wblock_dst(buf, 4),
                                          d_ewk[ex, hh * 4 * P:(hh + 1) * 4 * P,
                                                og * P:(og + 1) * P].rearrange("(c p) m -> p c m", p=P))
                        wait("pe", ldh[0], ldh[1])
                        for kc in range(4):
                            emiti("pe", (lambda buf, kc, bi, hh: lambda e: e.matmul(
                                ps_f[bi][:, :], wchunk(buf, kc), fw(t_B, hh * 4 + kc),
                                start=(hh == 0 and kc == 0), stop=(hh == 1 and kc == 3)))(buf, kc, bi, hh), "pe")
                        consumed(buf, "pe")
                    rlb = s_BsN if og % 2 == 0 else s_BsD
                    wait("act", "pe", cnt["pe"])
                    wait("act", "dve", hsq_guard[og % 2])
                    emiti("act", (lambda bi, rlb: lambda e: e.activation(
                        rlb[:, :], ps_f[bi][:, :], AF.Relu))(bi, rlb), "act")
                    bank_consumed(bi, "act")
                    wait("dve", "act", cnt["act"])
                    emiti("dve", (lambda og, rlb: lambda e: e.tensor_mul(
                        fw(t_h, og), rlb[:, :], rlb[:, :]))(og, rlb), "dve")
                    hsq_guard[og % 2] = cnt["dve"]
                h_pe = cnt["pe"]
                h_done = cnt["dve"]
                wait("dve", "pe", h_pe)
                for g in range(NG):
                    emiti("dve", (lambda ex, g: lambda e: e.scalar_tensor_tensor(
                        fw(t_B, g), fw(t_A, g), t_nr[:, g * 8 + ex:g * 8 + ex + 1],
                        fw(t_ei, g), ALU.mult, ALU.add))(ex, g), "dve")
                xr_pl = cnt["dve"]
                rsig_guard = [h_done, h_done]
                for og in range(NG):
                    bir = next_bank()
                    buf = ewrS[og % 2]
                    ldr = stream_load(buf, wblock_dst(buf, NG),
                                      d_ewr[ex, :, og * P:(og + 1) * P].rearrange("(c p) m -> p c m", p=P))
                    wait("pe", ldr[0], ldr[1])
                    if og == 0:
                        wait("pe", "dve", xr_pl)
                    for kc in range(NG):
                        emiti("pe", (lambda buf, kc, bir: lambda e: e.matmul(
                            ps_f[bir][:, :], wchunk(buf, kc), fw(t_B, kc),
                            start=(kc == 0), stop=(kc == NG - 1)))(buf, kc, bir), "pe")
                    consumed(buf, "pe")
                    r_mm_pe = cnt["pe"]
                    rsb = s_BsN if og % 2 == 0 else s_BsD
                    wait("act", "pe", r_mm_pe)
                    wait("act", "dve", rsig_guard[og % 2])
                    emiti("act", (lambda bir, rsb: lambda e: e.activation(
                        rsb[:, :], ps_f[bir][:, :], AF.Sigmoid))(bir, rsb), "act")
                    bank_consumed(bir, "act")
                    rsig_act = cnt["act"]
                    bikv = next_bank()
                    for hh in range(2):
                        buf = ewvS[hh]
                        ldv2 = stream_load(buf, wblock_dst(buf, NG),
                                           d_ewv[ex, hh * NG * P:(hh + 1) * NG * P,
                                                 og * P:(og + 1) * P].rearrange("(c p) m -> p c m", p=P))
                        wait("pe", ldv2[0], ldv2[1])
                        if og == 0 and hh == 0:
                            wait("pe", "dve", h_done)
                        for kc in range(NG):
                            emiti("pe", (lambda buf, kc, bikv, hh: lambda e: e.matmul(
                                ps_f[bikv][:, :], wchunk(buf, kc), fw(t_h, hh * NG + kc),
                                start=(hh == 0 and kc == 0), stop=(hh == 1 and kc == NG - 1)))(buf, kc, bikv, hh), "pe")
                        consumed(buf, "pe")
                    wait("dve", "pe", cnt["pe"])
                    wait("dve", "act", rsig_act)
                    emiti("dve", (lambda bikv, rsb: lambda e: e.tensor_mul(
                        s_t1[:, :], ps_f[bikv][:, :], rsb[:, :]))(bikv, rsb), "dve")
                    bank_consumed(bikv, "dve")
                    emiti("dve", lambda e: e.tensor_mul(s_t2[:, :], s_t1[:, :], ps_s2[:, :]), "dve")
                    asrc, adst = (t_acc, t_acc2) if ex % 2 == 0 else (t_acc2, t_acc)
                    emiti("dve", (lambda og, asrc, adst: lambda e: e.tensor_add(
                        fw(adst, og), fw(asrc, og), s_t2[:, :]))(og, asrc, adst), "dve")
                    rsig_guard[og % 2] = cnt["dve"]
                bfree_pe[0] = cnt["pe"]
                ps2_free[0] = cnt["dve"]
            moe_done = cnt["dve"]

            wait("pl", "dve", moe_done)
            emiti("pl", (lambda t: lambda e: e.dma_start(
                d_out.rearrange("(g p) n -> p g n", p=P)[:, :, (t - REAL0) * TT:(t - REAL0 + 1) * TT],
                fw3(t_acc)))(t), "st", 16)
            store_done[0] = cnt["st"]

        if is_att and t + 1 < NT:
            emiti("dve", lambda e: e.tensor_copy(
                t_tailE[:, :], fw3(t_ei)[:, :, TT - 1:TT]), "dve")
        tile_end_dve[0] = cnt["dve"]

    for t in range(NT):
        emit_tile(t)

    wait("pl", "st", cnt["st"])
    wait("pl", "x", cnt["x"])
    wait("sp", "st", cnt["st"])

    import contextlib
    with contextlib.ExitStack() as _st:
        sem_h = {}
        for ch in cnt:
            sem_h[ch] = _st.enter_context(nc.semaphore(f"s_{ch}"))
        block = _st.enter_context(nc.Block())

        def replay(eng, items):
            for it in items:
                if it[0] == "wait":
                    eng.wait_ge(sem_h[it[1]], it[2])
                else:
                    _, fn, ch, amt = it
                    inst = fn(eng)
                    if ch is not None:
                        inst.then_inc(sem_h[ch], amt)

        @block.tensor
        def _(e):
            replay(e, prog["pe"])

        @block.vector
        def _(e):
            replay(e, prog["dve"])

        @block.scalar
        def _(e):
            replay(e, prog["act"])

        @block.sync
        def _(e):
            replay(e, prog["sp"])

        @block.gpsimd
        def _(e):
            replay(e, prog["pl"])

    return nc


def _pack_pvec(inp):
    pv = np.zeros((P, PCOLS), np.float32)

    def put(name, vec):
        v = np.asarray(vec, np.float32).reshape(NG, P)
        pv[:, PV[name]:PV[name] + NG] = v.T

    put("ln1_g", inp["L0_attn_ln1_g"]); put("ln1_b", inp["L0_attn_ln1_b"])
    put("ln2_g", inp["L0_attn_ln2_g"]); put("ln2_b", inp["L0_attn_ln2_b"])
    put("mu_k", inp["L0_attn_tm_mu_k"]); put("mu_v", inp["L0_attn_tm_mu_v"])
    put("mu_r", inp["L0_attn_tm_mu_r"])
    put("td", inp["L0_attn_time_decay"]); put("tf", inp["L0_attn_time_first"])
    put("ffn_bias", inp["L0_ffn_bias"])
    cmk = np.asarray(inp["L0_attn_cm_mu_k"], np.float32)
    cmr = np.asarray(inp["L0_attn_cm_mu_r"], np.float32)
    for g in range(NG):
        for e in range(E):
            pv[:, PV["cm_k"] + g * 8 + e] = cmk[e, g * P:(g + 1) * P]
            pv[:, PV["cm_r"] + g * 8 + e] = cmr[e, g * P:(g + 1) * P]
    return pv


def kernel(**inputs):
    x = np.asarray(inputs["L0_attn_x"], np.float32)
    B, S, _ = x.shape
    pvec = _pack_pvec(inputs)
    ident = np.eye(P, dtype=np.float32)
    onehot = np.zeros((E, D), np.float32)
    for e in range(E):
        onehot[e, e * P:(e + 1) * P] = 1.0

    shared = dict(
        Wk=np.ascontiguousarray(inputs["L0_attn_Wk"], np.float32),
        Wv=np.ascontiguousarray(inputs["L0_attn_Wv"], np.float32),
        Wr=np.ascontiguousarray(inputs["L0_attn_Wr"], np.float32),
        Wo=np.ascontiguousarray(inputs["L0_attn_Wo"], np.float32),
        router_w_s=np.ascontiguousarray(
            np.asarray(inputs["L0_attn_router_w"], np.float32)
            .reshape(NG, P, E).transpose(1, 0, 2).reshape(P, NG * E)),
        eWk=np.ascontiguousarray(inputs["L0_attn_eWk"], np.float32),
        eWv=np.ascontiguousarray(inputs["L0_attn_eWv"], np.float32),
        eWr=np.ascontiguousarray(inputs["L0_attn_eWr"], np.float32),
        ffk=np.ascontiguousarray(inputs["L0_ffn_kernel"], np.float32),
        ffa_s=np.ascontiguousarray(
            np.asarray(inputs["L0_ffn_a"], np.float32)
            .reshape(NG, P, 16).transpose(1, 0, 2).reshape(P, NG * 16)),
        ffb=np.ascontiguousarray(inputs["L0_ffn_b"], np.float32),
        pvec=pvec, cident=ident,
        conescol=np.ones((P, 1), np.float32),
        conesrow=np.ones((1, P), np.float32),
        cinv1024=np.full((1, P), 1.0 / D, np.float32),
        conehot=onehot,
        czcol=np.zeros((P, 1), np.float32),
        ceps=np.full((P, 1), EPS, np.float32),
        czc16=np.zeros((P, 16), np.float32),
    )

    in_maps = []
    for c in range(8):
        b, q = c // 4, c % 4
        s0 = q * LC
        lo = s0 - HALO
        if lo < 0:
            xs = np.concatenate(
                [np.zeros((D, -lo), np.float32),
                 np.ascontiguousarray(x[b, 0:s0 + LC].T)], axis=1)
        else:
            xs = np.ascontiguousarray(x[b, lo:s0 + LC].T)
        first = 1.0 if q > 0 else 0.0
        m = dict(shared)
        m["xT"] = np.ascontiguousarray(xs)
        m["cme"] = np.full((P, 1), first, np.float32)
        m["cz01"] = np.full((P, 1), first, np.float32)
        in_maps.append(m)

    nc = build()
    res = run_bass_kernel_spmd(nc, in_maps, core_ids=list(range(8)))

    out = np.empty((B, S, D), np.float32)
    for c in range(8):
        b, q = c // 4, c % 4
        out[b, q * LC:(q + 1) * LC] = res.results[c]["outT"].T
    if DEBUG_TAPS:
        kernel.last_results = res
    return out


# version 22
# speedup vs baseline: 1.0762x; 1.0762x over previous
import numpy as np
import concourse.bass as bass
import concourse.mybir as mybir
from concourse.bass_utils import run_bass_kernel_spmd

F32 = mybir.dt.float32
I32 = mybir.dt.int32
AF = mybir.ActivationFunctionType
ALU = mybir.AluOpType
AX = mybir.AxisListType

P = 128
TT = 512
NT = 9
NG = 8
D = 1024
E = 8
FH = 2048
NGH = 16
HALO = 512
LC = 4096
TFULL = HALO + LC
EPS = 1e-5
REAL0 = 1
ATT0 = 0

DEBUG_TAPS = False

PV = {}
_pv_names = ["ln1_g", "ln1_b", "ln2_g", "ln2_b", "mu_k", "mu_v", "mu_r",
             "td", "tf", "ffn_bias"]
for _i, _n in enumerate(_pv_names):
    PV[_n] = _i * NG
PV["cm_k"] = 10 * NG
PV["cm_r"] = 10 * NG + 64
PCOLS = 10 * NG + 128


def build():
    nc = bass.Bass(target_bir_lowering=False)

    d_xT = nc.dram_tensor("xT", [D, TFULL], F32, kind="ExternalInput").ap()
    d_wk = nc.dram_tensor("Wk", [D, D], F32, kind="ExternalInput").ap()
    d_wv = nc.dram_tensor("Wv", [D, D], F32, kind="ExternalInput").ap()
    d_wr = nc.dram_tensor("Wr", [D, D], F32, kind="ExternalInput").ap()
    d_wo = nc.dram_tensor("Wo", [D, D], F32, kind="ExternalInput").ap()
    d_rw = nc.dram_tensor("router_w_s", [P, NG * E], F32, kind="ExternalInput").ap()
    d_ewk = nc.dram_tensor("eWk", [E, D, FH], F32, kind="ExternalInput").ap()
    d_ewv = nc.dram_tensor("eWv", [E, FH, D], F32, kind="ExternalInput").ap()
    d_ewr = nc.dram_tensor("eWr", [E, D, D], F32, kind="ExternalInput").ap()
    d_ffk = nc.dram_tensor("ffk", [D, D], F32, kind="ExternalInput").ap()
    d_ffa = nc.dram_tensor("ffa_s", [P, NG * 16], F32, kind="ExternalInput").ap()
    d_ffb = nc.dram_tensor("ffb", [16, D], F32, kind="ExternalInput").ap()
    d_pvec = nc.dram_tensor("pvec", [P, PCOLS], F32, kind="ExternalInput").ap()
    d_ident = nc.dram_tensor("cident", [P, P], F32, kind="ExternalInput").ap()
    d_onescol = nc.dram_tensor("conescol", [P, 1], F32, kind="ExternalInput").ap()
    d_onesrow = nc.dram_tensor("conesrow", [1, P], F32, kind="ExternalInput").ap()
    d_inv1024 = nc.dram_tensor("cinv1024", [1, P], F32, kind="ExternalInput").ap()
    d_onehot = nc.dram_tensor("conehot", [E, D], F32, kind="ExternalInput").ap()
    d_zcol = nc.dram_tensor("czcol", [P, 1], F32, kind="ExternalInput").ap()
    d_zc16 = nc.dram_tensor("czc16", [P, 16], F32, kind="ExternalInput").ap()
    d_me = nc.dram_tensor("cme", [P, 1], F32, kind="ExternalInput").ap()
    d_eps = nc.dram_tensor("ceps", [P, 1], F32, kind="ExternalInput").ap()
    d_z01 = nc.dram_tensor("cz01", [P, 1], F32, kind="ExternalInput").ap()
    d_out = nc.dram_tensor("outT", [D, LC], F32, kind="ExternalOutput").ap()
    d_wq = nc.dram_tensor("wq_scratch", [D, D], F32, kind="Internal").ap()

    taps = {}
    if DEBUG_TAPS:
        for nm, shp in [("tap_wkv", [D, TFULL]), ("tap_x2", [D, TFULL]),
                        ("tap_ei", [D, TFULL]), ("tap_xn", [D, TFULL]),
                        ("tap_tw", [P, 32 * NT]), ("tap_acc2", [D, TFULL])]:
            taps[nm] = nc.dram_tensor(nm, shp, F32, kind="ExternalOutput").ap()

    t_xn = nc.alloc_sbuf_tensor("t_xn", [P, NG * TT], F32)
    t_xn1 = nc.alloc_sbuf_tensor("t_xn1", [P, NG * TT], F32)
    t_acc = nc.alloc_sbuf_tensor("t_acc", [P, NG * TT], F32)
    t_ei = nc.alloc_sbuf_tensor("t_ei", [P, NG * TT], F32)
    t_A = nc.alloc_sbuf_tensor("t_A", [P, NG * TT], F32)
    t_B = nc.alloc_sbuf_tensor("t_B", [P, NG * TT], F32)
    t_h = nc.alloc_sbuf_tensor("t_h", [P, NGH * TT], F32)

    s_EK = nc.alloc_sbuf_tensor("s_EK", [P, TT], F32)
    s_EV = nc.alloc_sbuf_tensor("s_EV", [P, TT], F32)
    s_BsN = nc.alloc_sbuf_tensor("s_BsN", [P, TT], F32)
    s_BsD = nc.alloc_sbuf_tensor("s_BsD", [P, TT], F32)
    s_t1 = nc.alloc_sbuf_tensor("s_t1", [P, TT], F32)
    s_pl = nc.alloc_sbuf_tensor("s_pl", [P, TT], F32)
    s_pl2 = nc.alloc_sbuf_tensor("s_pl2", [P, TT], F32)
    s_t2 = nc.alloc_sbuf_tensor("s_t2", [P, TT], F32)
    b_mean = nc.alloc_sbuf_tensor("b_mean", [P, TT], F32)
    b_var = nc.alloc_sbuf_tensor("b_var", [P, TT], F32)
    b_rstd = nc.alloc_sbuf_tensor("b_rstd", [P, TT], F32)
    b_rstd1 = nc.alloc_sbuf_tensor("b_rstd1", [P, TT], F32)

    t_acc2 = nc.alloc_sbuf_tensor("t_acc2", [P, NG * TT], F32)
    wS = [nc.alloc_sbuf_tensor(f"wS{i}", [P, 4 * P], F32) for i in range(2)]
    ewkS = [nc.alloc_sbuf_tensor(f"ewkS{i}", [P, 4 * P], F32) for i in range(2)]
    ewvS = [nc.alloc_sbuf_tensor(f"ewvS{i}", [P, NG * P], F32) for i in range(2)]
    ewrS = [nc.alloc_sbuf_tensor(f"ewrS{i}", [P, NG * P], F32) for i in range(2)]

    t_la = b_rstd1[0:16, :]
    t_sums = b_mean[0:1, :]
    t_sumsq = b_var[0:1, :]
    t_pv = nc.alloc_sbuf_tensor("t_pv", [P, PCOLS], F32)
    t_gd = nc.alloc_sbuf_tensor("t_gd", [P, NG], F32)
    t_eu = nc.alloc_sbuf_tensor("t_eu", [P, NG], F32)
    t_nk = nc.alloc_sbuf_tensor("t_nk", [P, 64], F32)
    t_nr = nc.alloc_sbuf_tensor("t_nr", [P, 64], F32)
    t_e1 = nc.alloc_sbuf_tensor("t_e1", [P, NG], F32)
    t_cN = nc.alloc_sbuf_tensor("t_cN", [P, NG], F32)
    t_cD = nc.alloc_sbuf_tensor("t_cD", [P, NG], F32)
    t_tailA = nc.alloc_sbuf_tensor("t_tailA", [P, NG], F32)
    t_tailE = nc.alloc_sbuf_tensor("t_tailE", [P, NG], F32)
    t_mtail = nc.alloc_sbuf_tensor("t_mtail", [P, NG], F32)
    t_id = nc.alloc_sbuf_tensor("t_id", [P, P], F32)
    t_oc = nc.alloc_sbuf_tensor("t_oc", [P, 1], F32)
    t_or = nc.alloc_sbuf_tensor("t_or", [1, P], F32)
    t_iv = nc.alloc_sbuf_tensor("t_iv", [1, P], F32)
    t_oh = nc.alloc_sbuf_tensor("t_oh", [E, D], F32)
    t_zc = nc.alloc_sbuf_tensor("t_zc", [P, 1], F32)
    t_me = nc.alloc_sbuf_tensor("t_me", [P, 1], F32)
    t_eps = nc.alloc_sbuf_tensor("t_eps", [P, 1], F32)
    t_z01 = nc.alloc_sbuf_tensor("t_z01", [P, 1], F32)
    t_rws = nc.alloc_sbuf_tensor("t_rws", [P, NG * E], F32)
    t_as = nc.alloc_sbuf_tensor("t_as", [P, NG * 16], F32)
    t_bs = nc.alloc_sbuf_tensor("t_bs", [16, D], F32)
    t_lg = nc.alloc_sbuf_tensor("t_lg", [P, 32], F32)
    t_m1 = nc.alloc_sbuf_tensor("t_m1", [P, 4], F32)
    t_m2 = nc.alloc_sbuf_tensor("t_m2", [P, 4], F32)
    t_d12 = nc.alloc_sbuf_tensor("t_d12", [P, 4], F32)
    t_w1 = nc.alloc_sbuf_tensor("t_w1", [P, 4], F32)
    t_mk1 = nc.alloc_sbuf_tensor("t_mk1", [P, 32], F32)
    t_mk2 = nc.alloc_sbuf_tensor("t_mk2", [P, 32], F32)
    t_twm = nc.alloc_sbuf_tensor("t_twm", [P, 32], F32)
    t_twT = nc.alloc_sbuf_tensor("t_twT", [E, TT], F32)
    t_mx8 = nc.alloc_sbuf_tensor("t_mx8", [P, NG], F32)
    t_mx1 = nc.alloc_sbuf_tensor("t_mx1", [P, 1], F32)
    t_mrow = nc.alloc_sbuf_tensor("t_mrow", [1, P], F32)
    t_M = nc.alloc_sbuf_tensor("t_M", [1, 1], F32)
    t_scB = nc.alloc_sbuf_tensor("t_scB", [P, 1], F32)
    t_isB = nc.alloc_sbuf_tensor("t_isB", [P, 1], F32)

    PSN = 5
    ps_f = [nc.alloc_psum_tensor(f"ps_f{i}", [P, TT], F32) for i in range(PSN)]
    ps_s1 = nc.alloc_psum_tensor("ps_s1", [P, TT], F32)
    ps_s2 = nc.alloc_psum_tensor("ps_s2", [P, TT], F32)
    ps_s3 = nc.alloc_psum_tensor("ps_s3", [P, TT], F32)

    prog = {k: [] for k in ["pe", "dve", "act", "sp", "pl"]}
    WCH = ["w0", "w1", "w0b", "w1b", "ek0", "ek1", "ev0", "ev1", "er0", "er1"]
    cnt = {"pe": 0, "dve": 0, "act": 0, "ld": 0, "q": 0, "x": 0, "st": 0, "plc": 0}
    for _c in WCH:
        cnt[_c] = 0

    def emit(q, fn):
        prog[q].append(("op", fn, None, 0))

    def emiti(q, fn, ch, amt=1):
        if (q in ("dve", "act") and ch == q) or (q == "pl" and ch == "plc"):
            prog[q].append(("wait", ch, cnt[ch], 0))
        cnt[ch] += amt
        prog[q].append(("op", fn, ch, amt))

    def wait(q, ch, val):
        if val > 0:
            prog[q].append(("wait", ch, val, 0))

    bank_free = [("dve", 0)] * PSN
    bank_i = [0]

    def next_bank():
        i = bank_i[0] % PSN
        bank_i[0] += 1
        ch, v = bank_free[i]
        wait("pe", ch, v)
        return i

    def bank_consumed(i, ch):
        bank_free[i] = (ch, cnt[ch])

    def dma(dst, src):
        return lambda e: e.dma_start(dst, src)

    def fw(tens, g):
        return tens[:, g * TT:(g + 1) * TT]

    def fw3(tens, ngroups=NG):
        return tens.ap().rearrange("p (g n) -> p g n", n=TT)[:, 0:ngroups, :]

    def pcol(name, g):
        c = PV[name] + g
        return t_pv[:, c:c + 1]

    def wchunk(buf, kc):
        return buf[:, kc * P:(kc + 1) * P]

    def mblock_src(w_ap, og, kc0=0, nkc=None):
        if nkc is None:
            sl = w_ap[:, og * P:(og + 1) * P]
        else:
            sl = w_ap[kc0 * P:(kc0 + nkc) * P, og * P:(og + 1) * P]
        return sl.rearrange("(c p) m -> p c m", p=P)

    def wblock_dst(buf, nkc):
        return buf.ap().rearrange("p (c m) -> p c m", m=P)[:, 0:nkc, :]

    for dst, src in [(t_pv[:, :], d_pvec), (t_id[:, :], d_ident),
                     (t_oc[:, :], d_onescol), (t_or[:, :], d_onesrow),
                     (t_iv[:, :], d_inv1024), (t_oh[:, :], d_onehot),
                     (t_zc[:, :], d_zcol), (t_eps[:, :], d_eps),
                     (t_cN[:, :], d_zc16[:, 0:8]),
                     (t_cD[:, :], d_zc16[:, 8:16]), (t_me[:, :], d_me),
                     (t_z01[:, :], d_z01),
                     (t_rws[:, :], d_rw),
                     (t_as[:, :], d_ffa),
                     (t_bs[:, :], d_ffb)]:
        emiti("sp", dma(dst, src), "ld", 16)
    SETUP_LD = cnt["ld"]

    wait("act", "ld", SETUP_LD)
    emiti("act", lambda e: e.activation(t_e1[:, :], t_pv[:, PV["td"]:PV["td"] + NG], AF.Exp), "act")
    wait("act", "act", cnt["act"])
    emiti("act", lambda e: e.activation(t_gd[:, :], t_e1[:, :], AF.Exp, scale=-1.0), "act")
    emiti("act", lambda e: e.activation(t_eu[:, :], t_pv[:, PV["tf"]:PV["tf"] + NG], AF.Exp), "act")
    SETUP_ACT = cnt["act"]
    wait("dve", "ld", SETUP_LD)
    emiti("dve", lambda e: e.tensor_scalar(t_nk[:, :], t_pv[:, PV["cm_k"]:PV["cm_k"] + 64], 1.0, None, ALU.subtract), "dve")
    emiti("dve", lambda e: e.tensor_scalar(t_nr[:, :], t_pv[:, PV["cm_r"]:PV["cm_r"] + 64], 1.0, None, ALU.subtract), "dve")
    wait("dve", "dve", cnt["dve"])

    QLOAD = ewrS[1][:, 0:D]
    for kc in range(NG):
        emiti("sp", dma(QLOAD, d_ffk[kc * P:(kc + 1) * P, :]), "q", 16)
        wait("dve", "q", cnt["q"])
        emiti("dve", (lambda kc: lambda e: e.reduce_max(
            t_mx8[:, kc:kc + 1], QLOAD, axis=AX.X, apply_absolute_value=True))(kc), "dve")
        wait("sp", "dve", cnt["dve"])
    emiti("dve", lambda e: e.reduce_max(t_mx1[:, :], t_mx8[:, :], axis=AX.X), "dve")
    wait("pe", "ld", SETUP_LD)
    wait("pe", "dve", cnt["dve"])
    emiti("pe", lambda e: e.matmul(ps_s1[0:1, 0:P], t_mx1[:, :], t_id[:, :], start=True, stop=True), "pe")
    wait("dve", "pe", cnt["pe"])
    emiti("dve", lambda e: e.tensor_copy(t_mrow[:, :], ps_s1[0:1, 0:P]), "dve")
    wait("dve", "dve", cnt["dve"])
    emiti("dve", lambda e: e.reduce_max(t_M[:, :], t_mrow[:, :], axis=AX.X), "dve")
    wait("pe", "dve", cnt["dve"])
    emiti("pe", lambda e: e.matmul(ps_s3[0:P, 0:1], t_or[:, :], t_M[:, :], start=True, stop=True), "pe")
    wait("dve", "pe", cnt["pe"])
    emiti("dve", lambda e: e.tensor_scalar(t_scB[:, :], ps_s3[0:P, 0:1], 1.0 / 7.0, None, ALU.mult), "dve")
    wait("dve", "dve", cnt["dve"])
    emiti("dve", lambda e: e.reciprocal(t_isB[:, :], t_scB[:, :]), "dve")
    wait("pl", "dve", cnt["dve"])
    QU = ewvS[0][:, 0:D]
    QP = ewvS[1][:, 0:D]
    QQ = ewrS[0][:, 0:D]
    for kc in range(NG):
        wait("sp", "q", cnt["q"])
        emiti("sp", dma(QLOAD, d_ffk[kc * P:(kc + 1) * P, :]), "q", 16)
        wait("dve", "q", cnt["q"])
        emiti("dve", lambda e: e.tensor_scalar(QU, QLOAD, t_isB[:, 0:1], None, ALU.mult), "dve")
        emiti("dve", lambda e: e.tensor_scalar(QP, QU, -7.5, None, ALU.is_ge), "dve")
        cur, nxt = QP, QQ
        for j in range(-7, 7):
            thr = j + 0.5
            emiti("dve", (lambda thr, cur, nxt: lambda e: e.scalar_tensor_tensor(
                nxt, QU, thr, cur, ALU.is_ge, ALU.add))(thr, cur, nxt), "dve")
            cur, nxt = nxt, cur
        emiti("dve", (lambda cur, nxt: lambda e: e.tensor_scalar(
            nxt, cur, 8.0, None, ALU.subtract))(cur, nxt), "dve")
        q8 = nxt
        emiti("dve", (lambda q8: lambda e: e.tensor_scalar(
            QU, q8, t_scB[:, 0:1], None, ALU.mult))(q8), "dve")
        wait("sp", "dve", cnt["dve"])
        emiti("sp", dma(d_wq[kc * P:(kc + 1) * P, :], QU), "q", 16)
        wait("dve", "q", cnt["q"])
    WQ_LD = cnt["q"]

    wchan = {id(wS[0]): "w0", id(wS[1]): "w1",
             id(ewkS[0]): "ek0", id(ewkS[1]): "ek1",
             id(ewvS[0]): "ev0", id(ewvS[1]): "ev1",
             id(ewrS[0]): "er0", id(ewrS[1]): "er1"}
    wfree = {id(wS[0]): ("pe", 0), id(wS[1]): ("pe", 0),
             id(ewkS[0]): ("pe", 0), id(ewkS[1]): ("pe", 0),
             id(ewvS[0]): ("q", WQ_LD), id(ewvS[1]): ("q", WQ_LD),
             id(ewrS[0]): ("q", WQ_LD), id(ewrS[1]): ("q", WQ_LD)}

    def stream_load(buf, dst3, src_ap):
        ch, v = wfree[id(buf)]
        wait("sp", ch, v)
        wc = wchan[id(buf)]
        emiti("sp", dma(dst3, src_ap), wc, 16)
        return (wc, cnt[wc])

    wchanB = {id(wS[0]): "w0b", id(wS[1]): "w1b"}

    def stream_load2(buf, w_dram, og, kc0):
        ch, v = wfree[id(buf)]
        wait("sp", ch, v)
        wc, wcb = wchan[id(buf)], wchanB[id(buf)]
        d3 = buf.ap().rearrange("p (c m) -> p c m", m=P)
        emiti("sp", dma(d3[:, 0:2, :], mblock_src(w_dram, og, kc0, 2)), wc, 16)
        emiti("sp", dma(d3[:, 2:4, :], mblock_src(w_dram, og, kc0 + 2, 2)), wcb, 16)
        return [(wc, cnt[wc]), (wcb, cnt[wcb])]

    def consumed(buf, ch):
        wfree[id(buf)] = (ch, cnt[ch])

    def half_proj(w_dram, og, rhs_fw, bi, first_extra_wait=None):
        for h in range(2):
            buf = wS[wslot[0] % 2]; wslot[0] += 1
            for tk in stream_load2(buf, w_dram, og, h * 4):
                wait("pe", tk[0], tk[1])
            if h == 0 and first_extra_wait is not None:
                wait("pe", "dve", first_extra_wait)
            for kc in range(4):
                emiti("pe", (lambda buf, kc, bi, h: lambda e: e.matmul(
                    ps_f[bi][:, :], wchunk(buf, kc), rhs_fw(h * 4 + kc),
                    start=(h == 0 and kc == 0), stop=(h == 1 and kc == 3)))(buf, kc, bi, h), "pe")
            consumed(buf, "pe")

    wait("dve", "act", SETUP_ACT)
    wait("act", "ld", SETUP_LD)
    wait("sp", "q", WQ_LD)

    xload_ld = [0]
    store_done = [0]
    stat_free = [0]
    ps2_free = [0]
    ps3_free = [0]
    bfree_pe = [0]
    pe_sq = [0] * NG
    tile_end_dve = [0]
    wslot = [0]
    ei_done_s = [0]

    emiti("pl", dma(fw3(t_xn1), d_xT.rearrange("(g p) n -> p g n", p=P)[:, :, 0:TT]), "x", 16)
    xload_ld[0] = cnt["x"]

    def ln_stats(src_fw, prev_dve_guard):
        wait("pe", "dve", stat_free[0])
        wait("pe", "dve", prev_dve_guard)
        for g in range(NG):
            emiti("pe", (lambda g: lambda e: e.matmul(
                ps_s1[0:1, :], t_oc[:, :], src_fw(g), start=(g == 0), stop=(g == NG - 1)))(g), "pe")
        wait("act", "dve", prev_dve_guard)
        for g in range(NG):
            sq = s_t1 if g % 2 == 0 else s_t2
            if g >= 2:
                wait("act", "pe", pe_sq[g - 2])
            emiti("act", (lambda g, sq: lambda e: e.activation(sq[:, :], src_fw(g), AF.Square))(g, sq), "act")
            wait("pe", "act", cnt["act"])
            emiti("pe", (lambda g, sq: lambda e: e.matmul(
                ps_s2[0:1, :], t_oc[:, :], sq[:, :], start=(g == 0), stop=(g == NG - 1)))(g, sq), "pe")
            pe_sq[g] = cnt["pe"]
        wait("dve", "pe", cnt["pe"])
        emiti("dve", lambda e: e.tensor_copy(t_sums, ps_s1[0:1, :]), "dve")
        emiti("dve", lambda e: e.tensor_copy(t_sumsq, ps_s2[0:1, :]), "dve")
        wait("dve", "dve", cnt["dve"])
        wait("pe", "dve", cnt["dve"])
        emiti("pe", lambda e: e.matmul(ps_s1[:, :], t_iv[:, :], t_sums, start=True, stop=True), "pe")
        emiti("pe", lambda e: e.matmul(ps_s2[:, :], t_iv[:, :], t_sumsq, start=True, stop=True), "pe")
        wait("dve", "pe", cnt["pe"])
        emiti("dve", lambda e: e.tensor_copy(b_mean[:, :], ps_s1[:, :]), "dve")
        emiti("dve", lambda e: e.tensor_copy(b_var[:, :], ps_s2[:, :]), "dve")
        stat_free[0] = cnt["dve"]
        emiti("dve", lambda e: e.tensor_mul(s_t1[:, :], b_mean[:, :], b_mean[:, :]), "dve")
        emiti("dve", lambda e: e.tensor_sub(s_t2[:, :], b_var[:, :], s_t1[:, :]), "dve")
        wait("act", "dve", cnt["dve"])
        emiti("act", lambda e: e.activation(s_t1[:, :], s_t2[:, :], AF.Sqrt, bias=t_eps[:, 0:1]), "act")
        wait("dve", "act", cnt["act"])
        emiti("dve", lambda e: e.reciprocal(b_rstd[:, :], s_t1[:, :]), "dve")

    def emit_tile(t):
        is_att = t >= ATT0
        is_real = t >= REAL0
        x_t = t_xn1
        xld = xload_ld[0]

        wait("pe", "x", xld)
        wait("act", "x", xld)
        ln_stats(lambda g: fw(x_t, g), tile_end_dve[0])
        emiti("dve", lambda e: e.tensor_mul(s_t1[:, :], b_rstd[:, :], b_rstd[:, :]), "dve")
        emiti("dve", lambda e: e.tensor_mul(b_rstd1[:, :], s_t1[:, :], s_t2[:, :]), "dve")
        wait("act", "dve", cnt["dve"])
        emiti("act", lambda e: e.activation(s_t2[:, :], b_rstd1[:, :], AF.Sqrt, bias=t_eps[:, 0:1]), "act")
        wait("dve", "act", cnt["act"])
        emiti("dve", lambda e: e.reciprocal(b_rstd1[:, :], s_t2[:, :]), "dve")
        for g in range(NG):
            emiti("dve", (lambda g: lambda e: e.tensor_sub(s_t1[:, :], fw(x_t, g), b_mean[:, :]))(g), "dve")
            emiti("dve", (lambda g: lambda e: e.tensor_mul(fw(t_xn, g), s_t1[:, :], b_rstd[:, :]))(g), "dve")
        if is_real:
            wait("dve", "st", store_done[0])
            for g in range(NG):
                emiti("dve", (lambda g: lambda e: e.scalar_tensor_tensor(
                    fw(t_acc, g), fw(x_t, g), pcol("ffn_bias", g), fw(t_xn, g),
                    ALU.add, ALU.add))(g), "dve")
        acc_init = cnt["dve"]
        for g in range(NG):
            emiti("dve", (lambda g: lambda e: e.tensor_mul(s_t1[:, :], fw(t_xn, g), b_rstd1[:, :]))(g), "dve")
            emiti("dve", (lambda g: lambda e: e.tensor_scalar(
                fw(t_xn1, g), s_t1[:, :], pcol("ln1_g", g), pcol("ln1_b", g),
                ALU.mult, ALU.add))(g), "dve")
        xn1_done = cnt["dve"]
        xn_done = cnt["dve"]
        if DEBUG_TAPS:
            wait("pl", "dve", xn_done)
            emiti("pl", (lambda t: lambda e: e.dma_start(
                taps["tap_xn"].rearrange("(g p) n -> p g n", p=P)[:, :, t * TT:(t + 1) * TT],
                fw3(t_xn)))(t), "st", 16)

        if is_real:
            wait("pe", "dve", xn_done)
            wait("pe", "dve", ps3_free[0])
            for g in range(NG):
                emiti("pe", (lambda g: lambda e: e.matmul(
                    ps_s3[0:16, :], t_as[:, g * 16:(g + 1) * 16], fw(t_xn, g),
                    start=(g == 0), stop=(g == NG - 1)))(g), "pe")
            wait("dve", "pe", cnt["pe"])
            emiti("dve", lambda e: e.tensor_copy(t_la, ps_s3[0:16, :]), "dve")
            wait("dve", "dve", cnt["dve"])
            ps3_free[0] = cnt["dve"]
            la_done = cnt["dve"]
            for og in range(NG):
                bi = next_bank()
                for h in range(2):
                    buf = wS[wslot[0] % 2]; wslot[0] += 1
                    for tk in stream_load2(buf, d_wq, og, h * 4):
                        wait("pe", tk[0], tk[1])
                    if h == 0 and og == 0:
                        wait("pe", "dve", la_done)
                    for kc in range(4):
                        emiti("pe", (lambda buf, kc, bi, h: lambda e: e.matmul(
                            ps_f[bi][:, :], wchunk(buf, kc), fw(t_xn, h * 4 + kc),
                            start=(h == 0 and kc == 0), stop=False))(buf, kc, bi, h), "pe")
                    consumed(buf, "pe")
                emiti("pe", (lambda og, bi: lambda e: e.matmul(
                    ps_f[bi][:, :], t_bs[:, og * P:(og + 1) * P], t_la,
                    start=False, stop=True))(og, bi), "pe")
                wait("dve", "pe", cnt["pe"])
                wait("dve", "dve", acc_init)
                emiti("dve", (lambda og, bi: lambda e: e.tensor_add(
                    fw(t_acc2, og), fw(t_acc, og), ps_f[bi][:, :]))(og, bi), "dve")
                bank_consumed(bi, "dve")

        tail_src = t_mtail if t == REAL0 else t_tailA
        if t == REAL0:
            wait("dve", "dve", xn1_done)
            emiti("dve", lambda e: e.tensor_scalar(
                t_mtail[:, :], t_tailA[:, :], t_z01[:, 0:1], None, ALU.mult), "dve")
            wait("dve", "dve", cnt["dve"])
        wait("dve", "dve", xn1_done)
        for g in range(NG):
            xx0 = t_zc[:, 0:1] if t == 0 else tail_src[:, g:g + 1]
            emiti("dve", (lambda g, xx0: lambda e: e.tensor_sub(
                s_t1[:, 0:1], fw(t_xn1, g)[:, 0:1], xx0))(g, xx0), "dve")
            emiti("dve", (lambda g: lambda e: e.tensor_sub(
                s_t1[:, 1:TT], fw(t_xn1, g)[:, 1:TT], fw(t_xn1, g)[:, 0:TT - 1]))(g), "dve")
            for name, dst in (("mu_k", t_A), ("mu_v", t_B)):
                emiti("dve", (lambda g, name, dst, xx0: lambda e: e.scalar_tensor_tensor(
                    fw(dst, g)[:, 0:1], s_t1[:, 0:1], pcol(name, g), xx0,
                    ALU.mult, ALU.add))(g, name, dst, xx0), "dve")
                emiti("dve", (lambda g, name, dst: lambda e: e.scalar_tensor_tensor(
                    fw(dst, g)[:, 1:TT], s_t1[:, 1:TT], pcol(name, g),
                    fw(t_xn1, g)[:, 0:TT - 1], ALU.mult, ALU.add))(g, name, dst), "dve")
        mix_kv = cnt["dve"]

        ek_guard = [mix_kv]
        wait("pe", "dve", mix_kv)
        for og in range(NG):
            bik = next_bank()
            half_proj(d_wk, og, lambda kc: fw(t_A, kc), bik)
            kdone = cnt["pe"]
            biv = next_bank()
            half_proj(d_wv, og, lambda kc: fw(t_B, kc), biv)
            vdone = cnt["pe"]
            wait("act", "pe", kdone)
            wait("act", "dve", ek_guard[0])
            emiti("act", (lambda bik: lambda e: e.activation(s_EK[:, :], ps_f[bik][:, :], AF.Exp))(bik), "act")
            bank_consumed(bik, "act")
            wait("dve", "act", cnt["act"])
            wait("dve", "pe", vdone)
            emiti("dve", (lambda biv: lambda e: e.tensor_mul(s_EV[:, :], s_EK[:, :], ps_f[biv][:, :]))(biv), "dve")
            bank_consumed(biv, "dve")
            gbb = t_gd[:, og:og + 1].broadcast_to([P, TT])
            if t < REAL0:
                emiti("dve", lambda e: e.tensor_scalar(s_t2[:, :], s_EV[:, :], t_me[:, 0:1], None, ALU.mult), "dve")
                emiti("dve", lambda e: e.tensor_scalar(s_t1[:, :], s_EK[:, :], t_me[:, 0:1], None, ALU.mult), "dve")
                dsrc = s_t1
                nsrc = s_t2
            else:
                dsrc = s_EK
                nsrc = s_EV
            emiti("dve", (lambda og, gbb, nsrc: lambda e: e.tensor_tensor_scan(
                s_BsN[:, :], gbb, nsrc[:, :], t_cN[:, og:og + 1], ALU.mult, ALU.add))(og, gbb, nsrc), "dve")
            emiti("dve", (lambda og, gbb, dsrc: lambda e: e.tensor_tensor_scan(
                s_BsD[:, :], gbb, dsrc[:, :], t_cD[:, og:og + 1], ALU.mult, ALU.add))(og, gbb, dsrc), "dve")
            if is_att:
                emiti("dve", (lambda og: lambda e: e.scalar_tensor_tensor(
                    s_t1[:, 0:1], s_EV[:, 0:1], t_eu[:, og:og + 1], t_cN[:, og:og + 1],
                    ALU.mult, ALU.add))(og), "dve")
                emiti("dve", (lambda og: lambda e: e.scalar_tensor_tensor(
                    s_t1[:, 1:TT], s_EV[:, 1:TT], t_eu[:, og:og + 1], s_BsN[:, 0:TT - 1],
                    ALU.mult, ALU.add))(og), "dve")
                emiti("dve", (lambda og: lambda e: e.scalar_tensor_tensor(
                    s_t2[:, 0:1], s_EK[:, 0:1], t_eu[:, og:og + 1], t_cD[:, og:og + 1],
                    ALU.mult, ALU.add))(og), "dve")
                emiti("dve", (lambda og: lambda e: e.scalar_tensor_tensor(
                    s_t2[:, 1:TT], s_EK[:, 1:TT], t_eu[:, og:og + 1], s_BsD[:, 0:TT - 1],
                    ALU.mult, ALU.add))(og), "dve")
                emiti("dve", lambda e: e.reciprocal(s_EK[:, :], s_t2[:, :]), "dve")
                emiti("dve", (lambda og: lambda e: e.tensor_mul(
                    fw(t_h, og), s_t1[:, :], s_EK[:, :]))(og), "dve")
            ek_guard[0] = cnt["dve"]
            emiti("dve", (lambda og: lambda e: e.tensor_copy(
                t_cN[:, og:og + 1], s_BsN[:, TT - 1:TT]))(og), "dve")
            emiti("dve", (lambda og: lambda e: e.tensor_copy(
                t_cD[:, og:og + 1], s_BsD[:, TT - 1:TT]))(og), "dve")
            wait("dve", "dve", cnt["dve"])
        scan_done = cnt["dve"]

        kv_pe_end = cnt["pe"]
        if is_att:
            wait("dve", "pe", kv_pe_end)
            for g in range(NG):
                xx0 = t_zc[:, 0:1] if t == 0 else tail_src[:, g:g + 1]
                emiti("dve", (lambda g, xx0: lambda e: e.tensor_sub(
                    s_pl[:, 0:1], fw(t_xn1, g)[:, 0:1], xx0))(g, xx0), "dve")
                emiti("dve", (lambda g: lambda e: e.tensor_sub(
                    s_pl[:, 1:TT], fw(t_xn1, g)[:, 1:TT], fw(t_xn1, g)[:, 0:TT - 1]))(g), "dve")
                emiti("dve", (lambda g, xx0: lambda e: e.scalar_tensor_tensor(
                    fw(t_A, g)[:, 0:1], s_pl[:, 0:1], pcol("mu_r", g), xx0,
                    ALU.mult, ALU.add))(g, xx0), "dve")
                emiti("dve", (lambda g: lambda e: e.scalar_tensor_tensor(
                    fw(t_A, g)[:, 1:TT], s_pl[:, 1:TT], pcol("mu_r", g),
                    fw(t_xn1, g)[:, 0:TT - 1], ALU.mult, ALU.add))(g), "dve")
        xr_done = cnt["dve"]
        emiti("dve", lambda e: e.tensor_copy(
            t_tailA[:, :], fw3(t_xn1)[:, :, TT - 1:TT]), "dve")
        tailA_done = cnt["dve"]
        if t + 1 < NT:
            wait("pl", "dve", tailA_done)
            emiti("pl", (lambda t: lambda e: e.dma_start(
                fw3(t_xn1), d_xT.rearrange("(g p) n -> p g n", p=P)[:, :, (t + 1) * TT:(t + 2) * TT]))(t), "x", 16)
            xload_ld[0] = cnt["x"]

        if is_att:
            if DEBUG_TAPS:
                wait("pl", "dve", scan_done)
                emiti("pl", (lambda t: lambda e: e.dma_start(
                    taps["tap_wkv"].rearrange("(g p) n -> p g n", p=P)[:, :, t * TT:(t + 1) * TT],
                    fw3(t_h)))(t), "st", 16)
            rw_guard = [scan_done, scan_done]
            wait("pe", "dve", xr_done)
            for og in range(NG):
                bir = next_bank()
                half_proj(d_wr, og, lambda kc: fw(t_A, kc), bir)
                rbuf = s_t1 if og % 2 == 0 else s_t2
                wait("act", "pe", cnt["pe"])
                wait("act", "dve", rw_guard[og % 2])
                emiti("act", (lambda bir, rbuf: lambda e: e.activation(
                    rbuf[:, :], ps_f[bir][:, :], AF.Sigmoid))(bir, rbuf), "act")
                bank_consumed(bir, "act")
                wait("dve", "act", cnt["act"])
                if DEBUG_TAPS:
                    wait("dve", "st", cnt["st"])
                emiti("dve", (lambda og, rbuf: lambda e: e.tensor_mul(
                    fw(t_h, NG + og), fw(t_h, og), rbuf[:, :]))(og, rbuf), "dve")
                rw_guard[og % 2] = cnt["dve"]
            rw_done = cnt["dve"]

            wait("pe", "dve", rw_done)
            for og in range(NG):
                bio = next_bank()
                half_proj(d_wo, og, lambda kc: fw(t_h, NG + kc), bio)
                wait("dve", "pe", cnt["pe"])
                if is_real:
                    emiti("dve", (lambda og, bio: lambda e: e.tensor_add(
                        fw(t_acc, og), fw(t_acc2, og), ps_f[bio][:, :]))(og, bio), "dve")
                emiti("dve", (lambda og, bio: lambda e: e.tensor_add(
                    fw(t_acc2, og), fw(t_xn, og), ps_f[bio][:, :]))(og, bio), "dve")
                bank_consumed(bio, "dve")
            x2_done = cnt["dve"]
            if DEBUG_TAPS and is_real:
                wait("pl", "dve", x2_done)
                emiti("pl", (lambda t: lambda e: e.dma_start(
                    taps["tap_acc2"].rearrange("(g p) n -> p g n", p=P)[:, :, t * TT:(t + 1) * TT],
                    fw3(t_acc)))(t), "st", 16)
            if DEBUG_TAPS:
                wait("pl", "dve", x2_done)
                emiti("pl", (lambda t: lambda e: e.dma_start(
                    taps["tap_x2"].rearrange("(g p) n -> p g n", p=P)[:, :, t * TT:(t + 1) * TT],
                    fw3(t_acc2)))(t), "st", 16)

            ln_stats(lambda g: fw(t_acc2, g), x2_done)
            for g in range(NG):
                emiti("dve", (lambda g: lambda e: e.tensor_sub(
                    fw(t_ei, g), fw(t_acc2, g), b_mean[:, :]))(g), "dve")
                emiti("dve", (lambda g: lambda e: e.tensor_mul(
                    s_t1[:, :], fw(t_ei, g), b_rstd[:, :]))(g), "dve")
                emiti("dve", (lambda g: lambda e: e.tensor_scalar(
                    fw(t_ei, g), s_t1[:, :], pcol("ln2_g", g), pcol("ln2_b", g),
                    ALU.mult, ALU.add))(g), "dve")
            ei_done_s[0] = cnt["dve"]
            if DEBUG_TAPS:
                wait("pl", "dve", ei_done_s[0])
                emiti("pl", (lambda t: lambda e: e.dma_start(
                    taps["tap_ei"].rearrange("(g p) n -> p g n", p=P)[:, :, t * TT:(t + 1) * TT],
                    fw3(t_ei)))(t), "st", 16)

        if is_real:
            ei_done = ei_done_s[0]
            wait("pe", "dve", ei_done)
            wait("pe", "dve", stat_free[0])
            for sb in range(4):
                for g in range(NG):
                    emiti("pe", (lambda sb, g: lambda e: e.matmul(
                        ps_s1[0:P, sb * 8:(sb + 1) * 8],
                        fw(t_ei, g)[:, sb * P:(sb + 1) * P],
                        t_rws[:, g * 8:(g + 1) * 8],
                        start=(g == 0), stop=(g == NG - 1)))(sb, g), "pe")
            wait("dve", "pe", cnt["pe"])
            emiti("dve", lambda e: e.tensor_copy(t_lg[:, :], ps_s1[0:P, 0:32]), "dve")
            wait("dve", "dve", cnt["dve"])
            lg3 = t_lg.ap().rearrange("p (c e) -> p c e", e=E)
            mk13 = t_mk1.ap().rearrange("p (c e) -> p c e", e=E)
            mk23 = t_mk2.ap().rearrange("p (c e) -> p c e", e=E)
            tw3 = t_twm.ap().rearrange("p (c e) -> p c e", e=E)
            emiti("dve", lambda e: e.reduce_max(t_m1[:, :], lg3, axis=AX.X), "dve")
            wait("dve", "dve", cnt["dve"])
            emiti("dve", lambda e: e.tensor_tensor(
                mk13, lg3, t_m1.ap().unsqueeze(2).broadcast_to([P, 4, E]), ALU.is_equal), "dve")
            emiti("dve", lambda e: e.scalar_tensor_tensor(
                t_mk2[:, :], t_mk1[:, :], -1e30, t_lg[:, :], ALU.mult, ALU.add), "dve")
            wait("dve", "dve", cnt["dve"])
            emiti("dve", lambda e: e.reduce_max(t_m2[:, :], mk23, axis=AX.X), "dve")
            wait("dve", "dve", cnt["dve"])
            emiti("dve", lambda e: e.tensor_tensor(
                mk23, lg3, t_m2.ap().unsqueeze(2).broadcast_to([P, 4, E]), ALU.is_equal), "dve")
            emiti("dve", lambda e: e.tensor_sub(t_d12[:, :], t_m1[:, :], t_m2[:, :]), "dve")
            wait("act", "dve", cnt["dve"])
            emiti("act", lambda e: e.activation(t_w1[:, :], t_d12[:, :], AF.Sigmoid), "act")
            wait("dve", "act", cnt["act"])
            emiti("dve", lambda e: e.tensor_sub(t_lg[:, :], t_mk1[:, :], t_mk2[:, :]), "dve")
            emiti("dve", lambda e: e.tensor_tensor(
                t_twm.ap().rearrange("p (c e) -> p c e", e=E), t_lg.ap().rearrange("p (c e) -> p c e", e=E),
                t_w1.ap().unsqueeze(2).broadcast_to([P, 4, E]), ALU.mult), "dve")
            emiti("dve", lambda e: e.tensor_add(t_lg[:, :], t_twm[:, :], t_mk2[:, :]), "dve")
            emiti("dve", lambda e: e.tensor_copy(t_twm[:, :], t_lg[:, :]), "dve")
            wait("dve", "dve", cnt["dve"])
            tw_done = cnt["dve"]
            if DEBUG_TAPS:
                wait("pl", "dve", tw_done)
                emiti("pl", (lambda t: lambda e: e.dma_start(
                    taps["tap_tw"][:, t * 32:(t + 1) * 32], t_twm[:, :]))(t), "st", 16)
            wait("pe", "dve", tw_done)
            for sb in range(4):
                emiti("pe", (lambda sb: lambda e: e.transpose(
                    ps_s1[0:8, sb * P:(sb + 1) * P], t_twm[:, sb * 8:(sb + 1) * 8],
                    t_id[:, :]))(sb), "pe")
            wait("dve", "pe", cnt["pe"])
            emiti("dve", lambda e: e.tensor_copy(t_twT[:, :], ps_s1[0:8, :]), "dve")
            wait("dve", "dve", cnt["dve"])
            stat_free[0] = cnt["dve"]
            twT_done = cnt["dve"]

            etail = t_mtail if t == REAL0 else t_tailE
            if t == REAL0:
                emiti("dve", lambda e: e.tensor_scalar(
                    t_mtail[:, :], t_tailE[:, :], t_z01[:, 0:1], None, ALU.mult), "dve")
                wait("dve", "dve", cnt["dve"])
            for g in range(NG):
                emiti("dve", (lambda g, etail: lambda e: e.tensor_sub(
                    fw(t_A, g)[:, 0:1], fw(t_ei, g)[:, 0:1], etail[:, g:g + 1]))(g, etail), "dve")
                emiti("dve", (lambda g: lambda e: e.tensor_sub(
                    fw(t_A, g)[:, 1:TT], fw(t_ei, g)[:, 1:TT], fw(t_ei, g)[:, 0:TT - 1]))(g), "dve")
            d2_done = cnt["dve"]

            def ld_ewk0(exn):
                return [stream_load(ewkS[hh], wblock_dst(ewkS[hh], 4),
                                    d_ewk[exn, hh * 4 * P:(hh + 1) * 4 * P, 0:P]
                                    .rearrange("(c p) m -> p c m", p=P)) for hh in range(2)]

            pre_ewk = ld_ewk0(0)
            pre_rkv = [None]
            xk_pre = [0]
            for ex in range(E):
                tb = ps_s2 if ex % 2 == 0 else ps_s3
                tb_free = ps2_free if ex % 2 == 0 else ps3_free
                wait("pe", "dve", twT_done)
                wait("pe", "dve", tb_free[0])
                emiti("pe", (lambda ex, tb: lambda e: e.matmul(
                    tb[:, :], t_oh[:, ex * P:(ex + 1) * P], t_twT[:, :],
                    start=True, stop=True))(ex, tb), "pe")
                if ex == 0:
                    wait("dve", "dve", d2_done)
                    wait("dve", "pe", bfree_pe[0])
                    for g in range(NG):
                        emiti("dve", (lambda ex, g: lambda e: e.scalar_tensor_tensor(
                            fw(t_B, g), fw(t_A, g), t_nk[:, g * 8 + ex:g * 8 + ex + 1],
                            fw(t_ei, g), ALU.mult, ALU.add))(ex, g), "dve")
                    xk_pl = cnt["dve"]
                else:
                    xk_pl = xk_pre[0]
                wait("pe", "dve", xk_pl)
                hsq_guard = [0, 0]
                for og in range(NGH):
                    bi = next_bank()
                    for hh in range(2):
                        buf = ewkS[hh]
                        if og == 0:
                            ldh = pre_ewk[hh]
                        else:
                            ldh = stream_load(buf, wblock_dst(buf, 4),
                                              d_ewk[ex, hh * 4 * P:(hh + 1) * 4 * P,
                                                    og * P:(og + 1) * P].rearrange("(c p) m -> p c m", p=P))
                        wait("pe", ldh[0], ldh[1])
                        for kc in range(4):
                            emiti("pe", (lambda buf, kc, bi, hh: lambda e: e.matmul(
                                ps_f[bi][:, :], wchunk(buf, kc), fw(t_B, hh * 4 + kc),
                                start=(hh == 0 and kc == 0), stop=(hh == 1 and kc == 3)))(buf, kc, bi, hh), "pe")
                        consumed(buf, "pe")
                    if og == NGH - 4:
                        pre_rkv[0] = (
                            stream_load(ewrS[0], wblock_dst(ewrS[0], NG),
                                        d_ewr[ex, :, 0:P].rearrange("(c p) m -> p c m", p=P)),
                            [stream_load(ewvS[hh], wblock_dst(ewvS[hh], NG),
                                         d_ewv[ex, hh * NG * P:(hh + 1) * NG * P, 0:P]
                                         .rearrange("(c p) m -> p c m", p=P)) for hh in range(2)])
                    rlb = s_BsN if og % 2 == 0 else s_BsD
                    wait("act", "pe", cnt["pe"])
                    wait("act", "dve", hsq_guard[og % 2])
                    emiti("act", (lambda bi, rlb: lambda e: e.activation(
                        rlb[:, :], ps_f[bi][:, :], AF.Relu))(bi, rlb), "act")
                    bank_consumed(bi, "act")
                    wait("dve", "act", cnt["act"])
                    emiti("dve", (lambda og, rlb: lambda e: e.tensor_mul(
                        fw(t_h, og), rlb[:, :], rlb[:, :]))(og, rlb), "dve")
                    hsq_guard[og % 2] = cnt["dve"]
                h_pe = cnt["pe"]
                h_done = cnt["dve"]
                wait("dve", "pe", h_pe)
                for g in range(NG):
                    emiti("dve", (lambda ex, g: lambda e: e.scalar_tensor_tensor(
                        fw(t_B, g), fw(t_A, g), t_nr[:, g * 8 + ex:g * 8 + ex + 1],
                        fw(t_ei, g), ALU.mult, ALU.add))(ex, g), "dve")
                xr_pl = cnt["dve"]
                rsig_guard = [h_done, h_done]
                for og in range(NG):
                    bir = next_bank()
                    buf = ewrS[og % 2]
                    if og == 0:
                        ldr = pre_rkv[0][0]
                    else:
                        ldr = stream_load(buf, wblock_dst(buf, NG),
                                          d_ewr[ex, :, og * P:(og + 1) * P].rearrange("(c p) m -> p c m", p=P))
                    wait("pe", ldr[0], ldr[1])
                    if og == 0:
                        wait("pe", "dve", xr_pl)
                    for kc in range(NG):
                        emiti("pe", (lambda buf, kc, bir: lambda e: e.matmul(
                            ps_f[bir][:, :], wchunk(buf, kc), fw(t_B, kc),
                            start=(kc == 0), stop=(kc == NG - 1)))(buf, kc, bir), "pe")
                    consumed(buf, "pe")
                    r_mm_pe = cnt["pe"]
                    if og == NG - 1 and ex < E - 1:
                        wait("dve", "pe", r_mm_pe)
                        for g in range(NG):
                            emiti("dve", (lambda exn, g: lambda e: e.scalar_tensor_tensor(
                                fw(t_B, g), fw(t_A, g), t_nk[:, g * 8 + exn:g * 8 + exn + 1],
                                fw(t_ei, g), ALU.mult, ALU.add))(ex + 1, g), "dve")
                        xk_pre[0] = cnt["dve"]
                    rsb = s_BsN if og % 2 == 0 else s_BsD
                    wait("act", "pe", r_mm_pe)
                    wait("act", "dve", rsig_guard[og % 2])
                    emiti("act", (lambda bir, rsb: lambda e: e.activation(
                        rsb[:, :], ps_f[bir][:, :], AF.Sigmoid))(bir, rsb), "act")
                    bank_consumed(bir, "act")
                    rsig_act = cnt["act"]
                    bikv = next_bank()
                    for hh in range(2):
                        buf = ewvS[hh]
                        if og == 0:
                            ldv2 = pre_rkv[0][1][hh]
                        else:
                            ldv2 = stream_load(buf, wblock_dst(buf, NG),
                                               d_ewv[ex, hh * NG * P:(hh + 1) * NG * P,
                                                     og * P:(og + 1) * P].rearrange("(c p) m -> p c m", p=P))
                        wait("pe", ldv2[0], ldv2[1])
                        if og == 0 and hh == 0:
                            wait("pe", "dve", h_done)
                        for kc in range(NG):
                            emiti("pe", (lambda buf, kc, bikv, hh: lambda e: e.matmul(
                                ps_f[bikv][:, :], wchunk(buf, kc), fw(t_h, hh * NG + kc),
                                start=(hh == 0 and kc == 0), stop=(hh == 1 and kc == NG - 1)))(buf, kc, bikv, hh), "pe")
                        consumed(buf, "pe")
                    if og == 0 and ex < E - 1:
                        pre_ewk = ld_ewk0(ex + 1)
                    wait("dve", "pe", cnt["pe"])
                    wait("dve", "act", rsig_act)
                    emiti("dve", (lambda bikv, rsb: lambda e: e.tensor_mul(
                        s_t1[:, :], ps_f[bikv][:, :], rsb[:, :]))(bikv, rsb), "dve")
                    bank_consumed(bikv, "dve")
                    emiti("dve", (lambda tb: lambda e: e.tensor_mul(
                        s_t2[:, :], s_t1[:, :], tb[:, :]))(tb), "dve")
                    asrc, adst = (t_acc, t_acc2) if ex % 2 == 0 else (t_acc2, t_acc)
                    emiti("dve", (lambda og, asrc, adst: lambda e: e.tensor_add(
                        fw(adst, og), fw(asrc, og), s_t2[:, :]))(og, asrc, adst), "dve")
                    rsig_guard[og % 2] = cnt["dve"]
                bfree_pe[0] = cnt["pe"]
                tb_free[0] = cnt["dve"]
            moe_done = cnt["dve"]

            wait("pl", "dve", moe_done)
            emiti("pl", (lambda t: lambda e: e.dma_start(
                d_out.rearrange("(g p) n -> p g n", p=P)[:, :, (t - REAL0) * TT:(t - REAL0 + 1) * TT],
                fw3(t_acc)))(t), "st", 16)
            store_done[0] = cnt["st"]

        if is_att and t + 1 < NT:
            emiti("dve", lambda e: e.tensor_copy(
                t_tailE[:, :], fw3(t_ei)[:, :, TT - 1:TT]), "dve")
        tile_end_dve[0] = cnt["dve"]

    for t in range(NT):
        emit_tile(t)

    wait("pl", "st", cnt["st"])
    wait("pl", "x", cnt["x"])
    wait("sp", "st", cnt["st"])

    import contextlib
    with contextlib.ExitStack() as _st:
        sem_h = {}
        for ch in cnt:
            sem_h[ch] = _st.enter_context(nc.semaphore(f"s_{ch}"))
        block = _st.enter_context(nc.Block())

        def replay(eng, items):
            for it in items:
                if it[0] == "wait":
                    eng.wait_ge(sem_h[it[1]], it[2])
                else:
                    _, fn, ch, amt = it
                    inst = fn(eng)
                    if ch is not None:
                        inst.then_inc(sem_h[ch], amt)

        @block.tensor
        def _(e):
            replay(e, prog["pe"])

        @block.vector
        def _(e):
            replay(e, prog["dve"])

        @block.scalar
        def _(e):
            replay(e, prog["act"])

        @block.sync
        def _(e):
            replay(e, prog["sp"])

        @block.gpsimd
        def _(e):
            replay(e, prog["pl"])

    return nc


def _pack_pvec(inp):
    pv = np.zeros((P, PCOLS), np.float32)

    def put(name, vec):
        v = np.asarray(vec, np.float32).reshape(NG, P)
        pv[:, PV[name]:PV[name] + NG] = v.T

    put("ln1_g", inp["L0_attn_ln1_g"]); put("ln1_b", inp["L0_attn_ln1_b"])
    put("ln2_g", inp["L0_attn_ln2_g"]); put("ln2_b", inp["L0_attn_ln2_b"])
    put("mu_k", inp["L0_attn_tm_mu_k"]); put("mu_v", inp["L0_attn_tm_mu_v"])
    put("mu_r", inp["L0_attn_tm_mu_r"])
    put("td", inp["L0_attn_time_decay"]); put("tf", inp["L0_attn_time_first"])
    put("ffn_bias", inp["L0_ffn_bias"])
    cmk = np.asarray(inp["L0_attn_cm_mu_k"], np.float32)
    cmr = np.asarray(inp["L0_attn_cm_mu_r"], np.float32)
    for g in range(NG):
        for e in range(E):
            pv[:, PV["cm_k"] + g * 8 + e] = cmk[e, g * P:(g + 1) * P]
            pv[:, PV["cm_r"] + g * 8 + e] = cmr[e, g * P:(g + 1) * P]
    return pv


def kernel(**inputs):
    x = np.asarray(inputs["L0_attn_x"], np.float32)
    B, S, _ = x.shape
    pvec = _pack_pvec(inputs)
    ident = np.eye(P, dtype=np.float32)
    onehot = np.zeros((E, D), np.float32)
    for e in range(E):
        onehot[e, e * P:(e + 1) * P] = 1.0

    shared = dict(
        Wk=np.ascontiguousarray(inputs["L0_attn_Wk"], np.float32),
        Wv=np.ascontiguousarray(inputs["L0_attn_Wv"], np.float32),
        Wr=np.ascontiguousarray(inputs["L0_attn_Wr"], np.float32),
        Wo=np.ascontiguousarray(inputs["L0_attn_Wo"], np.float32),
        router_w_s=np.ascontiguousarray(
            np.asarray(inputs["L0_attn_router_w"], np.float32)
            .reshape(NG, P, E).transpose(1, 0, 2).reshape(P, NG * E)),
        eWk=np.ascontiguousarray(inputs["L0_attn_eWk"], np.float32),
        eWv=np.ascontiguousarray(inputs["L0_attn_eWv"], np.float32),
        eWr=np.ascontiguousarray(inputs["L0_attn_eWr"], np.float32),
        ffk=np.ascontiguousarray(inputs["L0_ffn_kernel"], np.float32),
        ffa_s=np.ascontiguousarray(
            np.asarray(inputs["L0_ffn_a"], np.float32)
            .reshape(NG, P, 16).transpose(1, 0, 2).reshape(P, NG * 16)),
        ffb=np.ascontiguousarray(inputs["L0_ffn_b"], np.float32),
        pvec=pvec, cident=ident,
        conescol=np.ones((P, 1), np.float32),
        conesrow=np.ones((1, P), np.float32),
        cinv1024=np.full((1, P), 1.0 / D, np.float32),
        conehot=onehot,
        czcol=np.zeros((P, 1), np.float32),
        ceps=np.full((P, 1), EPS, np.float32),
        czc16=np.zeros((P, 16), np.float32),
    )

    in_maps = []
    for c in range(8):
        b, q = c // 4, c % 4
        s0 = q * LC
        lo = s0 - HALO
        if lo < 0:
            xs = np.concatenate(
                [np.zeros((D, -lo), np.float32),
                 np.ascontiguousarray(x[b, 0:s0 + LC].T)], axis=1)
        else:
            xs = np.ascontiguousarray(x[b, lo:s0 + LC].T)
        first = 1.0 if q > 0 else 0.0
        m = dict(shared)
        m["xT"] = np.ascontiguousarray(xs)
        m["cme"] = np.full((P, 1), first, np.float32)
        m["cz01"] = np.full((P, 1), first, np.float32)
        in_maps.append(m)

    nc = build()
    res = run_bass_kernel_spmd(nc, in_maps, core_ids=list(range(8)))

    out = np.empty((B, S, D), np.float32)
    for c in range(8):
        b, q = c // 4, c % 4
        out[b, q * LC:(q + 1) * LC] = res.results[c]["outT"].T
    if DEBUG_TAPS:
        kernel.last_results = res
    return out


# version 23
# speedup vs baseline: 1.0819x; 1.0053x over previous
import numpy as np
import concourse.bass as bass
import concourse.mybir as mybir
from concourse.bass_utils import run_bass_kernel_spmd

F32 = mybir.dt.float32
I32 = mybir.dt.int32
AF = mybir.ActivationFunctionType
ALU = mybir.AluOpType
AX = mybir.AxisListType

P = 128
TT = 512
NT = 9
NG = 8
D = 1024
E = 8
FH = 2048
NGH = 16
HALO = 512
LC = 4096
TFULL = HALO + LC
EPS = 1e-5
REAL0 = 1
ATT0 = 0

DEBUG_TAPS = False

PV = {}
_pv_names = ["ln1_g", "ln1_b", "ln2_g", "ln2_b", "mu_k", "mu_v", "mu_r",
             "td", "tf", "ffn_bias"]
for _i, _n in enumerate(_pv_names):
    PV[_n] = _i * NG
PV["cm_k"] = 10 * NG
PV["cm_r"] = 10 * NG + 64
PCOLS = 10 * NG + 128


def build():
    nc = bass.Bass(target_bir_lowering=False)

    d_xT = nc.dram_tensor("xT", [D, TFULL], F32, kind="ExternalInput").ap()
    d_wk = nc.dram_tensor("Wk", [NG, P, NG, P], F32, kind="ExternalInput").ap()
    d_wv = nc.dram_tensor("Wv", [NG, P, NG, P], F32, kind="ExternalInput").ap()
    d_wr = nc.dram_tensor("Wr", [NG, P, NG, P], F32, kind="ExternalInput").ap()
    d_wo = nc.dram_tensor("Wo", [NG, P, NG, P], F32, kind="ExternalInput").ap()
    d_rw = nc.dram_tensor("router_w_s", [P, NG * E], F32, kind="ExternalInput").ap()
    d_ewk = nc.dram_tensor("eWk", [E, 2 * NG, P, NG, P], F32, kind="ExternalInput").ap()
    d_ewv = nc.dram_tensor("eWv", [E, NG, P, 2 * NG, P], F32, kind="ExternalInput").ap()
    d_ewr = nc.dram_tensor("eWr", [E, NG, P, NG, P], F32, kind="ExternalInput").ap()
    d_ffk = nc.dram_tensor("ffk", [D, D], F32, kind="ExternalInput").ap()
    d_ffa = nc.dram_tensor("ffa_s", [P, NG * 16], F32, kind="ExternalInput").ap()
    d_ffb = nc.dram_tensor("ffb", [16, D], F32, kind="ExternalInput").ap()
    d_pvec = nc.dram_tensor("pvec", [P, PCOLS], F32, kind="ExternalInput").ap()
    d_ident = nc.dram_tensor("cident", [P, P], F32, kind="ExternalInput").ap()
    d_onescol = nc.dram_tensor("conescol", [P, 1], F32, kind="ExternalInput").ap()
    d_onesrow = nc.dram_tensor("conesrow", [1, P], F32, kind="ExternalInput").ap()
    d_inv1024 = nc.dram_tensor("cinv1024", [1, P], F32, kind="ExternalInput").ap()
    d_onehot = nc.dram_tensor("conehot", [E, D], F32, kind="ExternalInput").ap()
    d_zcol = nc.dram_tensor("czcol", [P, 1], F32, kind="ExternalInput").ap()
    d_zc16 = nc.dram_tensor("czc16", [P, 16], F32, kind="ExternalInput").ap()
    d_me = nc.dram_tensor("cme", [P, 1], F32, kind="ExternalInput").ap()
    d_eps = nc.dram_tensor("ceps", [P, 1], F32, kind="ExternalInput").ap()
    d_z01 = nc.dram_tensor("cz01", [P, 1], F32, kind="ExternalInput").ap()
    d_out = nc.dram_tensor("outT", [D, LC], F32, kind="ExternalOutput").ap()
    d_wq = nc.dram_tensor("wq_scratch", [P, NG, NG, P], F32, kind="Internal").ap()

    taps = {}
    if DEBUG_TAPS:
        for nm, shp in [("tap_wkv", [D, TFULL]), ("tap_x2", [D, TFULL]),
                        ("tap_ei", [D, TFULL]), ("tap_xn", [D, TFULL]),
                        ("tap_tw", [P, 32 * NT]), ("tap_acc2", [D, TFULL])]:
            taps[nm] = nc.dram_tensor(nm, shp, F32, kind="ExternalOutput").ap()

    t_xn = nc.alloc_sbuf_tensor("t_xn", [P, NG * TT], F32)
    t_xn1 = nc.alloc_sbuf_tensor("t_xn1", [P, NG * TT], F32)
    t_acc = nc.alloc_sbuf_tensor("t_acc", [P, NG * TT], F32)
    t_ei = nc.alloc_sbuf_tensor("t_ei", [P, NG * TT], F32)
    t_A = nc.alloc_sbuf_tensor("t_A", [P, NG * TT], F32)
    t_B = nc.alloc_sbuf_tensor("t_B", [P, NG * TT], F32)
    t_h = nc.alloc_sbuf_tensor("t_h", [P, NGH * TT], F32)

    s_EK = nc.alloc_sbuf_tensor("s_EK", [P, TT], F32)
    s_EV = nc.alloc_sbuf_tensor("s_EV", [P, TT], F32)
    s_BsN = nc.alloc_sbuf_tensor("s_BsN", [P, TT], F32)
    s_BsD = nc.alloc_sbuf_tensor("s_BsD", [P, TT], F32)
    s_t1 = nc.alloc_sbuf_tensor("s_t1", [P, TT], F32)
    s_pl = nc.alloc_sbuf_tensor("s_pl", [P, TT], F32)
    s_pl2 = nc.alloc_sbuf_tensor("s_pl2", [P, TT], F32)
    s_t2 = nc.alloc_sbuf_tensor("s_t2", [P, TT], F32)
    b_mean = nc.alloc_sbuf_tensor("b_mean", [P, TT], F32)
    b_var = nc.alloc_sbuf_tensor("b_var", [P, TT], F32)
    b_rstd = nc.alloc_sbuf_tensor("b_rstd", [P, TT], F32)
    b_rstd1 = nc.alloc_sbuf_tensor("b_rstd1", [P, TT], F32)

    t_acc2 = nc.alloc_sbuf_tensor("t_acc2", [P, NG * TT], F32)
    wS = [nc.alloc_sbuf_tensor(f"wS{i}", [P, 4 * P], F32) for i in range(2)]
    ewkS = [nc.alloc_sbuf_tensor(f"ewkS{i}", [P, 4 * P], F32) for i in range(2)]
    ewvS = [nc.alloc_sbuf_tensor(f"ewvS{i}", [P, NG * P], F32) for i in range(2)]
    ewrS = [nc.alloc_sbuf_tensor(f"ewrS{i}", [P, NG * P], F32) for i in range(2)]

    t_la = b_rstd1[0:16, :]
    t_sums = b_mean[0:1, :]
    t_sumsq = b_var[0:1, :]
    t_pv = nc.alloc_sbuf_tensor("t_pv", [P, PCOLS], F32)
    t_gd = nc.alloc_sbuf_tensor("t_gd", [P, NG], F32)
    t_eu = nc.alloc_sbuf_tensor("t_eu", [P, NG], F32)
    t_nk = nc.alloc_sbuf_tensor("t_nk", [P, 64], F32)
    t_nr = nc.alloc_sbuf_tensor("t_nr", [P, 64], F32)
    t_e1 = nc.alloc_sbuf_tensor("t_e1", [P, NG], F32)
    t_cN = nc.alloc_sbuf_tensor("t_cN", [P, NG], F32)
    t_cD = nc.alloc_sbuf_tensor("t_cD", [P, NG], F32)
    t_tailA = nc.alloc_sbuf_tensor("t_tailA", [P, NG], F32)
    t_tailE = nc.alloc_sbuf_tensor("t_tailE", [P, NG], F32)
    t_mtail = nc.alloc_sbuf_tensor("t_mtail", [P, NG], F32)
    t_id = nc.alloc_sbuf_tensor("t_id", [P, P], F32)
    t_oc = nc.alloc_sbuf_tensor("t_oc", [P, 1], F32)
    t_or = nc.alloc_sbuf_tensor("t_or", [1, P], F32)
    t_iv = nc.alloc_sbuf_tensor("t_iv", [1, P], F32)
    t_oh = nc.alloc_sbuf_tensor("t_oh", [E, D], F32)
    t_zc = nc.alloc_sbuf_tensor("t_zc", [P, 1], F32)
    t_me = nc.alloc_sbuf_tensor("t_me", [P, 1], F32)
    t_eps = nc.alloc_sbuf_tensor("t_eps", [P, 1], F32)
    t_z01 = nc.alloc_sbuf_tensor("t_z01", [P, 1], F32)
    t_rws = nc.alloc_sbuf_tensor("t_rws", [P, NG * E], F32)
    t_as = nc.alloc_sbuf_tensor("t_as", [P, NG * 16], F32)
    t_bs = nc.alloc_sbuf_tensor("t_bs", [16, D], F32)
    t_lg = nc.alloc_sbuf_tensor("t_lg", [P, 32], F32)
    t_m1 = nc.alloc_sbuf_tensor("t_m1", [P, 4], F32)
    t_m2 = nc.alloc_sbuf_tensor("t_m2", [P, 4], F32)
    t_d12 = nc.alloc_sbuf_tensor("t_d12", [P, 4], F32)
    t_w1 = nc.alloc_sbuf_tensor("t_w1", [P, 4], F32)
    t_mk1 = nc.alloc_sbuf_tensor("t_mk1", [P, 32], F32)
    t_mk2 = nc.alloc_sbuf_tensor("t_mk2", [P, 32], F32)
    t_twm = nc.alloc_sbuf_tensor("t_twm", [P, 32], F32)
    t_twT = nc.alloc_sbuf_tensor("t_twT", [E, TT], F32)
    t_mx8 = nc.alloc_sbuf_tensor("t_mx8", [P, NG], F32)
    t_mx1 = nc.alloc_sbuf_tensor("t_mx1", [P, 1], F32)
    t_mrow = nc.alloc_sbuf_tensor("t_mrow", [1, P], F32)
    t_M = nc.alloc_sbuf_tensor("t_M", [1, 1], F32)
    t_scB = nc.alloc_sbuf_tensor("t_scB", [P, 1], F32)
    t_isB = nc.alloc_sbuf_tensor("t_isB", [P, 1], F32)

    PSN = 5
    ps_f = [nc.alloc_psum_tensor(f"ps_f{i}", [P, TT], F32) for i in range(PSN)]
    ps_s1 = nc.alloc_psum_tensor("ps_s1", [P, TT], F32)
    ps_s2 = nc.alloc_psum_tensor("ps_s2", [P, TT], F32)
    ps_s3 = nc.alloc_psum_tensor("ps_s3", [P, TT], F32)

    prog = {k: [] for k in ["pe", "dve", "act", "sp", "pl"]}
    WCH = ["w0", "w1", "w0b", "w1b", "ek0", "ek1", "ev0", "ev1", "er0", "er1"]
    cnt = {"pe": 0, "dve": 0, "act": 0, "ld": 0, "q": 0, "x": 0, "st": 0, "plc": 0}
    for _c in WCH:
        cnt[_c] = 0

    def emit(q, fn):
        prog[q].append(("op", fn, None, 0))

    def emiti(q, fn, ch, amt=1):
        if (q in ("dve", "act") and ch == q) or (q == "pl" and ch == "plc"):
            prog[q].append(("wait", ch, cnt[ch], 0))
        cnt[ch] += amt
        prog[q].append(("op", fn, ch, amt))

    def wait(q, ch, val):
        if val > 0:
            prog[q].append(("wait", ch, val, 0))

    bank_free = [("dve", 0)] * PSN
    bank_i = [0]

    def next_bank():
        i = bank_i[0] % PSN
        bank_i[0] += 1
        ch, v = bank_free[i]
        wait("pe", ch, v)
        return i

    def bank_consumed(i, ch):
        bank_free[i] = (ch, cnt[ch])

    def dma(dst, src):
        return lambda e: e.dma_start(dst, src)

    def fw(tens, g):
        return tens[:, g * TT:(g + 1) * TT]

    def fw3(tens, ngroups=NG):
        return tens.ap().rearrange("p (g n) -> p g n", n=TT)[:, 0:ngroups, :]

    def pcol(name, g):
        c = PV[name] + g
        return t_pv[:, c:c + 1]

    def wchunk(buf, kc):
        return buf[:, kc * P:(kc + 1) * P]

    def mblock_src(w_ap, og, kc0=0, nkc=None):
        if nkc is None:
            sl = w_ap[:, og * P:(og + 1) * P]
        else:
            sl = w_ap[kc0 * P:(kc0 + nkc) * P, og * P:(og + 1) * P]
        return sl.rearrange("(c p) m -> p c m", p=P)

    def wblock_dst(buf, nkc):
        return buf.ap().rearrange("p (c m) -> p c m", m=P)[:, 0:nkc, :]

    for dst, src in [(t_pv[:, :], d_pvec), (t_id[:, :], d_ident),
                     (t_oc[:, :], d_onescol), (t_or[:, :], d_onesrow),
                     (t_iv[:, :], d_inv1024), (t_oh[:, :], d_onehot),
                     (t_zc[:, :], d_zcol), (t_eps[:, :], d_eps),
                     (t_cN[:, :], d_zc16[:, 0:8]),
                     (t_cD[:, :], d_zc16[:, 8:16]), (t_me[:, :], d_me),
                     (t_z01[:, :], d_z01),
                     (t_rws[:, :], d_rw),
                     (t_as[:, :], d_ffa),
                     (t_bs[:, :], d_ffb)]:
        emiti("sp", dma(dst, src), "ld", 16)
    SETUP_LD = cnt["ld"]

    wait("act", "ld", SETUP_LD)
    emiti("act", lambda e: e.activation(t_e1[:, :], t_pv[:, PV["td"]:PV["td"] + NG], AF.Exp), "act")
    wait("act", "act", cnt["act"])
    emiti("act", lambda e: e.activation(t_gd[:, :], t_e1[:, :], AF.Exp, scale=-1.0), "act")
    emiti("act", lambda e: e.activation(t_eu[:, :], t_pv[:, PV["tf"]:PV["tf"] + NG], AF.Exp), "act")
    SETUP_ACT = cnt["act"]
    wait("dve", "ld", SETUP_LD)
    emiti("dve", lambda e: e.tensor_scalar(t_nk[:, :], t_pv[:, PV["cm_k"]:PV["cm_k"] + 64], 1.0, None, ALU.subtract), "dve")
    emiti("dve", lambda e: e.tensor_scalar(t_nr[:, :], t_pv[:, PV["cm_r"]:PV["cm_r"] + 64], 1.0, None, ALU.subtract), "dve")
    wait("dve", "dve", cnt["dve"])

    QLOAD = ewrS[1][:, 0:D]
    for kc in range(NG):
        emiti("sp", dma(QLOAD, d_ffk[kc * P:(kc + 1) * P, :]), "q", 16)
        wait("dve", "q", cnt["q"])
        emiti("dve", (lambda kc: lambda e: e.reduce_max(
            t_mx8[:, kc:kc + 1], QLOAD, axis=AX.X, apply_absolute_value=True))(kc), "dve")
        wait("sp", "dve", cnt["dve"])
    emiti("dve", lambda e: e.reduce_max(t_mx1[:, :], t_mx8[:, :], axis=AX.X), "dve")
    wait("pe", "ld", SETUP_LD)
    wait("pe", "dve", cnt["dve"])
    emiti("pe", lambda e: e.matmul(ps_s1[0:1, 0:P], t_mx1[:, :], t_id[:, :], start=True, stop=True), "pe")
    wait("dve", "pe", cnt["pe"])
    emiti("dve", lambda e: e.tensor_copy(t_mrow[:, :], ps_s1[0:1, 0:P]), "dve")
    wait("dve", "dve", cnt["dve"])
    emiti("dve", lambda e: e.reduce_max(t_M[:, :], t_mrow[:, :], axis=AX.X), "dve")
    wait("pe", "dve", cnt["dve"])
    emiti("pe", lambda e: e.matmul(ps_s3[0:P, 0:1], t_or[:, :], t_M[:, :], start=True, stop=True), "pe")
    wait("dve", "pe", cnt["pe"])
    emiti("dve", lambda e: e.tensor_scalar(t_scB[:, :], ps_s3[0:P, 0:1], 1.0 / 7.0, None, ALU.mult), "dve")
    wait("dve", "dve", cnt["dve"])
    emiti("dve", lambda e: e.reciprocal(t_isB[:, :], t_scB[:, :]), "dve")
    wait("pl", "dve", cnt["dve"])
    QU = ewvS[0][:, 0:D]
    QP = ewvS[1][:, 0:D]
    QQ = ewrS[0][:, 0:D]
    for kc in range(NG):
        wait("sp", "q", cnt["q"])
        emiti("sp", dma(QLOAD, d_ffk[kc * P:(kc + 1) * P, :]), "q", 16)
        wait("dve", "q", cnt["q"])
        emiti("dve", lambda e: e.tensor_scalar(QU, QLOAD, t_isB[:, 0:1], None, ALU.mult), "dve")
        emiti("dve", lambda e: e.tensor_scalar(QP, QU, -7.5, None, ALU.is_ge), "dve")
        cur, nxt = QP, QQ
        for j in range(-7, 7):
            thr = j + 0.5
            emiti("dve", (lambda thr, cur, nxt: lambda e: e.scalar_tensor_tensor(
                nxt, QU, thr, cur, ALU.is_ge, ALU.add))(thr, cur, nxt), "dve")
            cur, nxt = nxt, cur
        emiti("dve", (lambda cur, nxt: lambda e: e.tensor_scalar(
            nxt, cur, 8.0, None, ALU.subtract))(cur, nxt), "dve")
        q8 = nxt
        emiti("dve", (lambda q8: lambda e: e.tensor_scalar(
            QU, q8, t_scB[:, 0:1], None, ALU.mult))(q8), "dve")
        wait("sp", "dve", cnt["dve"])
        emiti("sp", (lambda kc: dma(d_wq[:, :, kc, :],
                                    QU.rearrange("p (o m) -> p o m", m=P)))(kc), "q", 16)
        wait("dve", "q", cnt["q"])
    WQ_LD = cnt["q"]

    wchan = {id(wS[0]): "w0", id(wS[1]): "w1",
             id(ewkS[0]): "ek0", id(ewkS[1]): "ek1",
             id(ewvS[0]): "ev0", id(ewvS[1]): "ev1",
             id(ewrS[0]): "er0", id(ewrS[1]): "er1"}
    wfree = {id(wS[0]): ("pe", 0), id(wS[1]): ("pe", 0),
             id(ewkS[0]): ("pe", 0), id(ewkS[1]): ("pe", 0),
             id(ewvS[0]): ("q", WQ_LD), id(ewvS[1]): ("q", WQ_LD),
             id(ewrS[0]): ("q", WQ_LD), id(ewrS[1]): ("q", WQ_LD)}

    def stream_load(buf, dst3, src_ap):
        ch, v = wfree[id(buf)]
        wait("sp", ch, v)
        wc = wchan[id(buf)]
        emiti("sp", dma(dst3, src_ap), wc, 16)
        return (wc, cnt[wc])

    wchanB = {id(wS[0]): "w0b", id(wS[1]): "w1b"}

    def stream_load2(buf, srcfn, kc0):
        ch, v = wfree[id(buf)]
        wait("sp", ch, v)
        wc, wcb = wchan[id(buf)], wchanB[id(buf)]
        d3 = buf.ap().rearrange("p (c m) -> p c m", m=P)
        emiti("sp", dma(d3[:, 0:2, :], srcfn(kc0, 2)), wc, 16)
        emiti("sp", dma(d3[:, 2:4, :], srcfn(kc0 + 2, 2)), wcb, 16)
        return [(wc, cnt[wc]), (wcb, cnt[wcb])]

    def consumed(buf, ch):
        wfree[id(buf)] = (ch, cnt[ch])

    def half_proj(w_dram, og, rhs_fw, bi, first_extra_wait=None):
        for h in range(2):
            buf = wS[wslot[0] % 2]; wslot[0] += 1
            for tk in stream_load2(
                    buf, lambda k0, n: w_dram[og, :, k0:k0 + n, :], h * 4):
                wait("pe", tk[0], tk[1])
            if h == 0 and first_extra_wait is not None:
                wait("pe", "dve", first_extra_wait)
            for kc in range(4):
                emiti("pe", (lambda buf, kc, bi, h: lambda e: e.matmul(
                    ps_f[bi][:, :], wchunk(buf, kc), rhs_fw(h * 4 + kc),
                    start=(h == 0 and kc == 0), stop=(h == 1 and kc == 3)))(buf, kc, bi, h), "pe")
            consumed(buf, "pe")

    wait("dve", "act", SETUP_ACT)
    wait("act", "ld", SETUP_LD)
    wait("sp", "q", WQ_LD)

    xload_ld = [0]
    store_done = [0]
    stat_free = [0]
    ps2_free = [0]
    ps3_free = [0]
    bfree_pe = [0]
    pe_sq = [0] * NG
    tile_end_dve = [0]
    wslot = [0]
    ei_done_s = [0]

    emiti("pl", dma(fw3(t_xn1), d_xT.rearrange("(g p) n -> p g n", p=P)[:, :, 0:TT]), "x", 16)
    xload_ld[0] = cnt["x"]

    def ln_stats(src_fw, prev_dve_guard):
        wait("pe", "dve", stat_free[0])
        wait("pe", "dve", prev_dve_guard)
        for g in range(NG):
            emiti("pe", (lambda g: lambda e: e.matmul(
                ps_s1[0:1, :], t_oc[:, :], src_fw(g), start=(g == 0), stop=(g == NG - 1)))(g), "pe")
        wait("act", "dve", prev_dve_guard)
        for g in range(NG):
            sq = s_t1 if g % 2 == 0 else s_t2
            if g >= 2:
                wait("act", "pe", pe_sq[g - 2])
            emiti("act", (lambda g, sq: lambda e: e.activation(sq[:, :], src_fw(g), AF.Square))(g, sq), "act")
            wait("pe", "act", cnt["act"])
            emiti("pe", (lambda g, sq: lambda e: e.matmul(
                ps_s2[0:1, :], t_oc[:, :], sq[:, :], start=(g == 0), stop=(g == NG - 1)))(g, sq), "pe")
            pe_sq[g] = cnt["pe"]
        wait("dve", "pe", cnt["pe"])
        emiti("dve", lambda e: e.tensor_copy(t_sums, ps_s1[0:1, :]), "dve")
        emiti("dve", lambda e: e.tensor_copy(t_sumsq, ps_s2[0:1, :]), "dve")
        wait("dve", "dve", cnt["dve"])
        wait("pe", "dve", cnt["dve"])
        emiti("pe", lambda e: e.matmul(ps_s1[:, :], t_iv[:, :], t_sums, start=True, stop=True), "pe")
        emiti("pe", lambda e: e.matmul(ps_s2[:, :], t_iv[:, :], t_sumsq, start=True, stop=True), "pe")
        wait("dve", "pe", cnt["pe"])
        emiti("dve", lambda e: e.tensor_copy(b_mean[:, :], ps_s1[:, :]), "dve")
        emiti("dve", lambda e: e.tensor_copy(b_var[:, :], ps_s2[:, :]), "dve")
        stat_free[0] = cnt["dve"]
        emiti("dve", lambda e: e.tensor_mul(s_t1[:, :], b_mean[:, :], b_mean[:, :]), "dve")
        emiti("dve", lambda e: e.tensor_sub(s_t2[:, :], b_var[:, :], s_t1[:, :]), "dve")
        wait("act", "dve", cnt["dve"])
        emiti("act", lambda e: e.activation(s_t1[:, :], s_t2[:, :], AF.Sqrt, bias=t_eps[:, 0:1]), "act")
        wait("dve", "act", cnt["act"])
        emiti("dve", lambda e: e.reciprocal(b_rstd[:, :], s_t1[:, :]), "dve")

    def emit_tile(t):
        is_att = t >= ATT0
        is_real = t >= REAL0
        x_t = t_xn1
        xld = xload_ld[0]

        wait("pe", "x", xld)
        wait("act", "x", xld)
        ln_stats(lambda g: fw(x_t, g), tile_end_dve[0])
        emiti("dve", lambda e: e.tensor_mul(s_t1[:, :], b_rstd[:, :], b_rstd[:, :]), "dve")
        emiti("dve", lambda e: e.tensor_mul(b_rstd1[:, :], s_t1[:, :], s_t2[:, :]), "dve")
        wait("act", "dve", cnt["dve"])
        emiti("act", lambda e: e.activation(s_t2[:, :], b_rstd1[:, :], AF.Sqrt, bias=t_eps[:, 0:1]), "act")
        wait("dve", "act", cnt["act"])
        emiti("dve", lambda e: e.reciprocal(b_rstd1[:, :], s_t2[:, :]), "dve")
        for g in range(NG):
            emiti("dve", (lambda g: lambda e: e.tensor_sub(s_t1[:, :], fw(x_t, g), b_mean[:, :]))(g), "dve")
            emiti("dve", (lambda g: lambda e: e.tensor_mul(fw(t_xn, g), s_t1[:, :], b_rstd[:, :]))(g), "dve")
        if is_real:
            wait("dve", "st", store_done[0])
            for g in range(NG):
                emiti("dve", (lambda g: lambda e: e.scalar_tensor_tensor(
                    fw(t_acc, g), fw(x_t, g), pcol("ffn_bias", g), fw(t_xn, g),
                    ALU.add, ALU.add))(g), "dve")
        acc_init = cnt["dve"]
        for g in range(NG):
            emiti("dve", (lambda g: lambda e: e.tensor_mul(s_t1[:, :], fw(t_xn, g), b_rstd1[:, :]))(g), "dve")
            emiti("dve", (lambda g: lambda e: e.tensor_scalar(
                fw(t_xn1, g), s_t1[:, :], pcol("ln1_g", g), pcol("ln1_b", g),
                ALU.mult, ALU.add))(g), "dve")
        xn1_done = cnt["dve"]
        xn_done = cnt["dve"]
        if DEBUG_TAPS:
            wait("pl", "dve", xn_done)
            emiti("pl", (lambda t: lambda e: e.dma_start(
                taps["tap_xn"].rearrange("(g p) n -> p g n", p=P)[:, :, t * TT:(t + 1) * TT],
                fw3(t_xn)))(t), "st", 16)

        if is_real:
            wait("pe", "dve", xn_done)
            wait("pe", "dve", ps3_free[0])
            for g in range(NG):
                emiti("pe", (lambda g: lambda e: e.matmul(
                    ps_s3[0:16, :], t_as[:, g * 16:(g + 1) * 16], fw(t_xn, g),
                    start=(g == 0), stop=(g == NG - 1)))(g), "pe")
            wait("dve", "pe", cnt["pe"])
            emiti("dve", lambda e: e.tensor_copy(t_la, ps_s3[0:16, :]), "dve")
            wait("dve", "dve", cnt["dve"])
            ps3_free[0] = cnt["dve"]
            la_done = cnt["dve"]
            for og in range(NG):
                bi = next_bank()
                for h in range(2):
                    buf = wS[wslot[0] % 2]; wslot[0] += 1
                    for tk in stream_load2(
                            buf, (lambda og: lambda k0, n: d_wq[:, og, k0:k0 + n, :])(og), h * 4):
                        wait("pe", tk[0], tk[1])
                    if h == 0 and og == 0:
                        wait("pe", "dve", la_done)
                    for kc in range(4):
                        emiti("pe", (lambda buf, kc, bi, h: lambda e: e.matmul(
                            ps_f[bi][:, :], wchunk(buf, kc), fw(t_xn, h * 4 + kc),
                            start=(h == 0 and kc == 0), stop=False))(buf, kc, bi, h), "pe")
                    consumed(buf, "pe")
                emiti("pe", (lambda og, bi: lambda e: e.matmul(
                    ps_f[bi][:, :], t_bs[:, og * P:(og + 1) * P], t_la,
                    start=False, stop=True))(og, bi), "pe")
                wait("dve", "pe", cnt["pe"])
                wait("dve", "dve", acc_init)
                emiti("dve", (lambda og, bi: lambda e: e.tensor_add(
                    fw(t_acc2, og), fw(t_acc, og), ps_f[bi][:, :]))(og, bi), "dve")
                bank_consumed(bi, "dve")

        tail_src = t_mtail if t == REAL0 else t_tailA
        if t == REAL0:
            wait("dve", "dve", xn1_done)
            emiti("dve", lambda e: e.tensor_scalar(
                t_mtail[:, :], t_tailA[:, :], t_z01[:, 0:1], None, ALU.mult), "dve")
            wait("dve", "dve", cnt["dve"])
        wait("dve", "dve", xn1_done)
        for g in range(NG):
            xx0 = t_zc[:, 0:1] if t == 0 else tail_src[:, g:g + 1]
            emiti("dve", (lambda g, xx0: lambda e: e.tensor_sub(
                s_t1[:, 0:1], fw(t_xn1, g)[:, 0:1], xx0))(g, xx0), "dve")
            emiti("dve", (lambda g: lambda e: e.tensor_sub(
                s_t1[:, 1:TT], fw(t_xn1, g)[:, 1:TT], fw(t_xn1, g)[:, 0:TT - 1]))(g), "dve")
            for name, dst in (("mu_k", t_A), ("mu_v", t_B)):
                emiti("dve", (lambda g, name, dst, xx0: lambda e: e.scalar_tensor_tensor(
                    fw(dst, g)[:, 0:1], s_t1[:, 0:1], pcol(name, g), xx0,
                    ALU.mult, ALU.add))(g, name, dst, xx0), "dve")
                emiti("dve", (lambda g, name, dst: lambda e: e.scalar_tensor_tensor(
                    fw(dst, g)[:, 1:TT], s_t1[:, 1:TT], pcol(name, g),
                    fw(t_xn1, g)[:, 0:TT - 1], ALU.mult, ALU.add))(g, name, dst), "dve")
        mix_kv = cnt["dve"]

        ek_guard = [mix_kv]
        wait("pe", "dve", mix_kv)
        for og in range(NG):
            bik = next_bank()
            half_proj(d_wk, og, lambda kc: fw(t_A, kc), bik)
            kdone = cnt["pe"]
            biv = next_bank()
            half_proj(d_wv, og, lambda kc: fw(t_B, kc), biv)
            vdone = cnt["pe"]
            wait("act", "pe", kdone)
            wait("act", "dve", ek_guard[0])
            emiti("act", (lambda bik: lambda e: e.activation(s_EK[:, :], ps_f[bik][:, :], AF.Exp))(bik), "act")
            bank_consumed(bik, "act")
            wait("dve", "act", cnt["act"])
            wait("dve", "pe", vdone)
            emiti("dve", (lambda biv: lambda e: e.tensor_mul(s_EV[:, :], s_EK[:, :], ps_f[biv][:, :]))(biv), "dve")
            bank_consumed(biv, "dve")
            gbb = t_gd[:, og:og + 1].broadcast_to([P, TT])
            if t < REAL0:
                emiti("dve", lambda e: e.tensor_scalar(s_t2[:, :], s_EV[:, :], t_me[:, 0:1], None, ALU.mult), "dve")
                emiti("dve", lambda e: e.tensor_scalar(s_t1[:, :], s_EK[:, :], t_me[:, 0:1], None, ALU.mult), "dve")
                dsrc = s_t1
                nsrc = s_t2
            else:
                dsrc = s_EK
                nsrc = s_EV
            emiti("dve", (lambda og, gbb, nsrc: lambda e: e.tensor_tensor_scan(
                s_BsN[:, :], gbb, nsrc[:, :], t_cN[:, og:og + 1], ALU.mult, ALU.add))(og, gbb, nsrc), "dve")
            emiti("dve", (lambda og, gbb, dsrc: lambda e: e.tensor_tensor_scan(
                s_BsD[:, :], gbb, dsrc[:, :], t_cD[:, og:og + 1], ALU.mult, ALU.add))(og, gbb, dsrc), "dve")
            if is_att:
                emiti("dve", (lambda og: lambda e: e.scalar_tensor_tensor(
                    s_t1[:, 0:1], s_EV[:, 0:1], t_eu[:, og:og + 1], t_cN[:, og:og + 1],
                    ALU.mult, ALU.add))(og), "dve")
                emiti("dve", (lambda og: lambda e: e.scalar_tensor_tensor(
                    s_t1[:, 1:TT], s_EV[:, 1:TT], t_eu[:, og:og + 1], s_BsN[:, 0:TT - 1],
                    ALU.mult, ALU.add))(og), "dve")
                emiti("dve", (lambda og: lambda e: e.scalar_tensor_tensor(
                    s_t2[:, 0:1], s_EK[:, 0:1], t_eu[:, og:og + 1], t_cD[:, og:og + 1],
                    ALU.mult, ALU.add))(og), "dve")
                emiti("dve", (lambda og: lambda e: e.scalar_tensor_tensor(
                    s_t2[:, 1:TT], s_EK[:, 1:TT], t_eu[:, og:og + 1], s_BsD[:, 0:TT - 1],
                    ALU.mult, ALU.add))(og), "dve")
                emiti("dve", lambda e: e.reciprocal(s_EK[:, :], s_t2[:, :]), "dve")
                emiti("dve", (lambda og: lambda e: e.tensor_mul(
                    fw(t_h, og), s_t1[:, :], s_EK[:, :]))(og), "dve")
            ek_guard[0] = cnt["dve"]
            emiti("dve", (lambda og: lambda e: e.tensor_copy(
                t_cN[:, og:og + 1], s_BsN[:, TT - 1:TT]))(og), "dve")
            emiti("dve", (lambda og: lambda e: e.tensor_copy(
                t_cD[:, og:og + 1], s_BsD[:, TT - 1:TT]))(og), "dve")
            wait("dve", "dve", cnt["dve"])
        scan_done = cnt["dve"]

        kv_pe_end = cnt["pe"]
        if is_att:
            wait("dve", "pe", kv_pe_end)
            for g in range(NG):
                xx0 = t_zc[:, 0:1] if t == 0 else tail_src[:, g:g + 1]
                emiti("dve", (lambda g, xx0: lambda e: e.tensor_sub(
                    s_pl[:, 0:1], fw(t_xn1, g)[:, 0:1], xx0))(g, xx0), "dve")
                emiti("dve", (lambda g: lambda e: e.tensor_sub(
                    s_pl[:, 1:TT], fw(t_xn1, g)[:, 1:TT], fw(t_xn1, g)[:, 0:TT - 1]))(g), "dve")
                emiti("dve", (lambda g, xx0: lambda e: e.scalar_tensor_tensor(
                    fw(t_A, g)[:, 0:1], s_pl[:, 0:1], pcol("mu_r", g), xx0,
                    ALU.mult, ALU.add))(g, xx0), "dve")
                emiti("dve", (lambda g: lambda e: e.scalar_tensor_tensor(
                    fw(t_A, g)[:, 1:TT], s_pl[:, 1:TT], pcol("mu_r", g),
                    fw(t_xn1, g)[:, 0:TT - 1], ALU.mult, ALU.add))(g), "dve")
        xr_done = cnt["dve"]
        emiti("dve", lambda e: e.tensor_copy(
            t_tailA[:, :], fw3(t_xn1)[:, :, TT - 1:TT]), "dve")
        tailA_done = cnt["dve"]
        if t + 1 < NT:
            wait("pl", "dve", tailA_done)
            emiti("pl", (lambda t: lambda e: e.dma_start(
                fw3(t_xn1), d_xT.rearrange("(g p) n -> p g n", p=P)[:, :, (t + 1) * TT:(t + 2) * TT]))(t), "x", 16)
            xload_ld[0] = cnt["x"]

        if is_att:
            if DEBUG_TAPS:
                wait("pl", "dve", scan_done)
                emiti("pl", (lambda t: lambda e: e.dma_start(
                    taps["tap_wkv"].rearrange("(g p) n -> p g n", p=P)[:, :, t * TT:(t + 1) * TT],
                    fw3(t_h)))(t), "st", 16)
            rw_guard = [scan_done, scan_done]
            wait("pe", "dve", xr_done)
            for og in range(NG):
                bir = next_bank()
                half_proj(d_wr, og, lambda kc: fw(t_A, kc), bir)
                rbuf = s_t1 if og % 2 == 0 else s_t2
                wait("act", "pe", cnt["pe"])
                wait("act", "dve", rw_guard[og % 2])
                emiti("act", (lambda bir, rbuf: lambda e: e.activation(
                    rbuf[:, :], ps_f[bir][:, :], AF.Sigmoid))(bir, rbuf), "act")
                bank_consumed(bir, "act")
                wait("dve", "act", cnt["act"])
                if DEBUG_TAPS:
                    wait("dve", "st", cnt["st"])
                emiti("dve", (lambda og, rbuf: lambda e: e.tensor_mul(
                    fw(t_h, NG + og), fw(t_h, og), rbuf[:, :]))(og, rbuf), "dve")
                rw_guard[og % 2] = cnt["dve"]
            rw_done = cnt["dve"]

            wait("pe", "dve", rw_done)
            for og in range(NG):
                bio = next_bank()
                half_proj(d_wo, og, lambda kc: fw(t_h, NG + kc), bio)
                wait("dve", "pe", cnt["pe"])
                if is_real:
                    emiti("dve", (lambda og, bio: lambda e: e.tensor_add(
                        fw(t_acc, og), fw(t_acc2, og), ps_f[bio][:, :]))(og, bio), "dve")
                emiti("dve", (lambda og, bio: lambda e: e.tensor_add(
                    fw(t_acc2, og), fw(t_xn, og), ps_f[bio][:, :]))(og, bio), "dve")
                bank_consumed(bio, "dve")
            x2_done = cnt["dve"]
            if DEBUG_TAPS and is_real:
                wait("pl", "dve", x2_done)
                emiti("pl", (lambda t: lambda e: e.dma_start(
                    taps["tap_acc2"].rearrange("(g p) n -> p g n", p=P)[:, :, t * TT:(t + 1) * TT],
                    fw3(t_acc)))(t), "st", 16)
            if DEBUG_TAPS:
                wait("pl", "dve", x2_done)
                emiti("pl", (lambda t: lambda e: e.dma_start(
                    taps["tap_x2"].rearrange("(g p) n -> p g n", p=P)[:, :, t * TT:(t + 1) * TT],
                    fw3(t_acc2)))(t), "st", 16)

            ln_stats(lambda g: fw(t_acc2, g), x2_done)
            for g in range(NG):
                emiti("dve", (lambda g: lambda e: e.tensor_sub(
                    fw(t_ei, g), fw(t_acc2, g), b_mean[:, :]))(g), "dve")
                emiti("dve", (lambda g: lambda e: e.tensor_mul(
                    s_t1[:, :], fw(t_ei, g), b_rstd[:, :]))(g), "dve")
                emiti("dve", (lambda g: lambda e: e.tensor_scalar(
                    fw(t_ei, g), s_t1[:, :], pcol("ln2_g", g), pcol("ln2_b", g),
                    ALU.mult, ALU.add))(g), "dve")
            ei_done_s[0] = cnt["dve"]
            if DEBUG_TAPS:
                wait("pl", "dve", ei_done_s[0])
                emiti("pl", (lambda t: lambda e: e.dma_start(
                    taps["tap_ei"].rearrange("(g p) n -> p g n", p=P)[:, :, t * TT:(t + 1) * TT],
                    fw3(t_ei)))(t), "st", 16)

        if is_real:
            ei_done = ei_done_s[0]
            wait("pe", "dve", ei_done)
            wait("pe", "dve", stat_free[0])
            for sb in range(4):
                for g in range(NG):
                    emiti("pe", (lambda sb, g: lambda e: e.matmul(
                        ps_s1[0:P, sb * 8:(sb + 1) * 8],
                        fw(t_ei, g)[:, sb * P:(sb + 1) * P],
                        t_rws[:, g * 8:(g + 1) * 8],
                        start=(g == 0), stop=(g == NG - 1)))(sb, g), "pe")
            wait("dve", "pe", cnt["pe"])
            emiti("dve", lambda e: e.tensor_copy(t_lg[:, :], ps_s1[0:P, 0:32]), "dve")
            wait("dve", "dve", cnt["dve"])
            lg3 = t_lg.ap().rearrange("p (c e) -> p c e", e=E)
            mk13 = t_mk1.ap().rearrange("p (c e) -> p c e", e=E)
            mk23 = t_mk2.ap().rearrange("p (c e) -> p c e", e=E)
            tw3 = t_twm.ap().rearrange("p (c e) -> p c e", e=E)
            emiti("dve", lambda e: e.reduce_max(t_m1[:, :], lg3, axis=AX.X), "dve")
            wait("dve", "dve", cnt["dve"])
            emiti("dve", lambda e: e.tensor_tensor(
                mk13, lg3, t_m1.ap().unsqueeze(2).broadcast_to([P, 4, E]), ALU.is_equal), "dve")
            emiti("dve", lambda e: e.scalar_tensor_tensor(
                t_mk2[:, :], t_mk1[:, :], -1e30, t_lg[:, :], ALU.mult, ALU.add), "dve")
            wait("dve", "dve", cnt["dve"])
            emiti("dve", lambda e: e.reduce_max(t_m2[:, :], mk23, axis=AX.X), "dve")
            wait("dve", "dve", cnt["dve"])
            emiti("dve", lambda e: e.tensor_tensor(
                mk23, lg3, t_m2.ap().unsqueeze(2).broadcast_to([P, 4, E]), ALU.is_equal), "dve")
            emiti("dve", lambda e: e.tensor_sub(t_d12[:, :], t_m1[:, :], t_m2[:, :]), "dve")
            wait("act", "dve", cnt["dve"])
            emiti("act", lambda e: e.activation(t_w1[:, :], t_d12[:, :], AF.Sigmoid), "act")
            wait("dve", "act", cnt["act"])
            emiti("dve", lambda e: e.tensor_sub(t_lg[:, :], t_mk1[:, :], t_mk2[:, :]), "dve")
            emiti("dve", lambda e: e.tensor_tensor(
                t_twm.ap().rearrange("p (c e) -> p c e", e=E), t_lg.ap().rearrange("p (c e) -> p c e", e=E),
                t_w1.ap().unsqueeze(2).broadcast_to([P, 4, E]), ALU.mult), "dve")
            emiti("dve", lambda e: e.tensor_add(t_lg[:, :], t_twm[:, :], t_mk2[:, :]), "dve")
            emiti("dve", lambda e: e.tensor_copy(t_twm[:, :], t_lg[:, :]), "dve")
            wait("dve", "dve", cnt["dve"])
            tw_done = cnt["dve"]
            if DEBUG_TAPS:
                wait("pl", "dve", tw_done)
                emiti("pl", (lambda t: lambda e: e.dma_start(
                    taps["tap_tw"][:, t * 32:(t + 1) * 32], t_twm[:, :]))(t), "st", 16)
            wait("pe", "dve", tw_done)
            for sb in range(4):
                emiti("pe", (lambda sb: lambda e: e.transpose(
                    ps_s1[0:8, sb * P:(sb + 1) * P], t_twm[:, sb * 8:(sb + 1) * 8],
                    t_id[:, :]))(sb), "pe")
            wait("dve", "pe", cnt["pe"])
            emiti("dve", lambda e: e.tensor_copy(t_twT[:, :], ps_s1[0:8, :]), "dve")
            wait("dve", "dve", cnt["dve"])
            stat_free[0] = cnt["dve"]
            twT_done = cnt["dve"]

            etail = t_mtail if t == REAL0 else t_tailE
            if t == REAL0:
                emiti("dve", lambda e: e.tensor_scalar(
                    t_mtail[:, :], t_tailE[:, :], t_z01[:, 0:1], None, ALU.mult), "dve")
                wait("dve", "dve", cnt["dve"])
            for g in range(NG):
                emiti("dve", (lambda g, etail: lambda e: e.tensor_sub(
                    fw(t_A, g)[:, 0:1], fw(t_ei, g)[:, 0:1], etail[:, g:g + 1]))(g, etail), "dve")
                emiti("dve", (lambda g: lambda e: e.tensor_sub(
                    fw(t_A, g)[:, 1:TT], fw(t_ei, g)[:, 1:TT], fw(t_ei, g)[:, 0:TT - 1]))(g), "dve")
            d2_done = cnt["dve"]

            def ld_ewk0(exn):
                return [stream_load(ewkS[hh], wblock_dst(ewkS[hh], 4),
                                    d_ewk[exn, 0, :, hh * 4:(hh + 1) * 4, :]) for hh in range(2)]

            pre_ewk = ld_ewk0(0)
            pre_rkv = [None]
            xk_pre = [0]
            for ex in range(E):
                tb = ps_s2 if ex % 2 == 0 else ps_s3
                tb_free = ps2_free if ex % 2 == 0 else ps3_free
                wait("pe", "dve", twT_done)
                wait("pe", "dve", tb_free[0])
                emiti("pe", (lambda ex, tb: lambda e: e.matmul(
                    tb[:, :], t_oh[:, ex * P:(ex + 1) * P], t_twT[:, :],
                    start=True, stop=True))(ex, tb), "pe")
                if ex == 0:
                    wait("dve", "dve", d2_done)
                    wait("dve", "pe", bfree_pe[0])
                    for g in range(NG):
                        emiti("dve", (lambda ex, g: lambda e: e.scalar_tensor_tensor(
                            fw(t_B, g), fw(t_A, g), t_nk[:, g * 8 + ex:g * 8 + ex + 1],
                            fw(t_ei, g), ALU.mult, ALU.add))(ex, g), "dve")
                    xk_pl = cnt["dve"]
                else:
                    xk_pl = xk_pre[0]
                wait("pe", "dve", xk_pl)
                hsq_guard = [0, 0]
                for og in range(NGH):
                    bi = next_bank()
                    for hh in range(2):
                        buf = ewkS[hh]
                        if og == 0:
                            ldh = pre_ewk[hh]
                        else:
                            ldh = stream_load(buf, wblock_dst(buf, 4),
                                              d_ewk[ex, og, :, hh * 4:(hh + 1) * 4, :])
                        wait("pe", ldh[0], ldh[1])
                        for kc in range(4):
                            emiti("pe", (lambda buf, kc, bi, hh: lambda e: e.matmul(
                                ps_f[bi][:, :], wchunk(buf, kc), fw(t_B, hh * 4 + kc),
                                start=(hh == 0 and kc == 0), stop=(hh == 1 and kc == 3)))(buf, kc, bi, hh), "pe")
                        consumed(buf, "pe")
                    if og == NGH - 4:
                        pre_rkv[0] = (
                            stream_load(ewrS[0], wblock_dst(ewrS[0], NG),
                                        d_ewr[ex, 0, :, :, :]),
                            [stream_load(ewvS[hh], wblock_dst(ewvS[hh], NG),
                                         d_ewv[ex, 0, :, hh * NG:(hh + 1) * NG, :]) for hh in range(2)])
                    rlb = s_BsN if og % 2 == 0 else s_BsD
                    wait("act", "pe", cnt["pe"])
                    wait("act", "dve", hsq_guard[og % 2])
                    emiti("act", (lambda bi, rlb: lambda e: e.activation(
                        rlb[:, :], ps_f[bi][:, :], AF.Relu))(bi, rlb), "act")
                    bank_consumed(bi, "act")
                    wait("dve", "act", cnt["act"])
                    emiti("dve", (lambda og, rlb: lambda e: e.tensor_mul(
                        fw(t_h, og), rlb[:, :], rlb[:, :]))(og, rlb), "dve")
                    hsq_guard[og % 2] = cnt["dve"]
                h_pe = cnt["pe"]
                h_done = cnt["dve"]
                wait("dve", "pe", h_pe)
                for g in range(NG):
                    emiti("dve", (lambda ex, g: lambda e: e.scalar_tensor_tensor(
                        fw(t_B, g), fw(t_A, g), t_nr[:, g * 8 + ex:g * 8 + ex + 1],
                        fw(t_ei, g), ALU.mult, ALU.add))(ex, g), "dve")
                xr_pl = cnt["dve"]
                rsig_guard = [h_done, h_done]
                for og in range(NG):
                    bir = next_bank()
                    buf = ewrS[og % 2]
                    if og == 0:
                        ldr = pre_rkv[0][0]
                    else:
                        ldr = stream_load(buf, wblock_dst(buf, NG),
                                          d_ewr[ex, og, :, :, :])
                    wait("pe", ldr[0], ldr[1])
                    if og == 0:
                        wait("pe", "dve", xr_pl)
                    for kc in range(NG):
                        emiti("pe", (lambda buf, kc, bir: lambda e: e.matmul(
                            ps_f[bir][:, :], wchunk(buf, kc), fw(t_B, kc),
                            start=(kc == 0), stop=(kc == NG - 1)))(buf, kc, bir), "pe")
                    consumed(buf, "pe")
                    r_mm_pe = cnt["pe"]
                    if og == NG - 1 and ex < E - 1:
                        wait("dve", "pe", r_mm_pe)
                        for g in range(NG):
                            emiti("dve", (lambda exn, g: lambda e: e.scalar_tensor_tensor(
                                fw(t_B, g), fw(t_A, g), t_nk[:, g * 8 + exn:g * 8 + exn + 1],
                                fw(t_ei, g), ALU.mult, ALU.add))(ex + 1, g), "dve")
                        xk_pre[0] = cnt["dve"]
                    rsb = s_BsN if og % 2 == 0 else s_BsD
                    wait("act", "pe", r_mm_pe)
                    wait("act", "dve", rsig_guard[og % 2])
                    emiti("act", (lambda bir, rsb: lambda e: e.activation(
                        rsb[:, :], ps_f[bir][:, :], AF.Sigmoid))(bir, rsb), "act")
                    bank_consumed(bir, "act")
                    rsig_act = cnt["act"]
                    bikv = next_bank()
                    for hh in range(2):
                        buf = ewvS[hh]
                        if og == 0:
                            ldv2 = pre_rkv[0][1][hh]
                        else:
                            ldv2 = stream_load(buf, wblock_dst(buf, NG),
                                               d_ewv[ex, og, :, hh * NG:(hh + 1) * NG, :])
                        wait("pe", ldv2[0], ldv2[1])
                        if og == 0 and hh == 0:
                            wait("pe", "dve", h_done)
                        for kc in range(NG):
                            emiti("pe", (lambda buf, kc, bikv, hh: lambda e: e.matmul(
                                ps_f[bikv][:, :], wchunk(buf, kc), fw(t_h, hh * NG + kc),
                                start=(hh == 0 and kc == 0), stop=(hh == 1 and kc == NG - 1)))(buf, kc, bikv, hh), "pe")
                        consumed(buf, "pe")
                    if og == 0 and ex < E - 1:
                        pre_ewk = ld_ewk0(ex + 1)
                    wait("dve", "pe", cnt["pe"])
                    wait("dve", "act", rsig_act)
                    emiti("dve", (lambda bikv, rsb: lambda e: e.tensor_mul(
                        s_t1[:, :], ps_f[bikv][:, :], rsb[:, :]))(bikv, rsb), "dve")
                    bank_consumed(bikv, "dve")
                    emiti("dve", (lambda tb: lambda e: e.tensor_mul(
                        s_t2[:, :], s_t1[:, :], tb[:, :]))(tb), "dve")
                    asrc, adst = (t_acc, t_acc2) if ex % 2 == 0 else (t_acc2, t_acc)
                    emiti("dve", (lambda og, asrc, adst: lambda e: e.tensor_add(
                        fw(adst, og), fw(asrc, og), s_t2[:, :]))(og, asrc, adst), "dve")
                    rsig_guard[og % 2] = cnt["dve"]
                bfree_pe[0] = cnt["pe"]
                tb_free[0] = cnt["dve"]
            moe_done = cnt["dve"]

            wait("pl", "dve", moe_done)
            emiti("pl", (lambda t: lambda e: e.dma_start(
                d_out.rearrange("(g p) n -> p g n", p=P)[:, :, (t - REAL0) * TT:(t - REAL0 + 1) * TT],
                fw3(t_acc)))(t), "st", 16)
            store_done[0] = cnt["st"]

        if is_att and t + 1 < NT:
            emiti("dve", lambda e: e.tensor_copy(
                t_tailE[:, :], fw3(t_ei)[:, :, TT - 1:TT]), "dve")
        tile_end_dve[0] = cnt["dve"]

    for t in range(NT):
        emit_tile(t)

    wait("pl", "st", cnt["st"])
    wait("pl", "x", cnt["x"])
    wait("sp", "st", cnt["st"])

    import contextlib
    with contextlib.ExitStack() as _st:
        sem_h = {}
        for ch in cnt:
            sem_h[ch] = _st.enter_context(nc.semaphore(f"s_{ch}"))
        block = _st.enter_context(nc.Block())

        def replay(eng, items):
            for it in items:
                if it[0] == "wait":
                    eng.wait_ge(sem_h[it[1]], it[2])
                else:
                    _, fn, ch, amt = it
                    inst = fn(eng)
                    if ch is not None:
                        inst.then_inc(sem_h[ch], amt)

        @block.tensor
        def _(e):
            replay(e, prog["pe"])

        @block.vector
        def _(e):
            replay(e, prog["dve"])

        @block.scalar
        def _(e):
            replay(e, prog["act"])

        @block.sync
        def _(e):
            replay(e, prog["sp"])

        @block.gpsimd
        def _(e):
            replay(e, prog["pl"])

    return nc


def _pack_pvec(inp):
    pv = np.zeros((P, PCOLS), np.float32)

    def put(name, vec):
        v = np.asarray(vec, np.float32).reshape(NG, P)
        pv[:, PV[name]:PV[name] + NG] = v.T

    put("ln1_g", inp["L0_attn_ln1_g"]); put("ln1_b", inp["L0_attn_ln1_b"])
    put("ln2_g", inp["L0_attn_ln2_g"]); put("ln2_b", inp["L0_attn_ln2_b"])
    put("mu_k", inp["L0_attn_tm_mu_k"]); put("mu_v", inp["L0_attn_tm_mu_v"])
    put("mu_r", inp["L0_attn_tm_mu_r"])
    put("td", inp["L0_attn_time_decay"]); put("tf", inp["L0_attn_time_first"])
    put("ffn_bias", inp["L0_ffn_bias"])
    cmk = np.asarray(inp["L0_attn_cm_mu_k"], np.float32)
    cmr = np.asarray(inp["L0_attn_cm_mu_r"], np.float32)
    for g in range(NG):
        for e in range(E):
            pv[:, PV["cm_k"] + g * 8 + e] = cmk[e, g * P:(g + 1) * P]
            pv[:, PV["cm_r"] + g * 8 + e] = cmr[e, g * P:(g + 1) * P]
    return pv


def kernel(**inputs):
    x = np.asarray(inputs["L0_attn_x"], np.float32)
    B, S, _ = x.shape
    pvec = _pack_pvec(inputs)
    ident = np.eye(P, dtype=np.float32)
    onehot = np.zeros((E, D), np.float32)
    for e in range(E):
        onehot[e, e * P:(e + 1) * P] = 1.0

    def swz(w):
        w = np.asarray(w, np.float32)
        K, M = w.shape
        return np.ascontiguousarray(
            w.reshape(K // P, P, M // P, P).transpose(2, 1, 0, 3))

    def swz_e(w):
        w = np.asarray(w, np.float32)
        _, K, M = w.shape
        return np.ascontiguousarray(
            w.reshape(E, K // P, P, M // P, P).transpose(0, 3, 2, 1, 4))

    shared = dict(
        Wk=swz(inputs["L0_attn_Wk"]),
        Wv=swz(inputs["L0_attn_Wv"]),
        Wr=swz(inputs["L0_attn_Wr"]),
        Wo=swz(inputs["L0_attn_Wo"]),
        router_w_s=np.ascontiguousarray(
            np.asarray(inputs["L0_attn_router_w"], np.float32)
            .reshape(NG, P, E).transpose(1, 0, 2).reshape(P, NG * E)),
        eWk=swz_e(inputs["L0_attn_eWk"]),
        eWv=swz_e(inputs["L0_attn_eWv"]),
        eWr=swz_e(inputs["L0_attn_eWr"]),
        ffk=np.ascontiguousarray(inputs["L0_ffn_kernel"], np.float32),
        ffa_s=np.ascontiguousarray(
            np.asarray(inputs["L0_ffn_a"], np.float32)
            .reshape(NG, P, 16).transpose(1, 0, 2).reshape(P, NG * 16)),
        ffb=np.ascontiguousarray(inputs["L0_ffn_b"], np.float32),
        pvec=pvec, cident=ident,
        conescol=np.ones((P, 1), np.float32),
        conesrow=np.ones((1, P), np.float32),
        cinv1024=np.full((1, P), 1.0 / D, np.float32),
        conehot=onehot,
        czcol=np.zeros((P, 1), np.float32),
        ceps=np.full((P, 1), EPS, np.float32),
        czc16=np.zeros((P, 16), np.float32),
    )

    in_maps = []
    for c in range(8):
        b, q = c // 4, c % 4
        s0 = q * LC
        lo = s0 - HALO
        if lo < 0:
            xs = np.concatenate(
                [np.zeros((D, -lo), np.float32),
                 np.ascontiguousarray(x[b, 0:s0 + LC].T)], axis=1)
        else:
            xs = np.ascontiguousarray(x[b, lo:s0 + LC].T)
        first = 1.0 if q > 0 else 0.0
        m = dict(shared)
        m["xT"] = np.ascontiguousarray(xs)
        m["cme"] = np.full((P, 1), first, np.float32)
        m["cz01"] = np.full((P, 1), first, np.float32)
        in_maps.append(m)

    nc = build()
    res = run_bass_kernel_spmd(nc, in_maps, core_ids=list(range(8)))

    out = np.empty((B, S, D), np.float32)
    for c in range(8):
        b, q = c // 4, c % 4
        out[b, q * LC:(q + 1) * LC] = res.results[c]["outT"].T
    if DEBUG_TAPS:
        kernel.last_results = res
    return out


# version 24
# speedup vs baseline: 1.0869x; 1.0046x over previous
import numpy as np
import concourse.bass as bass
import concourse.mybir as mybir
from concourse.bass_utils import run_bass_kernel_spmd

F32 = mybir.dt.float32
I32 = mybir.dt.int32
AF = mybir.ActivationFunctionType
ALU = mybir.AluOpType
AX = mybir.AxisListType

P = 128
TT = 512
NT = 9
NG = 8
D = 1024
E = 8
FH = 2048
NGH = 16
HALO = 512
LC = 4096
TFULL = HALO + LC
EPS = 1e-5
REAL0 = 1
ATT0 = 0

DEBUG_TAPS = False

PV = {}
_pv_names = ["ln1_g", "ln1_b", "ln2_g", "ln2_b", "mu_k", "mu_v", "mu_r",
             "td", "tf", "ffn_bias"]
for _i, _n in enumerate(_pv_names):
    PV[_n] = _i * NG
PV["cm_k"] = 10 * NG
PV["cm_r"] = 10 * NG + 64
PCOLS = 10 * NG + 128


def build():
    nc = bass.Bass(target_bir_lowering=False)

    d_xT = nc.dram_tensor("xT", [D, TFULL], F32, kind="ExternalInput").ap()
    d_wk = nc.dram_tensor("Wk", [NG, P, NG, P], F32, kind="ExternalInput").ap()
    d_wv = nc.dram_tensor("Wv", [NG, P, NG, P], F32, kind="ExternalInput").ap()
    d_wr = nc.dram_tensor("Wr", [NG, P, NG, P], F32, kind="ExternalInput").ap()
    d_wo = nc.dram_tensor("Wo", [NG, P, NG, P], F32, kind="ExternalInput").ap()
    d_rw = nc.dram_tensor("router_w_s", [P, NG * E], F32, kind="ExternalInput").ap()
    d_ewk = nc.dram_tensor("eWk", [E, 2 * NG, P, NG, P], F32, kind="ExternalInput").ap()
    d_ewv = nc.dram_tensor("eWv", [E, NG, P, 2 * NG, P], F32, kind="ExternalInput").ap()
    d_ewr = nc.dram_tensor("eWr", [E, NG, P, NG, P], F32, kind="ExternalInput").ap()
    d_ffk = nc.dram_tensor("ffk", [D, D], F32, kind="ExternalInput").ap()
    d_ffa = nc.dram_tensor("ffa_s", [P, NG * 16], F32, kind="ExternalInput").ap()
    d_ffb = nc.dram_tensor("ffb", [16, D], F32, kind="ExternalInput").ap()
    d_pvec = nc.dram_tensor("pvec", [P, PCOLS], F32, kind="ExternalInput").ap()
    d_ident = nc.dram_tensor("cident", [P, P], F32, kind="ExternalInput").ap()
    d_onescol = nc.dram_tensor("conescol", [P, 1], F32, kind="ExternalInput").ap()
    d_onesrow = nc.dram_tensor("conesrow", [1, P], F32, kind="ExternalInput").ap()
    d_inv1024 = nc.dram_tensor("cinv1024", [1, P], F32, kind="ExternalInput").ap()
    d_onehot = nc.dram_tensor("conehot", [E, D], F32, kind="ExternalInput").ap()
    d_zcol = nc.dram_tensor("czcol", [P, 1], F32, kind="ExternalInput").ap()
    d_zc16 = nc.dram_tensor("czc16", [P, 16], F32, kind="ExternalInput").ap()
    d_me = nc.dram_tensor("cme", [P, 1], F32, kind="ExternalInput").ap()
    d_eps = nc.dram_tensor("ceps", [P, 1], F32, kind="ExternalInput").ap()
    d_z01 = nc.dram_tensor("cz01", [P, 1], F32, kind="ExternalInput").ap()
    d_out = nc.dram_tensor("outT", [D, LC], F32, kind="ExternalOutput").ap()
    d_wq = nc.dram_tensor("wq_scratch", [P, NG, NG, P], F32, kind="Internal").ap()

    taps = {}
    if DEBUG_TAPS:
        for nm, shp in [("tap_wkv", [D, TFULL]), ("tap_x2", [D, TFULL]),
                        ("tap_ei", [D, TFULL]), ("tap_xn", [D, TFULL]),
                        ("tap_tw", [P, 32 * NT]), ("tap_acc2", [D, TFULL])]:
            taps[nm] = nc.dram_tensor(nm, shp, F32, kind="ExternalOutput").ap()

    t_xn = nc.alloc_sbuf_tensor("t_xn", [P, NG * TT], F32)
    t_xn1 = nc.alloc_sbuf_tensor("t_xn1", [P, NG * TT], F32)
    t_acc = nc.alloc_sbuf_tensor("t_acc", [P, NG * TT], F32)
    t_ei = nc.alloc_sbuf_tensor("t_ei", [P, NG * TT], F32)
    t_A = nc.alloc_sbuf_tensor("t_A", [P, NG * TT], F32)
    t_B = nc.alloc_sbuf_tensor("t_B", [P, NG * TT], F32)
    t_h = nc.alloc_sbuf_tensor("t_h", [P, NGH * TT], F32)

    s_EK = nc.alloc_sbuf_tensor("s_EK", [P, TT], F32)
    s_EV = nc.alloc_sbuf_tensor("s_EV", [P, TT], F32)
    s_BsN = nc.alloc_sbuf_tensor("s_BsN", [P, TT], F32)
    s_BsD = nc.alloc_sbuf_tensor("s_BsD", [P, TT], F32)
    s_t1 = nc.alloc_sbuf_tensor("s_t1", [P, TT], F32)
    s_pl = nc.alloc_sbuf_tensor("s_pl", [P, TT], F32)
    s_pl2 = nc.alloc_sbuf_tensor("s_pl2", [P, TT], F32)
    s_t2 = nc.alloc_sbuf_tensor("s_t2", [P, TT], F32)
    b_mean = nc.alloc_sbuf_tensor("b_mean", [P, TT], F32)
    b_var = nc.alloc_sbuf_tensor("b_var", [P, TT], F32)
    b_rstd = nc.alloc_sbuf_tensor("b_rstd", [P, TT], F32)
    b_rstd1 = nc.alloc_sbuf_tensor("b_rstd1", [P, TT], F32)

    t_acc2 = nc.alloc_sbuf_tensor("t_acc2", [P, NG * TT], F32)
    wS = [nc.alloc_sbuf_tensor(f"wS{i}", [P, 4 * P], F32) for i in range(2)]
    ewkS = [nc.alloc_sbuf_tensor(f"ewkS{i}", [P, 4 * P], F32) for i in range(2)]
    ewvS = [nc.alloc_sbuf_tensor(f"ewvS{i}", [P, NG * P], F32) for i in range(2)]
    ewrS = [nc.alloc_sbuf_tensor(f"ewrS{i}", [P, NG * P], F32) for i in range(2)]

    t_la = b_rstd1[0:16, :]
    t_sums = b_mean[0:1, :]
    t_sumsq = b_var[0:1, :]
    t_pv = nc.alloc_sbuf_tensor("t_pv", [P, PCOLS], F32)
    t_gd = nc.alloc_sbuf_tensor("t_gd", [P, NG], F32)
    t_eu = nc.alloc_sbuf_tensor("t_eu", [P, NG], F32)
    t_nk = nc.alloc_sbuf_tensor("t_nk", [P, 64], F32)
    t_nr = nc.alloc_sbuf_tensor("t_nr", [P, 64], F32)
    t_e1 = nc.alloc_sbuf_tensor("t_e1", [P, NG], F32)
    t_cN = nc.alloc_sbuf_tensor("t_cN", [P, NG], F32)
    t_cD = nc.alloc_sbuf_tensor("t_cD", [P, NG], F32)
    t_tailA = nc.alloc_sbuf_tensor("t_tailA", [P, NG], F32)
    t_tailE = nc.alloc_sbuf_tensor("t_tailE", [P, NG], F32)
    t_mtail = nc.alloc_sbuf_tensor("t_mtail", [P, NG], F32)
    t_id = nc.alloc_sbuf_tensor("t_id", [P, P], F32)
    t_oc = nc.alloc_sbuf_tensor("t_oc", [P, 1], F32)
    t_or = nc.alloc_sbuf_tensor("t_or", [1, P], F32)
    t_iv = nc.alloc_sbuf_tensor("t_iv", [1, P], F32)
    t_oh = nc.alloc_sbuf_tensor("t_oh", [E, D], F32)
    t_zc = nc.alloc_sbuf_tensor("t_zc", [P, 1], F32)
    t_me = nc.alloc_sbuf_tensor("t_me", [P, 1], F32)
    t_eps = nc.alloc_sbuf_tensor("t_eps", [P, 1], F32)
    t_z01 = nc.alloc_sbuf_tensor("t_z01", [P, 1], F32)
    t_rws = nc.alloc_sbuf_tensor("t_rws", [P, NG * E], F32)
    t_as = nc.alloc_sbuf_tensor("t_as", [P, NG * 16], F32)
    t_bs = nc.alloc_sbuf_tensor("t_bs", [16, D], F32)
    t_lg = nc.alloc_sbuf_tensor("t_lg", [P, 32], F32)
    t_m1 = nc.alloc_sbuf_tensor("t_m1", [P, 4], F32)
    t_m2 = nc.alloc_sbuf_tensor("t_m2", [P, 4], F32)
    t_d12 = nc.alloc_sbuf_tensor("t_d12", [P, 4], F32)
    t_w1 = nc.alloc_sbuf_tensor("t_w1", [P, 4], F32)
    t_mk1 = nc.alloc_sbuf_tensor("t_mk1", [P, 32], F32)
    t_mk2 = nc.alloc_sbuf_tensor("t_mk2", [P, 32], F32)
    t_twm = nc.alloc_sbuf_tensor("t_twm", [P, 32], F32)
    t_twT = nc.alloc_sbuf_tensor("t_twT", [E, TT], F32)
    t_mx8 = nc.alloc_sbuf_tensor("t_mx8", [P, NG], F32)
    t_mx1 = nc.alloc_sbuf_tensor("t_mx1", [P, 1], F32)
    t_mrow = nc.alloc_sbuf_tensor("t_mrow", [1, P], F32)
    t_M = nc.alloc_sbuf_tensor("t_M", [1, 1], F32)
    t_scB = nc.alloc_sbuf_tensor("t_scB", [P, 1], F32)
    t_isB = nc.alloc_sbuf_tensor("t_isB", [P, 1], F32)

    PSN = 5
    ps_f = [nc.alloc_psum_tensor(f"ps_f{i}", [P, TT], F32) for i in range(PSN)]
    ps_s1 = nc.alloc_psum_tensor("ps_s1", [P, TT], F32)
    ps_s2 = nc.alloc_psum_tensor("ps_s2", [P, TT], F32)
    ps_s3 = nc.alloc_psum_tensor("ps_s3", [P, TT], F32)

    prog = {k: [] for k in ["pe", "dve", "act", "sp", "pl"]}
    WCH = ["w0", "w1", "w0b", "w1b", "ek0", "ek1", "ev0", "ev1", "er0", "er1"]
    cnt = {"pe": 0, "dve": 0, "act": 0, "ld": 0, "q": 0, "x": 0, "st": 0, "plc": 0}
    for _c in WCH:
        cnt[_c] = 0

    def emit(q, fn):
        prog[q].append(("op", fn, None, 0))

    def emiti(q, fn, ch, amt=1):
        if (q in ("dve", "act") and ch == q) or (q == "pl" and ch == "plc"):
            prog[q].append(("wait", ch, cnt[ch], 0))
        cnt[ch] += amt
        prog[q].append(("op", fn, ch, amt))

    def wait(q, ch, val):
        if val > 0:
            prog[q].append(("wait", ch, val, 0))

    bank_free = [("dve", 0)] * PSN
    bank_i = [0]

    def next_bank():
        i = bank_i[0] % PSN
        bank_i[0] += 1
        ch, v = bank_free[i]
        wait("pe", ch, v)
        return i

    def bank_consumed(i, ch):
        bank_free[i] = (ch, cnt[ch])

    def dma(dst, src):
        return lambda e: e.dma_start(dst, src)

    def fw(tens, g):
        return tens[:, g * TT:(g + 1) * TT]

    def fw3(tens, ngroups=NG):
        return tens.ap().rearrange("p (g n) -> p g n", n=TT)[:, 0:ngroups, :]

    def pcol(name, g):
        c = PV[name] + g
        return t_pv[:, c:c + 1]

    def wchunk(buf, kc):
        return buf[:, kc * P:(kc + 1) * P]

    def mblock_src(w_ap, og, kc0=0, nkc=None):
        if nkc is None:
            sl = w_ap[:, og * P:(og + 1) * P]
        else:
            sl = w_ap[kc0 * P:(kc0 + nkc) * P, og * P:(og + 1) * P]
        return sl.rearrange("(c p) m -> p c m", p=P)

    def wblock_dst(buf, nkc):
        return buf.ap().rearrange("p (c m) -> p c m", m=P)[:, 0:nkc, :]

    for dst, src in [(t_pv[:, :], d_pvec), (t_id[:, :], d_ident),
                     (t_oc[:, :], d_onescol), (t_or[:, :], d_onesrow),
                     (t_iv[:, :], d_inv1024), (t_oh[:, :], d_onehot),
                     (t_zc[:, :], d_zcol), (t_eps[:, :], d_eps),
                     (t_cN[:, :], d_zc16[:, 0:8]),
                     (t_cD[:, :], d_zc16[:, 8:16]), (t_me[:, :], d_me),
                     (t_z01[:, :], d_z01),
                     (t_rws[:, :], d_rw),
                     (t_as[:, :], d_ffa),
                     (t_bs[:, :], d_ffb)]:
        emiti("sp", dma(dst, src), "ld", 16)
    SETUP_LD = cnt["ld"]

    wait("act", "ld", SETUP_LD)
    emiti("act", lambda e: e.activation(t_e1[:, :], t_pv[:, PV["td"]:PV["td"] + NG], AF.Exp), "act")
    wait("act", "act", cnt["act"])
    emiti("act", lambda e: e.activation(t_gd[:, :], t_e1[:, :], AF.Exp, scale=-1.0), "act")
    emiti("act", lambda e: e.activation(t_eu[:, :], t_pv[:, PV["tf"]:PV["tf"] + NG], AF.Exp), "act")
    SETUP_ACT = cnt["act"]
    wait("dve", "ld", SETUP_LD)
    emiti("dve", lambda e: e.tensor_scalar(t_nk[:, :], t_pv[:, PV["cm_k"]:PV["cm_k"] + 64], 1.0, None, ALU.subtract), "dve")
    emiti("dve", lambda e: e.tensor_scalar(t_nr[:, :], t_pv[:, PV["cm_r"]:PV["cm_r"] + 64], 1.0, None, ALU.subtract), "dve")
    wait("dve", "dve", cnt["dve"])

    QLOAD = ewrS[1][:, 0:D]
    for kc in range(NG):
        emiti("sp", dma(QLOAD, d_ffk[kc * P:(kc + 1) * P, :]), "q", 16)
        wait("dve", "q", cnt["q"])
        emiti("dve", (lambda kc: lambda e: e.reduce_max(
            t_mx8[:, kc:kc + 1], QLOAD, axis=AX.X, apply_absolute_value=True))(kc), "dve")
        wait("sp", "dve", cnt["dve"])
    emiti("dve", lambda e: e.reduce_max(t_mx1[:, :], t_mx8[:, :], axis=AX.X), "dve")
    wait("pe", "ld", SETUP_LD)
    wait("pe", "dve", cnt["dve"])
    emiti("pe", lambda e: e.matmul(ps_s1[0:1, 0:P], t_mx1[:, :], t_id[:, :], start=True, stop=True), "pe")
    wait("dve", "pe", cnt["pe"])
    emiti("dve", lambda e: e.tensor_copy(t_mrow[:, :], ps_s1[0:1, 0:P]), "dve")
    wait("dve", "dve", cnt["dve"])
    emiti("dve", lambda e: e.reduce_max(t_M[:, :], t_mrow[:, :], axis=AX.X), "dve")
    wait("pe", "dve", cnt["dve"])
    emiti("pe", lambda e: e.matmul(ps_s3[0:P, 0:1], t_or[:, :], t_M[:, :], start=True, stop=True), "pe")
    wait("dve", "pe", cnt["pe"])
    emiti("dve", lambda e: e.tensor_scalar(t_scB[:, :], ps_s3[0:P, 0:1], 1.0 / 7.0, None, ALU.mult), "dve")
    wait("dve", "dve", cnt["dve"])
    emiti("dve", lambda e: e.reciprocal(t_isB[:, :], t_scB[:, :]), "dve")
    wait("pl", "dve", cnt["dve"])
    QU = ewvS[0][:, 0:D]
    QP = ewvS[1][:, 0:D]
    QQ = ewrS[0][:, 0:D]
    for kc in range(NG):
        wait("sp", "q", cnt["q"])
        emiti("sp", dma(QLOAD, d_ffk[kc * P:(kc + 1) * P, :]), "q", 16)
        wait("dve", "q", cnt["q"])
        emiti("dve", lambda e: e.tensor_scalar(QU, QLOAD, t_isB[:, 0:1], None, ALU.mult), "dve")
        emiti("dve", lambda e: e.tensor_scalar(QP, QU, -7.5, None, ALU.is_ge), "dve")
        cur, nxt = QP, QQ
        for j in range(-7, 7):
            thr = j + 0.5
            emiti("dve", (lambda thr, cur, nxt: lambda e: e.scalar_tensor_tensor(
                nxt, QU, thr, cur, ALU.is_ge, ALU.add))(thr, cur, nxt), "dve")
            cur, nxt = nxt, cur
        emiti("dve", (lambda cur, nxt: lambda e: e.tensor_scalar(
            nxt, cur, 8.0, None, ALU.subtract))(cur, nxt), "dve")
        q8 = nxt
        emiti("dve", (lambda q8: lambda e: e.tensor_scalar(
            QU, q8, t_scB[:, 0:1], None, ALU.mult))(q8), "dve")
        wait("sp", "dve", cnt["dve"])
        emiti("sp", (lambda kc: dma(d_wq[:, :, kc, :],
                                    QU.rearrange("p (o m) -> p o m", m=P)))(kc), "q", 16)
        wait("dve", "q", cnt["q"])
    WQ_LD = cnt["q"]

    wchan = {id(wS[0]): "w0", id(wS[1]): "w1",
             id(ewkS[0]): "ek0", id(ewkS[1]): "ek1",
             id(ewvS[0]): "ev0", id(ewvS[1]): "ev1",
             id(ewrS[0]): "er0", id(ewrS[1]): "er1"}
    wfree = {id(wS[0]): ("pe", 0), id(wS[1]): ("pe", 0),
             id(ewkS[0]): ("pe", 0), id(ewkS[1]): ("pe", 0),
             id(ewvS[0]): ("q", WQ_LD), id(ewvS[1]): ("q", WQ_LD),
             id(ewrS[0]): ("q", WQ_LD), id(ewrS[1]): ("q", WQ_LD)}

    def stream_load(buf, dst3, src_ap):
        ch, v = wfree[id(buf)]
        wait("sp", ch, v)
        wc = wchan[id(buf)]
        emiti("sp", dma(dst3, src_ap), wc, 16)
        return (wc, cnt[wc])

    wchanB = {id(wS[0]): "w0b", id(wS[1]): "w1b"}

    def stream_load2(buf, srcfn, kc0):
        ch, v = wfree[id(buf)]
        wait("sp", ch, v)
        wc, wcb = wchan[id(buf)], wchanB[id(buf)]
        d3 = buf.ap().rearrange("p (c m) -> p c m", m=P)
        emiti("sp", dma(d3[:, 0:2, :], srcfn(kc0, 2)), wc, 16)
        emiti("sp", dma(d3[:, 2:4, :], srcfn(kc0 + 2, 2)), wcb, 16)
        return [(wc, cnt[wc]), (wcb, cnt[wcb])]

    def consumed(buf, ch):
        wfree[id(buf)] = (ch, cnt[ch])

    def half_proj(w_dram, og, rhs_fw, bi, first_extra_wait=None):
        for h in range(2):
            buf = wS[wslot[0] % 2]; wslot[0] += 1
            for tk in stream_load2(
                    buf, lambda k0, n: w_dram[og, :, k0:k0 + n, :], h * 4):
                wait("pe", tk[0], tk[1])
            if h == 0 and first_extra_wait is not None:
                wait("pe", "dve", first_extra_wait)
            for kc in range(4):
                emiti("pe", (lambda buf, kc, bi, h: lambda e: e.matmul(
                    ps_f[bi][:, :], wchunk(buf, kc), rhs_fw(h * 4 + kc),
                    start=(h == 0 and kc == 0), stop=(h == 1 and kc == 3)))(buf, kc, bi, h), "pe")
            consumed(buf, "pe")

    wait("dve", "act", SETUP_ACT)
    wait("act", "ld", SETUP_LD)
    wait("sp", "q", WQ_LD)

    xload_ld = [0]
    store_done = [0]
    stat_free = [0]
    ps2_free = [0]
    ps3_free = [0]
    bfree_pe = [0]
    pe_sq = [0] * NG
    tile_end_dve = [0]
    wslot = [0]
    ei_done_s = [0]

    x0src = d_xT.rearrange("(g p) n -> p g n", p=P)
    for en, g0, g1 in [("pl", 0, 3), ("sp", 3, 6), ("act", 6, 8)]:
        emiti(en, (lambda g0, g1: dma(fw3(t_xn1)[:, g0:g1, :],
                                      x0src[:, g0:g1, 0:TT]))(g0, g1), "x", 16)
    xload_ld[0] = cnt["x"]

    def ln_stats(src_fw, prev_dve_guard):
        wait("pe", "dve", stat_free[0])
        wait("pe", "dve", prev_dve_guard)
        for g in range(NG):
            emiti("pe", (lambda g: lambda e: e.matmul(
                ps_s1[0:1, :], t_oc[:, :], src_fw(g), start=(g == 0), stop=(g == NG - 1)))(g), "pe")
        wait("act", "dve", prev_dve_guard)
        for g in range(NG):
            sq = s_t1 if g % 2 == 0 else s_t2
            if g >= 2:
                wait("act", "pe", pe_sq[g - 2])
            emiti("act", (lambda g, sq: lambda e: e.activation(sq[:, :], src_fw(g), AF.Square))(g, sq), "act")
            wait("pe", "act", cnt["act"])
            emiti("pe", (lambda g, sq: lambda e: e.matmul(
                ps_s2[0:1, :], t_oc[:, :], sq[:, :], start=(g == 0), stop=(g == NG - 1)))(g, sq), "pe")
            pe_sq[g] = cnt["pe"]
        wait("dve", "pe", cnt["pe"])
        emiti("dve", lambda e: e.tensor_copy(t_sums, ps_s1[0:1, :]), "dve")
        emiti("dve", lambda e: e.tensor_copy(t_sumsq, ps_s2[0:1, :]), "dve")
        wait("dve", "dve", cnt["dve"])
        wait("pe", "dve", cnt["dve"])
        emiti("pe", lambda e: e.matmul(ps_s1[:, :], t_iv[:, :], t_sums, start=True, stop=True), "pe")
        emiti("pe", lambda e: e.matmul(ps_s2[:, :], t_iv[:, :], t_sumsq, start=True, stop=True), "pe")
        wait("dve", "pe", cnt["pe"])
        emiti("dve", lambda e: e.tensor_copy(b_mean[:, :], ps_s1[:, :]), "dve")
        emiti("dve", lambda e: e.tensor_copy(b_var[:, :], ps_s2[:, :]), "dve")
        stat_free[0] = cnt["dve"]
        emiti("dve", lambda e: e.tensor_mul(s_t1[:, :], b_mean[:, :], b_mean[:, :]), "dve")
        emiti("dve", lambda e: e.tensor_sub(s_t2[:, :], b_var[:, :], s_t1[:, :]), "dve")
        wait("act", "dve", cnt["dve"])
        emiti("act", lambda e: e.activation(s_t1[:, :], s_t2[:, :], AF.Sqrt, bias=t_eps[:, 0:1]), "act")
        wait("dve", "act", cnt["act"])
        emiti("dve", lambda e: e.reciprocal(b_rstd[:, :], s_t1[:, :]), "dve")

    def emit_tile(t):
        is_att = t >= ATT0
        is_real = t >= REAL0
        x_t = t_xn1
        xld = xload_ld[0]

        wait("pe", "x", xld)
        wait("act", "x", xld)
        ln_stats(lambda g: fw(x_t, g), tile_end_dve[0])
        emiti("dve", lambda e: e.tensor_mul(s_t1[:, :], b_rstd[:, :], b_rstd[:, :]), "dve")
        emiti("dve", lambda e: e.tensor_mul(b_rstd1[:, :], s_t1[:, :], s_t2[:, :]), "dve")
        wait("act", "dve", cnt["dve"])
        emiti("act", lambda e: e.activation(s_t2[:, :], b_rstd1[:, :], AF.Sqrt, bias=t_eps[:, 0:1]), "act")
        wait("dve", "act", cnt["act"])
        emiti("dve", lambda e: e.reciprocal(b_rstd1[:, :], s_t2[:, :]), "dve")
        for g in range(NG):
            emiti("dve", (lambda g: lambda e: e.tensor_sub(s_t1[:, :], fw(x_t, g), b_mean[:, :]))(g), "dve")
            emiti("dve", (lambda g: lambda e: e.tensor_mul(fw(t_xn, g), s_t1[:, :], b_rstd[:, :]))(g), "dve")
        if is_real:
            wait("dve", "st", store_done[0])
            for g in range(NG):
                emiti("dve", (lambda g: lambda e: e.scalar_tensor_tensor(
                    fw(t_acc, g), fw(x_t, g), pcol("ffn_bias", g), fw(t_xn, g),
                    ALU.add, ALU.add))(g), "dve")
        acc_init = cnt["dve"]
        for g in range(NG):
            emiti("dve", (lambda g: lambda e: e.tensor_mul(s_t1[:, :], fw(t_xn, g), b_rstd1[:, :]))(g), "dve")
            emiti("dve", (lambda g: lambda e: e.tensor_scalar(
                fw(t_xn1, g), s_t1[:, :], pcol("ln1_g", g), pcol("ln1_b", g),
                ALU.mult, ALU.add))(g), "dve")
        xn1_done = cnt["dve"]
        xn_done = cnt["dve"]
        if DEBUG_TAPS:
            wait("pl", "dve", xn_done)
            emiti("pl", (lambda t: lambda e: e.dma_start(
                taps["tap_xn"].rearrange("(g p) n -> p g n", p=P)[:, :, t * TT:(t + 1) * TT],
                fw3(t_xn)))(t), "st", 16)

        if is_real:
            wait("pe", "dve", xn_done)
            wait("pe", "dve", ps3_free[0])
            for g in range(NG):
                emiti("pe", (lambda g: lambda e: e.matmul(
                    ps_s3[0:16, :], t_as[:, g * 16:(g + 1) * 16], fw(t_xn, g),
                    start=(g == 0), stop=(g == NG - 1)))(g), "pe")
            wait("dve", "pe", cnt["pe"])
            emiti("dve", lambda e: e.tensor_copy(t_la, ps_s3[0:16, :]), "dve")
            wait("dve", "dve", cnt["dve"])
            ps3_free[0] = cnt["dve"]
            la_done = cnt["dve"]
            for og in range(NG):
                bi = next_bank()
                for h in range(2):
                    buf = wS[wslot[0] % 2]; wslot[0] += 1
                    for tk in stream_load2(
                            buf, (lambda og: lambda k0, n: d_wq[:, og, k0:k0 + n, :])(og), h * 4):
                        wait("pe", tk[0], tk[1])
                    if h == 0 and og == 0:
                        wait("pe", "dve", la_done)
                    for kc in range(4):
                        emiti("pe", (lambda buf, kc, bi, h: lambda e: e.matmul(
                            ps_f[bi][:, :], wchunk(buf, kc), fw(t_xn, h * 4 + kc),
                            start=(h == 0 and kc == 0), stop=False))(buf, kc, bi, h), "pe")
                    consumed(buf, "pe")
                emiti("pe", (lambda og, bi: lambda e: e.matmul(
                    ps_f[bi][:, :], t_bs[:, og * P:(og + 1) * P], t_la,
                    start=False, stop=True))(og, bi), "pe")
                wait("dve", "pe", cnt["pe"])
                wait("dve", "dve", acc_init)
                emiti("dve", (lambda og, bi: lambda e: e.tensor_add(
                    fw(t_acc2, og), fw(t_acc, og), ps_f[bi][:, :]))(og, bi), "dve")
                bank_consumed(bi, "dve")

        tail_src = t_mtail if t == REAL0 else t_tailA
        if t == REAL0:
            wait("dve", "dve", xn1_done)
            emiti("dve", lambda e: e.tensor_scalar(
                t_mtail[:, :], t_tailA[:, :], t_z01[:, 0:1], None, ALU.mult), "dve")
            wait("dve", "dve", cnt["dve"])
        wait("dve", "dve", xn1_done)
        for g in range(NG):
            xx0 = t_zc[:, 0:1] if t == 0 else tail_src[:, g:g + 1]
            emiti("dve", (lambda g, xx0: lambda e: e.tensor_sub(
                s_t1[:, 0:1], fw(t_xn1, g)[:, 0:1], xx0))(g, xx0), "dve")
            emiti("dve", (lambda g: lambda e: e.tensor_sub(
                s_t1[:, 1:TT], fw(t_xn1, g)[:, 1:TT], fw(t_xn1, g)[:, 0:TT - 1]))(g), "dve")
            for name, dst in (("mu_k", t_A), ("mu_v", t_B)):
                emiti("dve", (lambda g, name, dst, xx0: lambda e: e.scalar_tensor_tensor(
                    fw(dst, g)[:, 0:1], s_t1[:, 0:1], pcol(name, g), xx0,
                    ALU.mult, ALU.add))(g, name, dst, xx0), "dve")
                emiti("dve", (lambda g, name, dst: lambda e: e.scalar_tensor_tensor(
                    fw(dst, g)[:, 1:TT], s_t1[:, 1:TT], pcol(name, g),
                    fw(t_xn1, g)[:, 0:TT - 1], ALU.mult, ALU.add))(g, name, dst), "dve")
        mix_kv = cnt["dve"]

        ek_guard = [mix_kv]
        wait("pe", "dve", mix_kv)
        for og in range(NG):
            bik = next_bank()
            half_proj(d_wk, og, lambda kc: fw(t_A, kc), bik)
            kdone = cnt["pe"]
            biv = next_bank()
            half_proj(d_wv, og, lambda kc: fw(t_B, kc), biv)
            vdone = cnt["pe"]
            wait("act", "pe", kdone)
            wait("act", "dve", ek_guard[0])
            emiti("act", (lambda bik: lambda e: e.activation(s_EK[:, :], ps_f[bik][:, :], AF.Exp))(bik), "act")
            bank_consumed(bik, "act")
            wait("dve", "act", cnt["act"])
            wait("dve", "pe", vdone)
            emiti("dve", (lambda biv: lambda e: e.tensor_mul(s_EV[:, :], s_EK[:, :], ps_f[biv][:, :]))(biv), "dve")
            bank_consumed(biv, "dve")
            gbb = t_gd[:, og:og + 1].broadcast_to([P, TT])
            if t < REAL0:
                emiti("dve", lambda e: e.tensor_scalar(s_t2[:, :], s_EV[:, :], t_me[:, 0:1], None, ALU.mult), "dve")
                emiti("dve", lambda e: e.tensor_scalar(s_t1[:, :], s_EK[:, :], t_me[:, 0:1], None, ALU.mult), "dve")
                dsrc = s_t1
                nsrc = s_t2
            else:
                dsrc = s_EK
                nsrc = s_EV
            emiti("dve", (lambda og, gbb, nsrc: lambda e: e.tensor_tensor_scan(
                s_BsN[:, :], gbb, nsrc[:, :], t_cN[:, og:og + 1], ALU.mult, ALU.add))(og, gbb, nsrc), "dve")
            emiti("dve", (lambda og, gbb, dsrc: lambda e: e.tensor_tensor_scan(
                s_BsD[:, :], gbb, dsrc[:, :], t_cD[:, og:og + 1], ALU.mult, ALU.add))(og, gbb, dsrc), "dve")
            if is_att:
                emiti("dve", (lambda og: lambda e: e.scalar_tensor_tensor(
                    s_t1[:, 0:1], s_EV[:, 0:1], t_eu[:, og:og + 1], t_cN[:, og:og + 1],
                    ALU.mult, ALU.add))(og), "dve")
                emiti("dve", (lambda og: lambda e: e.scalar_tensor_tensor(
                    s_t1[:, 1:TT], s_EV[:, 1:TT], t_eu[:, og:og + 1], s_BsN[:, 0:TT - 1],
                    ALU.mult, ALU.add))(og), "dve")
                emiti("dve", (lambda og: lambda e: e.scalar_tensor_tensor(
                    s_t2[:, 0:1], s_EK[:, 0:1], t_eu[:, og:og + 1], t_cD[:, og:og + 1],
                    ALU.mult, ALU.add))(og), "dve")
                emiti("dve", (lambda og: lambda e: e.scalar_tensor_tensor(
                    s_t2[:, 1:TT], s_EK[:, 1:TT], t_eu[:, og:og + 1], s_BsD[:, 0:TT - 1],
                    ALU.mult, ALU.add))(og), "dve")
                emiti("dve", lambda e: e.reciprocal(s_EK[:, :], s_t2[:, :]), "dve")
                emiti("dve", (lambda og: lambda e: e.tensor_mul(
                    fw(t_h, og), s_t1[:, :], s_EK[:, :]))(og), "dve")
            ek_guard[0] = cnt["dve"]
            emiti("dve", (lambda og: lambda e: e.tensor_copy(
                t_cN[:, og:og + 1], s_BsN[:, TT - 1:TT]))(og), "dve")
            emiti("dve", (lambda og: lambda e: e.tensor_copy(
                t_cD[:, og:og + 1], s_BsD[:, TT - 1:TT]))(og), "dve")
            wait("dve", "dve", cnt["dve"])
        scan_done = cnt["dve"]

        kv_pe_end = cnt["pe"]
        if is_att:
            wait("dve", "pe", kv_pe_end)
            for g in range(NG):
                xx0 = t_zc[:, 0:1] if t == 0 else tail_src[:, g:g + 1]
                emiti("dve", (lambda g, xx0: lambda e: e.tensor_sub(
                    s_pl[:, 0:1], fw(t_xn1, g)[:, 0:1], xx0))(g, xx0), "dve")
                emiti("dve", (lambda g: lambda e: e.tensor_sub(
                    s_pl[:, 1:TT], fw(t_xn1, g)[:, 1:TT], fw(t_xn1, g)[:, 0:TT - 1]))(g), "dve")
                emiti("dve", (lambda g, xx0: lambda e: e.scalar_tensor_tensor(
                    fw(t_A, g)[:, 0:1], s_pl[:, 0:1], pcol("mu_r", g), xx0,
                    ALU.mult, ALU.add))(g, xx0), "dve")
                emiti("dve", (lambda g: lambda e: e.scalar_tensor_tensor(
                    fw(t_A, g)[:, 1:TT], s_pl[:, 1:TT], pcol("mu_r", g),
                    fw(t_xn1, g)[:, 0:TT - 1], ALU.mult, ALU.add))(g), "dve")
        xr_done = cnt["dve"]
        emiti("dve", lambda e: e.tensor_copy(
            t_tailA[:, :], fw3(t_xn1)[:, :, TT - 1:TT]), "dve")
        tailA_done = cnt["dve"]
        if t + 1 < NT:
            wait("pl", "dve", tailA_done)
            emiti("pl", (lambda t: lambda e: e.dma_start(
                fw3(t_xn1), d_xT.rearrange("(g p) n -> p g n", p=P)[:, :, (t + 1) * TT:(t + 2) * TT]))(t), "x", 16)
            xload_ld[0] = cnt["x"]

        if is_att:
            if DEBUG_TAPS:
                wait("pl", "dve", scan_done)
                emiti("pl", (lambda t: lambda e: e.dma_start(
                    taps["tap_wkv"].rearrange("(g p) n -> p g n", p=P)[:, :, t * TT:(t + 1) * TT],
                    fw3(t_h)))(t), "st", 16)
            rw_guard = [scan_done, scan_done]
            wait("pe", "dve", xr_done)
            for og in range(NG):
                bir = next_bank()
                half_proj(d_wr, og, lambda kc: fw(t_A, kc), bir)
                rbuf = s_t1 if og % 2 == 0 else s_t2
                wait("act", "pe", cnt["pe"])
                wait("act", "dve", rw_guard[og % 2])
                emiti("act", (lambda bir, rbuf: lambda e: e.activation(
                    rbuf[:, :], ps_f[bir][:, :], AF.Sigmoid))(bir, rbuf), "act")
                bank_consumed(bir, "act")
                wait("dve", "act", cnt["act"])
                if DEBUG_TAPS:
                    wait("dve", "st", cnt["st"])
                emiti("dve", (lambda og, rbuf: lambda e: e.tensor_mul(
                    fw(t_h, NG + og), fw(t_h, og), rbuf[:, :]))(og, rbuf), "dve")
                rw_guard[og % 2] = cnt["dve"]
            rw_done = cnt["dve"]

            wait("pe", "dve", rw_done)
            for og in range(NG):
                bio = next_bank()
                half_proj(d_wo, og, lambda kc: fw(t_h, NG + kc), bio)
                wait("dve", "pe", cnt["pe"])
                if is_real:
                    emiti("dve", (lambda og, bio: lambda e: e.tensor_add(
                        fw(t_acc, og), fw(t_acc2, og), ps_f[bio][:, :]))(og, bio), "dve")
                emiti("dve", (lambda og, bio: lambda e: e.tensor_add(
                    fw(t_acc2, og), fw(t_xn, og), ps_f[bio][:, :]))(og, bio), "dve")
                bank_consumed(bio, "dve")
            x2_done = cnt["dve"]
            if DEBUG_TAPS and is_real:
                wait("pl", "dve", x2_done)
                emiti("pl", (lambda t: lambda e: e.dma_start(
                    taps["tap_acc2"].rearrange("(g p) n -> p g n", p=P)[:, :, t * TT:(t + 1) * TT],
                    fw3(t_acc)))(t), "st", 16)
            if DEBUG_TAPS:
                wait("pl", "dve", x2_done)
                emiti("pl", (lambda t: lambda e: e.dma_start(
                    taps["tap_x2"].rearrange("(g p) n -> p g n", p=P)[:, :, t * TT:(t + 1) * TT],
                    fw3(t_acc2)))(t), "st", 16)

            ln_stats(lambda g: fw(t_acc2, g), x2_done)
            for g in range(NG):
                emiti("dve", (lambda g: lambda e: e.tensor_sub(
                    fw(t_ei, g), fw(t_acc2, g), b_mean[:, :]))(g), "dve")
                emiti("dve", (lambda g: lambda e: e.tensor_mul(
                    s_t1[:, :], fw(t_ei, g), b_rstd[:, :]))(g), "dve")
                emiti("dve", (lambda g: lambda e: e.tensor_scalar(
                    fw(t_ei, g), s_t1[:, :], pcol("ln2_g", g), pcol("ln2_b", g),
                    ALU.mult, ALU.add))(g), "dve")
            ei_done_s[0] = cnt["dve"]
            if DEBUG_TAPS:
                wait("pl", "dve", ei_done_s[0])
                emiti("pl", (lambda t: lambda e: e.dma_start(
                    taps["tap_ei"].rearrange("(g p) n -> p g n", p=P)[:, :, t * TT:(t + 1) * TT],
                    fw3(t_ei)))(t), "st", 16)

        if is_real:
            ei_done = ei_done_s[0]
            wait("pe", "dve", ei_done)
            wait("pe", "dve", stat_free[0])
            for sb in range(4):
                for g in range(NG):
                    emiti("pe", (lambda sb, g: lambda e: e.matmul(
                        ps_s1[0:P, sb * 8:(sb + 1) * 8],
                        fw(t_ei, g)[:, sb * P:(sb + 1) * P],
                        t_rws[:, g * 8:(g + 1) * 8],
                        start=(g == 0), stop=(g == NG - 1)))(sb, g), "pe")
            wait("dve", "pe", cnt["pe"])
            emiti("dve", lambda e: e.tensor_copy(t_lg[:, :], ps_s1[0:P, 0:32]), "dve")
            wait("dve", "dve", cnt["dve"])
            lg3 = t_lg.ap().rearrange("p (c e) -> p c e", e=E)
            mk13 = t_mk1.ap().rearrange("p (c e) -> p c e", e=E)
            mk23 = t_mk2.ap().rearrange("p (c e) -> p c e", e=E)
            tw3 = t_twm.ap().rearrange("p (c e) -> p c e", e=E)
            emiti("dve", lambda e: e.reduce_max(t_m1[:, :], lg3, axis=AX.X), "dve")
            wait("dve", "dve", cnt["dve"])
            emiti("dve", lambda e: e.tensor_tensor(
                mk13, lg3, t_m1.ap().unsqueeze(2).broadcast_to([P, 4, E]), ALU.is_equal), "dve")
            emiti("dve", lambda e: e.scalar_tensor_tensor(
                t_mk2[:, :], t_mk1[:, :], -1e30, t_lg[:, :], ALU.mult, ALU.add), "dve")
            wait("dve", "dve", cnt["dve"])
            emiti("dve", lambda e: e.reduce_max(t_m2[:, :], mk23, axis=AX.X), "dve")
            wait("dve", "dve", cnt["dve"])
            emiti("dve", lambda e: e.tensor_tensor(
                mk23, lg3, t_m2.ap().unsqueeze(2).broadcast_to([P, 4, E]), ALU.is_equal), "dve")
            emiti("dve", lambda e: e.tensor_sub(t_d12[:, :], t_m1[:, :], t_m2[:, :]), "dve")
            wait("act", "dve", cnt["dve"])
            emiti("act", lambda e: e.activation(t_w1[:, :], t_d12[:, :], AF.Sigmoid), "act")
            wait("dve", "act", cnt["act"])
            emiti("dve", lambda e: e.tensor_sub(t_lg[:, :], t_mk1[:, :], t_mk2[:, :]), "dve")
            emiti("dve", lambda e: e.tensor_tensor(
                t_twm.ap().rearrange("p (c e) -> p c e", e=E), t_lg.ap().rearrange("p (c e) -> p c e", e=E),
                t_w1.ap().unsqueeze(2).broadcast_to([P, 4, E]), ALU.mult), "dve")
            emiti("dve", lambda e: e.tensor_add(t_lg[:, :], t_twm[:, :], t_mk2[:, :]), "dve")
            emiti("dve", lambda e: e.tensor_copy(t_twm[:, :], t_lg[:, :]), "dve")
            wait("dve", "dve", cnt["dve"])
            tw_done = cnt["dve"]
            if DEBUG_TAPS:
                wait("pl", "dve", tw_done)
                emiti("pl", (lambda t: lambda e: e.dma_start(
                    taps["tap_tw"][:, t * 32:(t + 1) * 32], t_twm[:, :]))(t), "st", 16)
            wait("pe", "dve", tw_done)
            for sb in range(4):
                emiti("pe", (lambda sb: lambda e: e.transpose(
                    ps_s1[0:8, sb * P:(sb + 1) * P], t_twm[:, sb * 8:(sb + 1) * 8],
                    t_id[:, :]))(sb), "pe")
            wait("dve", "pe", cnt["pe"])
            emiti("dve", lambda e: e.tensor_copy(t_twT[:, :], ps_s1[0:8, :]), "dve")
            wait("dve", "dve", cnt["dve"])
            stat_free[0] = cnt["dve"]
            twT_done = cnt["dve"]

            etail = t_mtail if t == REAL0 else t_tailE
            if t == REAL0:
                emiti("dve", lambda e: e.tensor_scalar(
                    t_mtail[:, :], t_tailE[:, :], t_z01[:, 0:1], None, ALU.mult), "dve")
                wait("dve", "dve", cnt["dve"])
            for g in range(NG):
                emiti("dve", (lambda g, etail: lambda e: e.tensor_sub(
                    fw(t_A, g)[:, 0:1], fw(t_ei, g)[:, 0:1], etail[:, g:g + 1]))(g, etail), "dve")
                emiti("dve", (lambda g: lambda e: e.tensor_sub(
                    fw(t_A, g)[:, 1:TT], fw(t_ei, g)[:, 1:TT], fw(t_ei, g)[:, 0:TT - 1]))(g), "dve")
            d2_done = cnt["dve"]

            def ld_ewk0(exn):
                return [stream_load(ewkS[hh], wblock_dst(ewkS[hh], 4),
                                    d_ewk[exn, 0, :, hh * 4:(hh + 1) * 4, :]) for hh in range(2)]

            pre_ewk = ld_ewk0(0)
            pre_rkv = [None]
            xk_pre = [0]
            for ex in range(E):
                tb = ps_s2 if ex % 2 == 0 else ps_s3
                tb_free = ps2_free if ex % 2 == 0 else ps3_free
                wait("pe", "dve", twT_done)
                wait("pe", "dve", tb_free[0])
                emiti("pe", (lambda ex, tb: lambda e: e.matmul(
                    tb[:, :], t_oh[:, ex * P:(ex + 1) * P], t_twT[:, :],
                    start=True, stop=True))(ex, tb), "pe")
                if ex == 0:
                    wait("dve", "dve", d2_done)
                    wait("dve", "pe", bfree_pe[0])
                    for g in range(NG):
                        emiti("dve", (lambda ex, g: lambda e: e.scalar_tensor_tensor(
                            fw(t_B, g), fw(t_A, g), t_nk[:, g * 8 + ex:g * 8 + ex + 1],
                            fw(t_ei, g), ALU.mult, ALU.add))(ex, g), "dve")
                    xk_pl = cnt["dve"]
                else:
                    xk_pl = xk_pre[0]
                wait("pe", "dve", xk_pl)
                hsq_guard = [0, 0]
                for og in range(NGH):
                    bi = next_bank()
                    for hh in range(2):
                        buf = ewkS[hh]
                        if og == 0:
                            ldh = pre_ewk[hh]
                        else:
                            ldh = stream_load(buf, wblock_dst(buf, 4),
                                              d_ewk[ex, og, :, hh * 4:(hh + 1) * 4, :])
                        wait("pe", ldh[0], ldh[1])
                        for kc in range(4):
                            emiti("pe", (lambda buf, kc, bi, hh: lambda e: e.matmul(
                                ps_f[bi][:, :], wchunk(buf, kc), fw(t_B, hh * 4 + kc),
                                start=(hh == 0 and kc == 0), stop=(hh == 1 and kc == 3)))(buf, kc, bi, hh), "pe")
                        consumed(buf, "pe")
                    if og == NGH - 4:
                        pre_rkv[0] = (
                            stream_load(ewrS[0], wblock_dst(ewrS[0], NG),
                                        d_ewr[ex, 0, :, :, :]),
                            [stream_load(ewvS[hh], wblock_dst(ewvS[hh], NG),
                                         d_ewv[ex, 0, :, hh * NG:(hh + 1) * NG, :]) for hh in range(2)])
                    rlb = s_BsN if og % 2 == 0 else s_BsD
                    wait("act", "pe", cnt["pe"])
                    wait("act", "dve", hsq_guard[og % 2])
                    emiti("act", (lambda bi, rlb: lambda e: e.activation(
                        rlb[:, :], ps_f[bi][:, :], AF.Relu))(bi, rlb), "act")
                    bank_consumed(bi, "act")
                    wait("dve", "act", cnt["act"])
                    emiti("dve", (lambda og, rlb: lambda e: e.tensor_mul(
                        fw(t_h, og), rlb[:, :], rlb[:, :]))(og, rlb), "dve")
                    hsq_guard[og % 2] = cnt["dve"]
                h_pe = cnt["pe"]
                h_done = cnt["dve"]
                wait("dve", "pe", h_pe)
                for g in range(NG):
                    emiti("dve", (lambda ex, g: lambda e: e.scalar_tensor_tensor(
                        fw(t_B, g), fw(t_A, g), t_nr[:, g * 8 + ex:g * 8 + ex + 1],
                        fw(t_ei, g), ALU.mult, ALU.add))(ex, g), "dve")
                xr_pl = cnt["dve"]
                rsig_guard = [h_done, h_done]
                for og in range(NG):
                    bir = next_bank()
                    buf = ewrS[og % 2]
                    if og == 0:
                        ldr = pre_rkv[0][0]
                    else:
                        ldr = stream_load(buf, wblock_dst(buf, NG),
                                          d_ewr[ex, og, :, :, :])
                    wait("pe", ldr[0], ldr[1])
                    if og == 0:
                        wait("pe", "dve", xr_pl)
                    for kc in range(NG):
                        emiti("pe", (lambda buf, kc, bir: lambda e: e.matmul(
                            ps_f[bir][:, :], wchunk(buf, kc), fw(t_B, kc),
                            start=(kc == 0), stop=(kc == NG - 1)))(buf, kc, bir), "pe")
                    consumed(buf, "pe")
                    r_mm_pe = cnt["pe"]
                    if og == NG - 1 and ex < E - 1:
                        wait("dve", "pe", r_mm_pe)
                        for g in range(NG):
                            emiti("dve", (lambda exn, g: lambda e: e.scalar_tensor_tensor(
                                fw(t_B, g), fw(t_A, g), t_nk[:, g * 8 + exn:g * 8 + exn + 1],
                                fw(t_ei, g), ALU.mult, ALU.add))(ex + 1, g), "dve")
                        xk_pre[0] = cnt["dve"]
                    rsb = s_BsN if og % 2 == 0 else s_BsD
                    wait("act", "pe", r_mm_pe)
                    wait("act", "dve", rsig_guard[og % 2])
                    emiti("act", (lambda bir, rsb: lambda e: e.activation(
                        rsb[:, :], ps_f[bir][:, :], AF.Sigmoid))(bir, rsb), "act")
                    bank_consumed(bir, "act")
                    rsig_act = cnt["act"]
                    bikv = next_bank()
                    for hh in range(2):
                        buf = ewvS[hh]
                        if og == 0:
                            ldv2 = pre_rkv[0][1][hh]
                        else:
                            ldv2 = stream_load(buf, wblock_dst(buf, NG),
                                               d_ewv[ex, og, :, hh * NG:(hh + 1) * NG, :])
                        wait("pe", ldv2[0], ldv2[1])
                        if og == 0 and hh == 0:
                            wait("pe", "dve", h_done)
                        for kc in range(NG):
                            emiti("pe", (lambda buf, kc, bikv, hh: lambda e: e.matmul(
                                ps_f[bikv][:, :], wchunk(buf, kc), fw(t_h, hh * NG + kc),
                                start=(hh == 0 and kc == 0), stop=(hh == 1 and kc == NG - 1)))(buf, kc, bikv, hh), "pe")
                        consumed(buf, "pe")
                    if og == 0 and ex < E - 1:
                        pre_ewk = ld_ewk0(ex + 1)
                    wait("dve", "pe", cnt["pe"])
                    wait("dve", "act", rsig_act)
                    emiti("dve", (lambda bikv, rsb: lambda e: e.tensor_mul(
                        s_t1[:, :], ps_f[bikv][:, :], rsb[:, :]))(bikv, rsb), "dve")
                    bank_consumed(bikv, "dve")
                    emiti("dve", (lambda tb: lambda e: e.tensor_mul(
                        s_t2[:, :], s_t1[:, :], tb[:, :]))(tb), "dve")
                    asrc, adst = (t_acc, t_acc2) if ex % 2 == 0 else (t_acc2, t_acc)
                    emiti("dve", (lambda og, asrc, adst: lambda e: e.tensor_add(
                        fw(adst, og), fw(asrc, og), s_t2[:, :]))(og, asrc, adst), "dve")
                    rsig_guard[og % 2] = cnt["dve"]
                bfree_pe[0] = cnt["pe"]
                tb_free[0] = cnt["dve"]
            moe_done = cnt["dve"]

            odst = d_out.rearrange("(g p) n -> p g n", p=P)
            for en, g0, g1 in [("pl", 0, 3), ("sp", 3, 6), ("act", 6, 8)]:
                wait(en, "dve", moe_done)
                emiti(en, (lambda t, g0, g1: lambda e: e.dma_start(
                    odst[:, g0:g1, (t - REAL0) * TT:(t - REAL0 + 1) * TT],
                    fw3(t_acc)[:, g0:g1, :]))(t, g0, g1), "st", 16)
            store_done[0] = cnt["st"]

        if is_att and t + 1 < NT:
            emiti("dve", lambda e: e.tensor_copy(
                t_tailE[:, :], fw3(t_ei)[:, :, TT - 1:TT]), "dve")
        tile_end_dve[0] = cnt["dve"]

    for t in range(NT):
        emit_tile(t)

    wait("pl", "st", cnt["st"])
    wait("pl", "x", cnt["x"])
    wait("sp", "st", cnt["st"])

    import contextlib
    with contextlib.ExitStack() as _st:
        sem_h = {}
        for ch in cnt:
            sem_h[ch] = _st.enter_context(nc.semaphore(f"s_{ch}"))
        block = _st.enter_context(nc.Block())

        def replay(eng, items):
            for it in items:
                if it[0] == "wait":
                    eng.wait_ge(sem_h[it[1]], it[2])
                else:
                    _, fn, ch, amt = it
                    inst = fn(eng)
                    if ch is not None:
                        inst.then_inc(sem_h[ch], amt)

        @block.tensor
        def _(e):
            replay(e, prog["pe"])

        @block.vector
        def _(e):
            replay(e, prog["dve"])

        @block.scalar
        def _(e):
            replay(e, prog["act"])

        @block.sync
        def _(e):
            replay(e, prog["sp"])

        @block.gpsimd
        def _(e):
            replay(e, prog["pl"])

    return nc


def _pack_pvec(inp):
    pv = np.zeros((P, PCOLS), np.float32)

    def put(name, vec):
        v = np.asarray(vec, np.float32).reshape(NG, P)
        pv[:, PV[name]:PV[name] + NG] = v.T

    put("ln1_g", inp["L0_attn_ln1_g"]); put("ln1_b", inp["L0_attn_ln1_b"])
    put("ln2_g", inp["L0_attn_ln2_g"]); put("ln2_b", inp["L0_attn_ln2_b"])
    put("mu_k", inp["L0_attn_tm_mu_k"]); put("mu_v", inp["L0_attn_tm_mu_v"])
    put("mu_r", inp["L0_attn_tm_mu_r"])
    put("td", inp["L0_attn_time_decay"]); put("tf", inp["L0_attn_time_first"])
    put("ffn_bias", inp["L0_ffn_bias"])
    cmk = np.asarray(inp["L0_attn_cm_mu_k"], np.float32)
    cmr = np.asarray(inp["L0_attn_cm_mu_r"], np.float32)
    for g in range(NG):
        for e in range(E):
            pv[:, PV["cm_k"] + g * 8 + e] = cmk[e, g * P:(g + 1) * P]
            pv[:, PV["cm_r"] + g * 8 + e] = cmr[e, g * P:(g + 1) * P]
    return pv


def kernel(**inputs):
    x = np.asarray(inputs["L0_attn_x"], np.float32)
    B, S, _ = x.shape
    pvec = _pack_pvec(inputs)
    ident = np.eye(P, dtype=np.float32)
    onehot = np.zeros((E, D), np.float32)
    for e in range(E):
        onehot[e, e * P:(e + 1) * P] = 1.0

    def swz(w):
        w = np.asarray(w, np.float32)
        K, M = w.shape
        return np.ascontiguousarray(
            w.reshape(K // P, P, M // P, P).transpose(2, 1, 0, 3))

    def swz_e(w):
        w = np.asarray(w, np.float32)
        _, K, M = w.shape
        return np.ascontiguousarray(
            w.reshape(E, K // P, P, M // P, P).transpose(0, 3, 2, 1, 4))

    shared = dict(
        Wk=swz(inputs["L0_attn_Wk"]),
        Wv=swz(inputs["L0_attn_Wv"]),
        Wr=swz(inputs["L0_attn_Wr"]),
        Wo=swz(inputs["L0_attn_Wo"]),
        router_w_s=np.ascontiguousarray(
            np.asarray(inputs["L0_attn_router_w"], np.float32)
            .reshape(NG, P, E).transpose(1, 0, 2).reshape(P, NG * E)),
        eWk=swz_e(inputs["L0_attn_eWk"]),
        eWv=swz_e(inputs["L0_attn_eWv"]),
        eWr=swz_e(inputs["L0_attn_eWr"]),
        ffk=np.ascontiguousarray(inputs["L0_ffn_kernel"], np.float32),
        ffa_s=np.ascontiguousarray(
            np.asarray(inputs["L0_ffn_a"], np.float32)
            .reshape(NG, P, 16).transpose(1, 0, 2).reshape(P, NG * 16)),
        ffb=np.ascontiguousarray(inputs["L0_ffn_b"], np.float32),
        pvec=pvec, cident=ident,
        conescol=np.ones((P, 1), np.float32),
        conesrow=np.ones((1, P), np.float32),
        cinv1024=np.full((1, P), 1.0 / D, np.float32),
        conehot=onehot,
        czcol=np.zeros((P, 1), np.float32),
        ceps=np.full((P, 1), EPS, np.float32),
        czc16=np.zeros((P, 16), np.float32),
    )

    in_maps = []
    for c in range(8):
        b, q = c // 4, c % 4
        s0 = q * LC
        lo = s0 - HALO
        if lo < 0:
            xs = np.concatenate(
                [np.zeros((D, -lo), np.float32),
                 np.ascontiguousarray(x[b, 0:s0 + LC].T)], axis=1)
        else:
            xs = np.ascontiguousarray(x[b, lo:s0 + LC].T)
        first = 1.0 if q > 0 else 0.0
        m = dict(shared)
        m["xT"] = np.ascontiguousarray(xs)
        m["cme"] = np.full((P, 1), first, np.float32)
        m["cz01"] = np.full((P, 1), first, np.float32)
        in_maps.append(m)

    nc = build()
    res = run_bass_kernel_spmd(nc, in_maps, core_ids=list(range(8)))

    out = np.empty((B, S, D), np.float32)
    for c in range(8):
        b, q = c // 4, c % 4
        out[b, q * LC:(q + 1) * LC] = res.results[c]["outT"].T
    if DEBUG_TAPS:
        kernel.last_results = res
    return out


# version 25 (final)
# speedup vs baseline: 1.1178x; 1.0284x over previous
import numpy as np
import concourse.bass as bass
import concourse.mybir as mybir
from concourse.bass_utils import run_bass_kernel_spmd

F32 = mybir.dt.float32
I32 = mybir.dt.int32
AF = mybir.ActivationFunctionType
ALU = mybir.AluOpType
AX = mybir.AxisListType

P = 128
TT = 512
NT = 9
NG = 8
D = 1024
E = 8
FH = 2048
NGH = 16
HALO = 512
LC = 4096
TFULL = HALO + LC
EPS = 1e-5
REAL0 = 1
ATT0 = 0

DEBUG_TAPS = False

PV = {}
_pv_names = ["ln1_g", "ln1_b", "ln2_g", "ln2_b", "mu_k", "mu_v", "mu_r",
             "td", "tf", "ffn_bias"]
for _i, _n in enumerate(_pv_names):
    PV[_n] = _i * NG
PV["cm_k"] = 10 * NG
PV["cm_r"] = 10 * NG + 64
PCOLS = 10 * NG + 128


def build():
    nc = bass.Bass(target_bir_lowering=False)

    d_xT = nc.dram_tensor("xT", [D, TFULL], F32, kind="ExternalInput").ap()
    d_wk = nc.dram_tensor("Wk", [NG, P, NG, P], F32, kind="ExternalInput").ap()
    d_wv = nc.dram_tensor("Wv", [NG, P, NG, P], F32, kind="ExternalInput").ap()
    d_wr = nc.dram_tensor("Wr", [NG, P, NG, P], F32, kind="ExternalInput").ap()
    d_wo = nc.dram_tensor("Wo", [NG, P, NG, P], F32, kind="ExternalInput").ap()
    d_rw = nc.dram_tensor("router_w_s", [P, NG * E], F32, kind="ExternalInput").ap()
    d_ewk = nc.dram_tensor("eWk", [E, 2 * NG, P, NG, P], F32, kind="ExternalInput").ap()
    d_ewv = nc.dram_tensor("eWv", [E, NG, P, 2 * NG, P], F32, kind="ExternalInput").ap()
    d_ewr = nc.dram_tensor("eWr", [E, NG, P, NG, P], F32, kind="ExternalInput").ap()
    d_ffk = nc.dram_tensor("ffk", [D, D], F32, kind="ExternalInput").ap()
    d_ffa = nc.dram_tensor("ffa_s", [P, NG * 16], F32, kind="ExternalInput").ap()
    d_ffb = nc.dram_tensor("ffb", [16, D], F32, kind="ExternalInput").ap()
    d_pvec = nc.dram_tensor("pvec", [P, PCOLS], F32, kind="ExternalInput").ap()
    d_ident = nc.dram_tensor("cident", [P, P], F32, kind="ExternalInput").ap()
    d_onescol = nc.dram_tensor("conescol", [P, 1], F32, kind="ExternalInput").ap()
    d_onesrow = nc.dram_tensor("conesrow", [1, P], F32, kind="ExternalInput").ap()
    d_inv1024 = nc.dram_tensor("cinv1024", [1, P], F32, kind="ExternalInput").ap()
    d_onehot = nc.dram_tensor("conehot", [E, D], F32, kind="ExternalInput").ap()
    d_zcol = nc.dram_tensor("czcol", [P, 1], F32, kind="ExternalInput").ap()
    d_zc16 = nc.dram_tensor("czc16", [P, 16], F32, kind="ExternalInput").ap()
    d_me = nc.dram_tensor("cme", [P, 1], F32, kind="ExternalInput").ap()
    d_eps = nc.dram_tensor("ceps", [P, 1], F32, kind="ExternalInput").ap()
    d_z01 = nc.dram_tensor("cz01", [P, 1], F32, kind="ExternalInput").ap()
    d_out = nc.dram_tensor("outT", [D, LC], F32, kind="ExternalOutput").ap()
    d_wq = nc.dram_tensor("wq_scratch", [P, NG, NG, P], F32, kind="Internal").ap()

    taps = {}
    if DEBUG_TAPS:
        for nm, shp in [("tap_wkv", [D, TFULL]), ("tap_x2", [D, TFULL]),
                        ("tap_ei", [D, TFULL]), ("tap_xn", [D, TFULL]),
                        ("tap_tw", [P, 32 * NT]), ("tap_acc2", [D, TFULL])]:
            taps[nm] = nc.dram_tensor(nm, shp, F32, kind="ExternalOutput").ap()

    t_xn = nc.alloc_sbuf_tensor("t_xn", [P, NG * TT], F32)
    t_xn1 = nc.alloc_sbuf_tensor("t_xn1", [P, NG * TT], F32)
    t_acc = nc.alloc_sbuf_tensor("t_acc", [P, NG * TT], F32)
    t_ei = nc.alloc_sbuf_tensor("t_ei", [P, NG * TT], F32)
    t_A = nc.alloc_sbuf_tensor("t_A", [P, NG * TT], F32)
    t_B = nc.alloc_sbuf_tensor("t_B", [P, NG * TT], F32)
    t_h = nc.alloc_sbuf_tensor("t_h", [P, NGH * TT], F32)

    s_EK = nc.alloc_sbuf_tensor("s_EK", [P, TT], F32)
    s_EV = nc.alloc_sbuf_tensor("s_EV", [P, TT], F32)
    s_BsN = nc.alloc_sbuf_tensor("s_BsN", [P, TT], F32)
    s_BsD = nc.alloc_sbuf_tensor("s_BsD", [P, TT], F32)
    s_t1 = nc.alloc_sbuf_tensor("s_t1", [P, TT], F32)
    s_pl = nc.alloc_sbuf_tensor("s_pl", [P, TT], F32)
    s_pl2 = nc.alloc_sbuf_tensor("s_pl2", [P, TT], F32)
    s_t2 = nc.alloc_sbuf_tensor("s_t2", [P, TT], F32)
    b_mean = nc.alloc_sbuf_tensor("b_mean", [P, TT], F32)
    b_var = nc.alloc_sbuf_tensor("b_var", [P, TT], F32)
    b_rstd = nc.alloc_sbuf_tensor("b_rstd", [P, TT], F32)
    b_rstd1 = nc.alloc_sbuf_tensor("b_rstd1", [P, TT], F32)

    t_acc2 = nc.alloc_sbuf_tensor("t_acc2", [P, NG * TT], F32)
    wS = [nc.alloc_sbuf_tensor(f"wS{i}", [P, 4 * P], F32) for i in range(2)]
    ewkS = [nc.alloc_sbuf_tensor(f"ewkS{i}", [P, 4 * P], F32) for i in range(2)]
    ewvS = [nc.alloc_sbuf_tensor(f"ewvS{i}", [P, NG * P], F32) for i in range(2)]
    ewrS = [nc.alloc_sbuf_tensor(f"ewrS{i}", [P, NG * P], F32) for i in range(2)]

    t_la = b_rstd1[0:16, :]
    t_sums = b_mean[0:1, :]
    t_sumsq = b_var[0:1, :]
    t_pv = nc.alloc_sbuf_tensor("t_pv", [P, PCOLS], F32)
    t_gd = nc.alloc_sbuf_tensor("t_gd", [P, NG], F32)
    t_eu = nc.alloc_sbuf_tensor("t_eu", [P, NG], F32)
    t_nk = nc.alloc_sbuf_tensor("t_nk", [P, 64], F32)
    t_nr = nc.alloc_sbuf_tensor("t_nr", [P, 64], F32)
    t_e1 = nc.alloc_sbuf_tensor("t_e1", [P, NG], F32)
    t_cN = nc.alloc_sbuf_tensor("t_cN", [P, NG], F32)
    t_cD = nc.alloc_sbuf_tensor("t_cD", [P, NG], F32)
    t_tailA = nc.alloc_sbuf_tensor("t_tailA", [P, NG], F32)
    t_tailE = nc.alloc_sbuf_tensor("t_tailE", [P, NG], F32)
    t_mtail = nc.alloc_sbuf_tensor("t_mtail", [P, NG], F32)
    t_id = nc.alloc_sbuf_tensor("t_id", [P, P], F32)
    t_oc = nc.alloc_sbuf_tensor("t_oc", [P, 1], F32)
    t_or = nc.alloc_sbuf_tensor("t_or", [1, P], F32)
    t_iv = nc.alloc_sbuf_tensor("t_iv", [1, P], F32)
    t_oh = nc.alloc_sbuf_tensor("t_oh", [E, D], F32)
    t_zc = nc.alloc_sbuf_tensor("t_zc", [P, 1], F32)
    t_me = nc.alloc_sbuf_tensor("t_me", [P, 1], F32)
    t_eps = nc.alloc_sbuf_tensor("t_eps", [P, 1], F32)
    t_z01 = nc.alloc_sbuf_tensor("t_z01", [P, 1], F32)
    t_rws = nc.alloc_sbuf_tensor("t_rws", [P, NG * E], F32)
    t_as = nc.alloc_sbuf_tensor("t_as", [P, NG * 16], F32)
    t_bs = nc.alloc_sbuf_tensor("t_bs", [16, D], F32)
    t_lg = nc.alloc_sbuf_tensor("t_lg", [P, 32], F32)
    t_m1 = nc.alloc_sbuf_tensor("t_m1", [P, 4], F32)
    t_m2 = nc.alloc_sbuf_tensor("t_m2", [P, 4], F32)
    t_d12 = nc.alloc_sbuf_tensor("t_d12", [P, 4], F32)
    t_w1 = nc.alloc_sbuf_tensor("t_w1", [P, 4], F32)
    t_mk1 = nc.alloc_sbuf_tensor("t_mk1", [P, 32], F32)
    t_mk2 = nc.alloc_sbuf_tensor("t_mk2", [P, 32], F32)
    t_twm = nc.alloc_sbuf_tensor("t_twm", [P, 32], F32)
    t_twT = nc.alloc_sbuf_tensor("t_twT", [E, TT], F32)
    t_mx8 = nc.alloc_sbuf_tensor("t_mx8", [P, NG], F32)
    t_mx1 = nc.alloc_sbuf_tensor("t_mx1", [P, 1], F32)
    t_mrow = nc.alloc_sbuf_tensor("t_mrow", [1, P], F32)
    t_M = nc.alloc_sbuf_tensor("t_M", [1, 1], F32)
    t_scB = nc.alloc_sbuf_tensor("t_scB", [P, 1], F32)
    t_isB = nc.alloc_sbuf_tensor("t_isB", [P, 1], F32)

    PSN = 5
    ps_f = [nc.alloc_psum_tensor(f"ps_f{i}", [P, TT], F32) for i in range(PSN)]
    ps_s1 = nc.alloc_psum_tensor("ps_s1", [P, TT], F32)
    ps_s2 = nc.alloc_psum_tensor("ps_s2", [P, TT], F32)
    ps_s3 = nc.alloc_psum_tensor("ps_s3", [P, TT], F32)

    prog = {k: [] for k in ["pe", "dve", "act", "sp", "pl"]}
    WCH = ["w0", "w1", "w0b", "w1b", "ek0", "ek1", "ev0", "ev1", "er0", "er1"]
    cnt = {"pe": 0, "dve": 0, "act": 0, "ld": 0, "q": 0, "x": 0, "st": 0, "plc": 0}
    for _c in WCH:
        cnt[_c] = 0

    def emit(q, fn):
        prog[q].append(("op", fn, None, 0))

    def emiti(q, fn, ch, amt=1):
        if (q in ("dve", "act") and ch == q) or (q == "pl" and ch == "plc"):
            prog[q].append(("wait", ch, cnt[ch], 0))
        cnt[ch] += amt
        prog[q].append(("op", fn, ch, amt))

    def wait(q, ch, val):
        if val > 0:
            prog[q].append(("wait", ch, val, 0))

    bank_free = [("dve", 0)] * PSN
    bank_i = [0]

    def next_bank():
        i = bank_i[0] % PSN
        bank_i[0] += 1
        ch, v = bank_free[i]
        wait("pe", ch, v)
        return i

    def bank_consumed(i, ch):
        bank_free[i] = (ch, cnt[ch])

    def dma(dst, src):
        return lambda e: e.dma_start(dst, src)

    def fw(tens, g):
        return tens[:, g * TT:(g + 1) * TT]

    def fw3(tens, ngroups=NG):
        return tens.ap().rearrange("p (g n) -> p g n", n=TT)[:, 0:ngroups, :]

    def pcol(name, g):
        c = PV[name] + g
        return t_pv[:, c:c + 1]

    def wchunk(buf, kc):
        return buf[:, kc * P:(kc + 1) * P]

    def mblock_src(w_ap, og, kc0=0, nkc=None):
        if nkc is None:
            sl = w_ap[:, og * P:(og + 1) * P]
        else:
            sl = w_ap[kc0 * P:(kc0 + nkc) * P, og * P:(og + 1) * P]
        return sl.rearrange("(c p) m -> p c m", p=P)

    def wblock_dst(buf, nkc):
        return buf.ap().rearrange("p (c m) -> p c m", m=P)[:, 0:nkc, :]

    for dst, src in [(t_pv[:, :], d_pvec), (t_id[:, :], d_ident),
                     (t_oc[:, :], d_onescol), (t_or[:, :], d_onesrow),
                     (t_iv[:, :], d_inv1024), (t_oh[:, :], d_onehot),
                     (t_zc[:, :], d_zcol), (t_eps[:, :], d_eps),
                     (t_cN[:, :], d_zc16[:, 0:8]),
                     (t_cD[:, :], d_zc16[:, 8:16]), (t_me[:, :], d_me),
                     (t_z01[:, :], d_z01),
                     (t_rws[:, :], d_rw),
                     (t_as[:, :], d_ffa),
                     (t_bs[:, :], d_ffb)]:
        emiti("sp", dma(dst, src), "ld", 16)
    SETUP_LD = cnt["ld"]

    wait("act", "ld", SETUP_LD)
    emiti("act", lambda e: e.activation(t_e1[:, :], t_pv[:, PV["td"]:PV["td"] + NG], AF.Exp), "act")
    wait("act", "act", cnt["act"])
    emiti("act", lambda e: e.activation(t_gd[:, :], t_e1[:, :], AF.Exp, scale=-1.0), "act")
    emiti("act", lambda e: e.activation(t_eu[:, :], t_pv[:, PV["tf"]:PV["tf"] + NG], AF.Exp), "act")
    SETUP_ACT = cnt["act"]
    wait("dve", "ld", SETUP_LD)
    emiti("dve", lambda e: e.tensor_scalar(t_nk[:, :], t_pv[:, PV["cm_k"]:PV["cm_k"] + 64], 1.0, None, ALU.subtract), "dve")
    emiti("dve", lambda e: e.tensor_scalar(t_nr[:, :], t_pv[:, PV["cm_r"]:PV["cm_r"] + 64], 1.0, None, ALU.subtract), "dve")
    wait("dve", "dve", cnt["dve"])

    QLOAD = ewrS[1][:, 0:D]
    for kc in range(NG):
        emiti("sp", dma(QLOAD, d_ffk[kc * P:(kc + 1) * P, :]), "q", 16)
        wait("dve", "q", cnt["q"])
        emiti("dve", (lambda kc: lambda e: e.reduce_max(
            t_mx8[:, kc:kc + 1], QLOAD, axis=AX.X, apply_absolute_value=True))(kc), "dve")
        wait("sp", "dve", cnt["dve"])
    emiti("dve", lambda e: e.reduce_max(t_mx1[:, :], t_mx8[:, :], axis=AX.X), "dve")
    wait("pe", "ld", SETUP_LD)
    wait("pe", "dve", cnt["dve"])
    emiti("pe", lambda e: e.matmul(ps_s1[0:1, 0:P], t_mx1[:, :], t_id[:, :], start=True, stop=True), "pe")
    wait("dve", "pe", cnt["pe"])
    emiti("dve", lambda e: e.tensor_copy(t_mrow[:, :], ps_s1[0:1, 0:P]), "dve")
    wait("dve", "dve", cnt["dve"])
    emiti("dve", lambda e: e.reduce_max(t_M[:, :], t_mrow[:, :], axis=AX.X), "dve")
    wait("pe", "dve", cnt["dve"])
    emiti("pe", lambda e: e.matmul(ps_s3[0:P, 0:1], t_or[:, :], t_M[:, :], start=True, stop=True), "pe")
    wait("dve", "pe", cnt["pe"])
    emiti("dve", lambda e: e.tensor_scalar(t_scB[:, :], ps_s3[0:P, 0:1], 1.0 / 7.0, None, ALU.mult), "dve")
    wait("dve", "dve", cnt["dve"])
    emiti("dve", lambda e: e.reciprocal(t_isB[:, :], t_scB[:, :]), "dve")
    wait("pl", "dve", cnt["dve"])
    QU = ewvS[0][:, 0:D]
    QP = ewvS[1][:, 0:D]
    QQ = ewrS[0][:, 0:D]
    for kc in range(NG):
        wait("sp", "q", cnt["q"])
        emiti("sp", dma(QLOAD, d_ffk[kc * P:(kc + 1) * P, :]), "q", 16)
        wait("dve", "q", cnt["q"])
        emiti("dve", lambda e: e.tensor_scalar(QU, QLOAD, t_isB[:, 0:1], None, ALU.mult), "dve")
        emiti("dve", lambda e: e.tensor_scalar(QP, QU, -7.5, None, ALU.is_ge), "dve")
        cur, nxt = QP, QQ
        for j in range(-7, 7):
            thr = j + 0.5
            emiti("dve", (lambda thr, cur, nxt: lambda e: e.scalar_tensor_tensor(
                nxt, QU, thr, cur, ALU.is_ge, ALU.add))(thr, cur, nxt), "dve")
            cur, nxt = nxt, cur
        emiti("dve", (lambda cur, nxt: lambda e: e.tensor_scalar(
            nxt, cur, 8.0, None, ALU.subtract))(cur, nxt), "dve")
        q8 = nxt
        emiti("dve", (lambda q8: lambda e: e.tensor_scalar(
            QU, q8, t_scB[:, 0:1], None, ALU.mult))(q8), "dve")
        wait("sp", "dve", cnt["dve"])
        emiti("sp", (lambda kc: dma(d_wq[:, :, kc, :],
                                    QU.rearrange("p (o m) -> p o m", m=P)))(kc), "q", 16)
        wait("dve", "q", cnt["q"])
    WQ_LD = cnt["q"]

    wchan = {id(wS[0]): "w0", id(wS[1]): "w1",
             id(ewkS[0]): "ek0", id(ewkS[1]): "ek1",
             id(ewvS[0]): "ev0", id(ewvS[1]): "ev1",
             id(ewrS[0]): "er0", id(ewrS[1]): "er1"}
    wfree = {id(wS[0]): ("pe", 0), id(wS[1]): ("pe", 0),
             id(ewkS[0]): ("pe", 0), id(ewkS[1]): ("pe", 0),
             id(ewvS[0]): ("q", WQ_LD), id(ewvS[1]): ("q", WQ_LD),
             id(ewrS[0]): ("q", WQ_LD), id(ewrS[1]): ("q", WQ_LD)}

    def stream_load(buf, dst3, src_ap):
        ch, v = wfree[id(buf)]
        wait("sp", ch, v)
        wc = wchan[id(buf)]
        emiti("sp", dma(dst3, src_ap), wc, 16)
        return (wc, cnt[wc])

    wchanB = {id(wS[0]): "w0b", id(wS[1]): "w1b"}

    def stream_load2(buf, srcfn, kc0):
        ch, v = wfree[id(buf)]
        wait("sp", ch, v)
        wc, wcb = wchan[id(buf)], wchanB[id(buf)]
        d3 = buf.ap().rearrange("p (c m) -> p c m", m=P)
        emiti("sp", dma(d3[:, 0:2, :], srcfn(kc0, 2)), wc, 16)
        emiti("sp", dma(d3[:, 2:4, :], srcfn(kc0 + 2, 2)), wcb, 16)
        return [(wc, cnt[wc]), (wcb, cnt[wcb])]

    def consumed(buf, ch):
        wfree[id(buf)] = (ch, cnt[ch])

    def half_proj(w_dram, og, rhs_fw, bi, first_extra_wait=None):
        for h in range(2):
            buf = wS[wslot[0] % 2]; wslot[0] += 1
            for tk in stream_load2(
                    buf, lambda k0, n: w_dram[og, :, k0:k0 + n, :], h * 4):
                wait("pe", tk[0], tk[1])
            if h == 0 and first_extra_wait is not None:
                wait("pe", "dve", first_extra_wait)
            for kc in range(4):
                emiti("pe", (lambda buf, kc, bi, h: lambda e: e.matmul(
                    ps_f[bi][:, :], wchunk(buf, kc), rhs_fw(h * 4 + kc),
                    start=(h == 0 and kc == 0), stop=(h == 1 and kc == 3)))(buf, kc, bi, h), "pe")
            consumed(buf, "pe")

    wait("dve", "act", SETUP_ACT)
    wait("act", "ld", SETUP_LD)
    wait("sp", "q", WQ_LD)

    xload_ld = [0]
    store_done = [0]
    stat_free = [0]
    ps2_free = [0]
    ps3_free = [0]
    bfree_pe = [0]
    pe_sq = [0] * NG
    tile_end_dve = [0]
    wslot = [0]
    ei_done_s = [0]

    emiti("pl", dma(fw3(t_xn1), d_xT.rearrange("(g p) n -> p g n", p=P)[:, :, 0:TT]), "x", 16)
    xload_ld[0] = cnt["x"]

    def ln_stats(src_fw, prev_dve_guard):
        wait("pe", "dve", stat_free[0])
        wait("pe", "dve", prev_dve_guard)
        for g in range(NG):
            emiti("pe", (lambda g: lambda e: e.matmul(
                ps_s1[0:1, :], t_oc[:, :], src_fw(g), start=(g == 0), stop=(g == NG - 1)))(g), "pe")
        wait("act", "dve", prev_dve_guard)
        for g in range(NG):
            sq = s_t1 if g % 2 == 0 else s_t2
            if g >= 2:
                wait("act", "pe", pe_sq[g - 2])
            emiti("act", (lambda g, sq: lambda e: e.activation(sq[:, :], src_fw(g), AF.Square))(g, sq), "act")
            wait("pe", "act", cnt["act"])
            emiti("pe", (lambda g, sq: lambda e: e.matmul(
                ps_s2[0:1, :], t_oc[:, :], sq[:, :], start=(g == 0), stop=(g == NG - 1)))(g, sq), "pe")
            pe_sq[g] = cnt["pe"]
        wait("dve", "pe", cnt["pe"])
        emiti("dve", lambda e: e.tensor_copy(t_sums, ps_s1[0:1, :]), "dve")
        emiti("dve", lambda e: e.tensor_copy(t_sumsq, ps_s2[0:1, :]), "dve")
        wait("dve", "dve", cnt["dve"])
        wait("pe", "dve", cnt["dve"])
        emiti("pe", lambda e: e.matmul(ps_s1[:, :], t_iv[:, :], t_sums, start=True, stop=True), "pe")
        emiti("pe", lambda e: e.matmul(ps_s2[:, :], t_iv[:, :], t_sumsq, start=True, stop=True), "pe")
        wait("dve", "pe", cnt["pe"])
        emiti("dve", lambda e: e.tensor_copy(b_mean[:, :], ps_s1[:, :]), "dve")
        emiti("dve", lambda e: e.tensor_copy(b_var[:, :], ps_s2[:, :]), "dve")
        stat_free[0] = cnt["dve"]
        emiti("dve", lambda e: e.tensor_mul(s_t1[:, :], b_mean[:, :], b_mean[:, :]), "dve")
        emiti("dve", lambda e: e.tensor_sub(s_t2[:, :], b_var[:, :], s_t1[:, :]), "dve")
        wait("act", "dve", cnt["dve"])
        emiti("act", lambda e: e.activation(s_t1[:, :], s_t2[:, :], AF.Sqrt, bias=t_eps[:, 0:1]), "act")
        wait("dve", "act", cnt["act"])
        emiti("dve", lambda e: e.reciprocal(b_rstd[:, :], s_t1[:, :]), "dve")

    def emit_tile(t):
        is_att = t >= ATT0
        is_real = t >= REAL0
        x_t = t_xn1
        xld = xload_ld[0]

        wait("pe", "x", xld)
        wait("act", "x", xld)
        ln_stats(lambda g: fw(x_t, g), tile_end_dve[0])
        emiti("dve", lambda e: e.tensor_mul(s_t1[:, :], b_rstd[:, :], b_rstd[:, :]), "dve")
        emiti("dve", lambda e: e.tensor_mul(b_rstd1[:, :], s_t1[:, :], s_t2[:, :]), "dve")
        wait("act", "dve", cnt["dve"])
        emiti("act", lambda e: e.activation(s_t2[:, :], b_rstd1[:, :], AF.Sqrt, bias=t_eps[:, 0:1]), "act")
        wait("dve", "act", cnt["act"])
        emiti("dve", lambda e: e.reciprocal(b_rstd1[:, :], s_t2[:, :]), "dve")
        for g in range(NG):
            emiti("dve", (lambda g: lambda e: e.tensor_sub(s_t1[:, :], fw(x_t, g), b_mean[:, :]))(g), "dve")
            emiti("dve", (lambda g: lambda e: e.tensor_mul(fw(t_xn, g), s_t1[:, :], b_rstd[:, :]))(g), "dve")
        if is_real:
            wait("dve", "st", store_done[0])
            for g in range(NG):
                emiti("dve", (lambda g: lambda e: e.scalar_tensor_tensor(
                    fw(t_acc, g), fw(x_t, g), pcol("ffn_bias", g), fw(t_xn, g),
                    ALU.add, ALU.add))(g), "dve")
        acc_init = cnt["dve"]
        for g in range(NG):
            emiti("dve", (lambda g: lambda e: e.tensor_mul(s_t1[:, :], fw(t_xn, g), b_rstd1[:, :]))(g), "dve")
            emiti("dve", (lambda g: lambda e: e.tensor_scalar(
                fw(t_xn1, g), s_t1[:, :], pcol("ln1_g", g), pcol("ln1_b", g),
                ALU.mult, ALU.add))(g), "dve")
        xn1_done = cnt["dve"]
        xn_done = cnt["dve"]
        if DEBUG_TAPS:
            wait("pl", "dve", xn_done)
            emiti("pl", (lambda t: lambda e: e.dma_start(
                taps["tap_xn"].rearrange("(g p) n -> p g n", p=P)[:, :, t * TT:(t + 1) * TT],
                fw3(t_xn)))(t), "st", 16)

        if is_real:
            wait("pe", "dve", xn_done)
            wait("pe", "dve", ps3_free[0])
            for g in range(NG):
                emiti("pe", (lambda g: lambda e: e.matmul(
                    ps_s3[0:16, :], t_as[:, g * 16:(g + 1) * 16], fw(t_xn, g),
                    start=(g == 0), stop=(g == NG - 1)))(g), "pe")
            wait("dve", "pe", cnt["pe"])
            emiti("dve", lambda e: e.tensor_copy(t_la, ps_s3[0:16, :]), "dve")
            wait("dve", "dve", cnt["dve"])
            ps3_free[0] = cnt["dve"]
            la_done = cnt["dve"]
            for og in range(NG):
                bi = next_bank()
                for h in range(2):
                    buf = wS[wslot[0] % 2]; wslot[0] += 1
                    for tk in stream_load2(
                            buf, (lambda og: lambda k0, n: d_wq[:, og, k0:k0 + n, :])(og), h * 4):
                        wait("pe", tk[0], tk[1])
                    if h == 0 and og == 0:
                        wait("pe", "dve", la_done)
                    for kc in range(4):
                        emiti("pe", (lambda buf, kc, bi, h: lambda e: e.matmul(
                            ps_f[bi][:, :], wchunk(buf, kc), fw(t_xn, h * 4 + kc),
                            start=(h == 0 and kc == 0), stop=False))(buf, kc, bi, h), "pe")
                    consumed(buf, "pe")
                emiti("pe", (lambda og, bi: lambda e: e.matmul(
                    ps_f[bi][:, :], t_bs[:, og * P:(og + 1) * P], t_la,
                    start=False, stop=True))(og, bi), "pe")
                wait("dve", "pe", cnt["pe"])
                wait("dve", "dve", acc_init)
                emiti("dve", (lambda og, bi: lambda e: e.tensor_add(
                    fw(t_acc2, og), fw(t_acc, og), ps_f[bi][:, :]))(og, bi), "dve")
                bank_consumed(bi, "dve")

        tail_src = t_mtail if t == REAL0 else t_tailA
        if t == REAL0:
            wait("dve", "dve", xn1_done)
            emiti("dve", lambda e: e.tensor_scalar(
                t_mtail[:, :], t_tailA[:, :], t_z01[:, 0:1], None, ALU.mult), "dve")
            wait("dve", "dve", cnt["dve"])
        wait("dve", "dve", xn1_done)
        for g in range(NG):
            xx0 = t_zc[:, 0:1] if t == 0 else tail_src[:, g:g + 1]
            emiti("dve", (lambda g, xx0: lambda e: e.tensor_sub(
                s_t1[:, 0:1], fw(t_xn1, g)[:, 0:1], xx0))(g, xx0), "dve")
            emiti("dve", (lambda g: lambda e: e.tensor_sub(
                s_t1[:, 1:TT], fw(t_xn1, g)[:, 1:TT], fw(t_xn1, g)[:, 0:TT - 1]))(g), "dve")
            for name, dst in (("mu_k", t_A), ("mu_v", t_B)):
                emiti("dve", (lambda g, name, dst, xx0: lambda e: e.scalar_tensor_tensor(
                    fw(dst, g)[:, 0:1], s_t1[:, 0:1], pcol(name, g), xx0,
                    ALU.mult, ALU.add))(g, name, dst, xx0), "dve")
                emiti("dve", (lambda g, name, dst: lambda e: e.scalar_tensor_tensor(
                    fw(dst, g)[:, 1:TT], s_t1[:, 1:TT], pcol(name, g),
                    fw(t_xn1, g)[:, 0:TT - 1], ALU.mult, ALU.add))(g, name, dst), "dve")
        mix_kv = cnt["dve"]

        ek_guard = [mix_kv]
        wait("pe", "dve", mix_kv)
        for og in range(NG):
            bik = next_bank()
            half_proj(d_wk, og, lambda kc: fw(t_A, kc), bik)
            kdone = cnt["pe"]
            biv = next_bank()
            half_proj(d_wv, og, lambda kc: fw(t_B, kc), biv)
            vdone = cnt["pe"]
            wait("act", "pe", kdone)
            wait("act", "dve", ek_guard[0])
            emiti("act", (lambda bik: lambda e: e.activation(s_EK[:, :], ps_f[bik][:, :], AF.Exp))(bik), "act")
            bank_consumed(bik, "act")
            wait("dve", "act", cnt["act"])
            wait("dve", "pe", vdone)
            emiti("dve", (lambda biv: lambda e: e.tensor_mul(s_EV[:, :], s_EK[:, :], ps_f[biv][:, :]))(biv), "dve")
            bank_consumed(biv, "dve")
            gbb = t_gd[:, og:og + 1].broadcast_to([P, TT])
            if t < REAL0:
                emiti("dve", lambda e: e.tensor_scalar(s_t2[:, :], s_EV[:, :], t_me[:, 0:1], None, ALU.mult), "dve")
                emiti("dve", lambda e: e.tensor_scalar(s_t1[:, :], s_EK[:, :], t_me[:, 0:1], None, ALU.mult), "dve")
                dsrc = s_t1
                nsrc = s_t2
            else:
                dsrc = s_EK
                nsrc = s_EV
            emiti("dve", (lambda og, gbb, nsrc: lambda e: e.tensor_tensor_scan(
                s_BsN[:, :], gbb, nsrc[:, :], t_cN[:, og:og + 1], ALU.mult, ALU.add))(og, gbb, nsrc), "dve")
            emiti("dve", (lambda og, gbb, dsrc: lambda e: e.tensor_tensor_scan(
                s_BsD[:, :], gbb, dsrc[:, :], t_cD[:, og:og + 1], ALU.mult, ALU.add))(og, gbb, dsrc), "dve")
            if is_att:
                emiti("dve", (lambda og: lambda e: e.scalar_tensor_tensor(
                    s_t1[:, 0:1], s_EV[:, 0:1], t_eu[:, og:og + 1], t_cN[:, og:og + 1],
                    ALU.mult, ALU.add))(og), "dve")
                emiti("dve", (lambda og: lambda e: e.scalar_tensor_tensor(
                    s_t1[:, 1:TT], s_EV[:, 1:TT], t_eu[:, og:og + 1], s_BsN[:, 0:TT - 1],
                    ALU.mult, ALU.add))(og), "dve")
                emiti("dve", (lambda og: lambda e: e.scalar_tensor_tensor(
                    s_t2[:, 0:1], s_EK[:, 0:1], t_eu[:, og:og + 1], t_cD[:, og:og + 1],
                    ALU.mult, ALU.add))(og), "dve")
                emiti("dve", (lambda og: lambda e: e.scalar_tensor_tensor(
                    s_t2[:, 1:TT], s_EK[:, 1:TT], t_eu[:, og:og + 1], s_BsD[:, 0:TT - 1],
                    ALU.mult, ALU.add))(og), "dve")
                emiti("dve", lambda e: e.reciprocal(s_EK[:, :], s_t2[:, :]), "dve")
                emiti("dve", (lambda og: lambda e: e.tensor_mul(
                    fw(t_h, og), s_t1[:, :], s_EK[:, :]))(og), "dve")
            ek_guard[0] = cnt["dve"]
            emiti("dve", (lambda og: lambda e: e.tensor_copy(
                t_cN[:, og:og + 1], s_BsN[:, TT - 1:TT]))(og), "dve")
            emiti("dve", (lambda og: lambda e: e.tensor_copy(
                t_cD[:, og:og + 1], s_BsD[:, TT - 1:TT]))(og), "dve")
            wait("dve", "dve", cnt["dve"])
        scan_done = cnt["dve"]

        kv_pe_end = cnt["pe"]
        if is_att:
            wait("dve", "pe", kv_pe_end)
            for g in range(NG):
                xx0 = t_zc[:, 0:1] if t == 0 else tail_src[:, g:g + 1]
                emiti("dve", (lambda g, xx0: lambda e: e.tensor_sub(
                    s_pl[:, 0:1], fw(t_xn1, g)[:, 0:1], xx0))(g, xx0), "dve")
                emiti("dve", (lambda g: lambda e: e.tensor_sub(
                    s_pl[:, 1:TT], fw(t_xn1, g)[:, 1:TT], fw(t_xn1, g)[:, 0:TT - 1]))(g), "dve")
                emiti("dve", (lambda g, xx0: lambda e: e.scalar_tensor_tensor(
                    fw(t_A, g)[:, 0:1], s_pl[:, 0:1], pcol("mu_r", g), xx0,
                    ALU.mult, ALU.add))(g, xx0), "dve")
                emiti("dve", (lambda g: lambda e: e.scalar_tensor_tensor(
                    fw(t_A, g)[:, 1:TT], s_pl[:, 1:TT], pcol("mu_r", g),
                    fw(t_xn1, g)[:, 0:TT - 1], ALU.mult, ALU.add))(g), "dve")
        xr_done = cnt["dve"]
        emiti("dve", lambda e: e.tensor_copy(
            t_tailA[:, :], fw3(t_xn1)[:, :, TT - 1:TT]), "dve")
        tailA_done = cnt["dve"]
        if t + 1 < NT:
            wait("pl", "dve", tailA_done)
            emiti("pl", (lambda t: lambda e: e.dma_start(
                fw3(t_xn1), d_xT.rearrange("(g p) n -> p g n", p=P)[:, :, (t + 1) * TT:(t + 2) * TT]))(t), "x", 16)
            xload_ld[0] = cnt["x"]

        if is_att:
            if DEBUG_TAPS:
                wait("pl", "dve", scan_done)
                emiti("pl", (lambda t: lambda e: e.dma_start(
                    taps["tap_wkv"].rearrange("(g p) n -> p g n", p=P)[:, :, t * TT:(t + 1) * TT],
                    fw3(t_h)))(t), "st", 16)
            rw_guard = [scan_done, scan_done]
            wait("pe", "dve", xr_done)
            for og in range(NG):
                bir = next_bank()
                half_proj(d_wr, og, lambda kc: fw(t_A, kc), bir)
                rbuf = s_t1 if og % 2 == 0 else s_t2
                wait("act", "pe", cnt["pe"])
                wait("act", "dve", rw_guard[og % 2])
                emiti("act", (lambda bir, rbuf: lambda e: e.activation(
                    rbuf[:, :], ps_f[bir][:, :], AF.Sigmoid))(bir, rbuf), "act")
                bank_consumed(bir, "act")
                wait("dve", "act", cnt["act"])
                if DEBUG_TAPS:
                    wait("dve", "st", cnt["st"])
                emiti("dve", (lambda og, rbuf: lambda e: e.tensor_mul(
                    fw(t_h, NG + og), fw(t_h, og), rbuf[:, :]))(og, rbuf), "dve")
                rw_guard[og % 2] = cnt["dve"]
            rw_done = cnt["dve"]

            wait("pe", "dve", rw_done)
            for og in range(NG):
                bio = next_bank()
                half_proj(d_wo, og, lambda kc: fw(t_h, NG + kc), bio)
                wait("dve", "pe", cnt["pe"])
                if is_real:
                    emiti("dve", (lambda og, bio: lambda e: e.tensor_add(
                        fw(t_acc, og), fw(t_acc2, og), ps_f[bio][:, :]))(og, bio), "dve")
                emiti("dve", (lambda og, bio: lambda e: e.tensor_add(
                    fw(t_acc2, og), fw(t_xn, og), ps_f[bio][:, :]))(og, bio), "dve")
                bank_consumed(bio, "dve")
            x2_done = cnt["dve"]
            if DEBUG_TAPS and is_real:
                wait("pl", "dve", x2_done)
                emiti("pl", (lambda t: lambda e: e.dma_start(
                    taps["tap_acc2"].rearrange("(g p) n -> p g n", p=P)[:, :, t * TT:(t + 1) * TT],
                    fw3(t_acc)))(t), "st", 16)
            if DEBUG_TAPS:
                wait("pl", "dve", x2_done)
                emiti("pl", (lambda t: lambda e: e.dma_start(
                    taps["tap_x2"].rearrange("(g p) n -> p g n", p=P)[:, :, t * TT:(t + 1) * TT],
                    fw3(t_acc2)))(t), "st", 16)

            ln_stats(lambda g: fw(t_acc2, g), x2_done)
            for g in range(NG):
                emiti("dve", (lambda g: lambda e: e.tensor_sub(
                    fw(t_ei, g), fw(t_acc2, g), b_mean[:, :]))(g), "dve")
                emiti("dve", (lambda g: lambda e: e.tensor_mul(
                    s_t1[:, :], fw(t_ei, g), b_rstd[:, :]))(g), "dve")
                emiti("dve", (lambda g: lambda e: e.tensor_scalar(
                    fw(t_ei, g), s_t1[:, :], pcol("ln2_g", g), pcol("ln2_b", g),
                    ALU.mult, ALU.add))(g), "dve")
            ei_done_s[0] = cnt["dve"]
            if DEBUG_TAPS:
                wait("pl", "dve", ei_done_s[0])
                emiti("pl", (lambda t: lambda e: e.dma_start(
                    taps["tap_ei"].rearrange("(g p) n -> p g n", p=P)[:, :, t * TT:(t + 1) * TT],
                    fw3(t_ei)))(t), "st", 16)

        if is_real:
            ei_done = ei_done_s[0]
            wait("pe", "dve", ei_done)
            wait("pe", "dve", stat_free[0])
            for sb in range(4):
                for g in range(NG):
                    emiti("pe", (lambda sb, g: lambda e: e.matmul(
                        ps_s1[0:P, sb * 8:(sb + 1) * 8],
                        fw(t_ei, g)[:, sb * P:(sb + 1) * P],
                        t_rws[:, g * 8:(g + 1) * 8],
                        start=(g == 0), stop=(g == NG - 1)))(sb, g), "pe")
            wait("dve", "pe", cnt["pe"])
            emiti("dve", lambda e: e.tensor_copy(t_lg[:, :], ps_s1[0:P, 0:32]), "dve")
            wait("dve", "dve", cnt["dve"])
            lg3 = t_lg.ap().rearrange("p (c e) -> p c e", e=E)
            mk13 = t_mk1.ap().rearrange("p (c e) -> p c e", e=E)
            mk23 = t_mk2.ap().rearrange("p (c e) -> p c e", e=E)
            tw3 = t_twm.ap().rearrange("p (c e) -> p c e", e=E)
            emiti("dve", lambda e: e.reduce_max(t_m1[:, :], lg3, axis=AX.X), "dve")
            wait("dve", "dve", cnt["dve"])
            emiti("dve", lambda e: e.tensor_tensor(
                mk13, lg3, t_m1.ap().unsqueeze(2).broadcast_to([P, 4, E]), ALU.is_equal), "dve")
            emiti("dve", lambda e: e.scalar_tensor_tensor(
                t_mk2[:, :], t_mk1[:, :], -1e30, t_lg[:, :], ALU.mult, ALU.add), "dve")
            wait("dve", "dve", cnt["dve"])
            emiti("dve", lambda e: e.reduce_max(t_m2[:, :], mk23, axis=AX.X), "dve")
            wait("dve", "dve", cnt["dve"])
            emiti("dve", lambda e: e.tensor_tensor(
                mk23, lg3, t_m2.ap().unsqueeze(2).broadcast_to([P, 4, E]), ALU.is_equal), "dve")
            emiti("dve", lambda e: e.tensor_sub(t_d12[:, :], t_m1[:, :], t_m2[:, :]), "dve")
            wait("act", "dve", cnt["dve"])
            emiti("act", lambda e: e.activation(t_w1[:, :], t_d12[:, :], AF.Sigmoid), "act")
            wait("dve", "act", cnt["act"])
            emiti("dve", lambda e: e.tensor_sub(t_lg[:, :], t_mk1[:, :], t_mk2[:, :]), "dve")
            emiti("dve", lambda e: e.tensor_tensor(
                t_twm.ap().rearrange("p (c e) -> p c e", e=E), t_lg.ap().rearrange("p (c e) -> p c e", e=E),
                t_w1.ap().unsqueeze(2).broadcast_to([P, 4, E]), ALU.mult), "dve")
            emiti("dve", lambda e: e.tensor_add(t_lg[:, :], t_twm[:, :], t_mk2[:, :]), "dve")
            emiti("dve", lambda e: e.tensor_copy(t_twm[:, :], t_lg[:, :]), "dve")
            wait("dve", "dve", cnt["dve"])
            tw_done = cnt["dve"]
            if DEBUG_TAPS:
                wait("pl", "dve", tw_done)
                emiti("pl", (lambda t: lambda e: e.dma_start(
                    taps["tap_tw"][:, t * 32:(t + 1) * 32], t_twm[:, :]))(t), "st", 16)
            wait("pe", "dve", tw_done)
            for sb in range(4):
                emiti("pe", (lambda sb: lambda e: e.transpose(
                    ps_s1[0:8, sb * P:(sb + 1) * P], t_twm[:, sb * 8:(sb + 1) * 8],
                    t_id[:, :]))(sb), "pe")
            wait("dve", "pe", cnt["pe"])
            emiti("dve", lambda e: e.tensor_copy(t_twT[:, :], ps_s1[0:8, :]), "dve")
            wait("dve", "dve", cnt["dve"])
            stat_free[0] = cnt["dve"]
            twT_done = cnt["dve"]

            etail = t_mtail if t == REAL0 else t_tailE
            if t == REAL0:
                emiti("dve", lambda e: e.tensor_scalar(
                    t_mtail[:, :], t_tailE[:, :], t_z01[:, 0:1], None, ALU.mult), "dve")
                wait("dve", "dve", cnt["dve"])
            for g in range(NG):
                emiti("dve", (lambda g, etail: lambda e: e.tensor_sub(
                    fw(t_A, g)[:, 0:1], fw(t_ei, g)[:, 0:1], etail[:, g:g + 1]))(g, etail), "dve")
                emiti("dve", (lambda g: lambda e: e.tensor_sub(
                    fw(t_A, g)[:, 1:TT], fw(t_ei, g)[:, 1:TT], fw(t_ei, g)[:, 0:TT - 1]))(g), "dve")
            d2_done = cnt["dve"]

            def ld_ewk0(exn):
                return [stream_load(ewkS[hh], wblock_dst(ewkS[hh], 4),
                                    d_ewk[exn, 0, :, hh * 4:(hh + 1) * 4, :]) for hh in range(2)]

            pre_ewk = ld_ewk0(0)
            pre_rkv = [None]
            xk_pre = [0]
            sq_guard = [0, 0]
            for ex in range(E):
                tb = ps_s2 if ex % 2 == 0 else ps_s3
                tb_free = ps2_free if ex % 2 == 0 else ps3_free
                wait("pe", "dve", twT_done)
                wait("pe", "dve", tb_free[0])
                emiti("pe", (lambda ex, tb: lambda e: e.matmul(
                    tb[:, :], t_oh[:, ex * P:(ex + 1) * P], t_twT[:, :],
                    start=True, stop=True))(ex, tb), "pe")
                if ex == 0:
                    wait("dve", "dve", d2_done)
                    wait("dve", "pe", bfree_pe[0])
                    for g in range(NG):
                        emiti("dve", (lambda ex, g: lambda e: e.scalar_tensor_tensor(
                            fw(t_B, g), fw(t_A, g), t_nk[:, g * 8 + ex:g * 8 + ex + 1],
                            fw(t_ei, g), ALU.mult, ALU.add))(ex, g), "dve")
                    xk_pl = cnt["dve"]
                else:
                    xk_pl = xk_pre[0]
                wait("pe", "dve", xk_pl)
                hsq_guard = sq_guard
                for og in range(NGH):
                    bi = next_bank()
                    for hh in range(2):
                        buf = ewkS[hh]
                        if og == 0:
                            ldh = pre_ewk[hh]
                        else:
                            ldh = stream_load(buf, wblock_dst(buf, 4),
                                              d_ewk[ex, og, :, hh * 4:(hh + 1) * 4, :])
                        wait("pe", ldh[0], ldh[1])
                        for kc in range(4):
                            emiti("pe", (lambda buf, kc, bi, hh: lambda e: e.matmul(
                                ps_f[bi][:, :], wchunk(buf, kc), fw(t_B, hh * 4 + kc),
                                start=(hh == 0 and kc == 0), stop=(hh == 1 and kc == 3)))(buf, kc, bi, hh), "pe")
                        consumed(buf, "pe")
                    if og == NGH - 4:
                        pre_rkv[0] = (
                            stream_load(ewrS[0], wblock_dst(ewrS[0], NG),
                                        d_ewr[ex, 0, :, :, :]),
                            [stream_load(ewvS[hh], wblock_dst(ewvS[hh], NG),
                                         d_ewv[ex, 0, :, hh * NG:(hh + 1) * NG, :]) for hh in range(2)])
                    rlb = s_BsN if og % 2 == 0 else s_BsD
                    wait("act", "pe", cnt["pe"])
                    wait("act", "dve", hsq_guard[og % 2])
                    emiti("act", (lambda bi, rlb: lambda e: e.activation(
                        rlb[:, :], ps_f[bi][:, :], AF.Relu))(bi, rlb), "act")
                    bank_consumed(bi, "act")
                    wait("dve", "act", cnt["act"])
                    emiti("dve", (lambda og, rlb: lambda e: e.tensor_mul(
                        fw(t_h, og), rlb[:, :], rlb[:, :]))(og, rlb), "dve")
                    hsq_guard[og % 2] = cnt["dve"]
                h_pe = cnt["pe"]
                h_done = cnt["dve"]
                sq_guard[0] = max(sq_guard[0], h_done)
                sq_guard[1] = max(sq_guard[1], h_done)
                wait("dve", "pe", h_pe)
                for g in range(NG):
                    emiti("dve", (lambda ex, g: lambda e: e.scalar_tensor_tensor(
                        fw(t_B, g), fw(t_A, g), t_nr[:, g * 8 + ex:g * 8 + ex + 1],
                        fw(t_ei, g), ALU.mult, ALU.add))(ex, g), "dve")
                xr_pl = cnt["dve"]
                rsig_guard = sq_guard
                for og in range(NG):
                    bir = next_bank()
                    buf = ewrS[og % 2]
                    if og == 0:
                        ldr = pre_rkv[0][0]
                    else:
                        ldr = stream_load(buf, wblock_dst(buf, NG),
                                          d_ewr[ex, og, :, :, :])
                    wait("pe", ldr[0], ldr[1])
                    if og == 0:
                        wait("pe", "dve", xr_pl)
                    for kc in range(NG):
                        emiti("pe", (lambda buf, kc, bir: lambda e: e.matmul(
                            ps_f[bir][:, :], wchunk(buf, kc), fw(t_B, kc),
                            start=(kc == 0), stop=(kc == NG - 1)))(buf, kc, bir), "pe")
                    consumed(buf, "pe")
                    r_mm_pe = cnt["pe"]
                    if og == NG - 1 and ex < E - 1:
                        wait("dve", "pe", r_mm_pe)
                        for g in range(NG):
                            emiti("dve", (lambda exn, g: lambda e: e.scalar_tensor_tensor(
                                fw(t_B, g), fw(t_A, g), t_nk[:, g * 8 + exn:g * 8 + exn + 1],
                                fw(t_ei, g), ALU.mult, ALU.add))(ex + 1, g), "dve")
                        xk_pre[0] = cnt["dve"]
                    rsb = s_BsN if og % 2 == 0 else s_BsD
                    wait("act", "pe", r_mm_pe)
                    wait("act", "dve", rsig_guard[og % 2])
                    emiti("act", (lambda bir, rsb: lambda e: e.activation(
                        rsb[:, :], ps_f[bir][:, :], AF.Sigmoid))(bir, rsb), "act")
                    bank_consumed(bir, "act")
                    rsig_act = cnt["act"]
                    bikv = next_bank()
                    for hh in range(2):
                        buf = ewvS[hh]
                        if og == 0:
                            ldv2 = pre_rkv[0][1][hh]
                        else:
                            ldv2 = stream_load(buf, wblock_dst(buf, NG),
                                               d_ewv[ex, og, :, hh * NG:(hh + 1) * NG, :])
                        wait("pe", ldv2[0], ldv2[1])
                        if og == 0 and hh == 0:
                            wait("pe", "dve", h_done)
                        for kc in range(NG):
                            emiti("pe", (lambda buf, kc, bikv, hh: lambda e: e.matmul(
                                ps_f[bikv][:, :], wchunk(buf, kc), fw(t_h, hh * NG + kc),
                                start=(hh == 0 and kc == 0), stop=(hh == 1 and kc == NG - 1)))(buf, kc, bikv, hh), "pe")
                        consumed(buf, "pe")
                    if og == 0 and ex < E - 1:
                        pre_ewk = ld_ewk0(ex + 1)
                    wait("dve", "pe", cnt["pe"])
                    wait("dve", "act", rsig_act)
                    emiti("dve", (lambda bikv, rsb: lambda e: e.tensor_mul(
                        s_t1[:, :], ps_f[bikv][:, :], rsb[:, :]))(bikv, rsb), "dve")
                    bank_consumed(bikv, "dve")
                    emiti("dve", (lambda tb: lambda e: e.tensor_mul(
                        s_t2[:, :], s_t1[:, :], tb[:, :]))(tb), "dve")
                    asrc, adst = (t_acc, t_acc2) if ex % 2 == 0 else (t_acc2, t_acc)
                    emiti("dve", (lambda og, asrc, adst: lambda e: e.tensor_add(
                        fw(adst, og), fw(asrc, og), s_t2[:, :]))(og, asrc, adst), "dve")
                    rsig_guard[og % 2] = cnt["dve"]
                bfree_pe[0] = cnt["pe"]
                tb_free[0] = cnt["dve"]
            moe_done = cnt["dve"]

            wait("pl", "dve", moe_done)
            emiti("pl", (lambda t: lambda e: e.dma_start(
                d_out.rearrange("(g p) n -> p g n", p=P)[:, :, (t - REAL0) * TT:(t - REAL0 + 1) * TT],
                fw3(t_acc)))(t), "st", 16)
            store_done[0] = cnt["st"]

        if is_att and t + 1 < NT:
            emiti("dve", lambda e: e.tensor_copy(
                t_tailE[:, :], fw3(t_ei)[:, :, TT - 1:TT]), "dve")
        tile_end_dve[0] = cnt["dve"]

    for t in range(NT):
        emit_tile(t)

    wait("pl", "st", cnt["st"])
    wait("pl", "x", cnt["x"])
    wait("sp", "st", cnt["st"])

    import contextlib
    with contextlib.ExitStack() as _st:
        sem_h = {}
        for ch in cnt:
            sem_h[ch] = _st.enter_context(nc.semaphore(f"s_{ch}"))
        block = _st.enter_context(nc.Block())

        def replay(eng, items):
            for it in items:
                if it[0] == "wait":
                    eng.wait_ge(sem_h[it[1]], it[2])
                else:
                    _, fn, ch, amt = it
                    inst = fn(eng)
                    if ch is not None:
                        inst.then_inc(sem_h[ch], amt)

        @block.tensor
        def _(e):
            replay(e, prog["pe"])

        @block.vector
        def _(e):
            replay(e, prog["dve"])

        @block.scalar
        def _(e):
            replay(e, prog["act"])

        @block.sync
        def _(e):
            replay(e, prog["sp"])

        @block.gpsimd
        def _(e):
            replay(e, prog["pl"])

    return nc


def _pack_pvec(inp):
    pv = np.zeros((P, PCOLS), np.float32)

    def put(name, vec):
        v = np.asarray(vec, np.float32).reshape(NG, P)
        pv[:, PV[name]:PV[name] + NG] = v.T

    put("ln1_g", inp["L0_attn_ln1_g"]); put("ln1_b", inp["L0_attn_ln1_b"])
    put("ln2_g", inp["L0_attn_ln2_g"]); put("ln2_b", inp["L0_attn_ln2_b"])
    put("mu_k", inp["L0_attn_tm_mu_k"]); put("mu_v", inp["L0_attn_tm_mu_v"])
    put("mu_r", inp["L0_attn_tm_mu_r"])
    put("td", inp["L0_attn_time_decay"]); put("tf", inp["L0_attn_time_first"])
    put("ffn_bias", inp["L0_ffn_bias"])
    cmk = np.asarray(inp["L0_attn_cm_mu_k"], np.float32)
    cmr = np.asarray(inp["L0_attn_cm_mu_r"], np.float32)
    for g in range(NG):
        for e in range(E):
            pv[:, PV["cm_k"] + g * 8 + e] = cmk[e, g * P:(g + 1) * P]
            pv[:, PV["cm_r"] + g * 8 + e] = cmr[e, g * P:(g + 1) * P]
    return pv


def kernel(**inputs):
    x = np.asarray(inputs["L0_attn_x"], np.float32)
    B, S, _ = x.shape
    pvec = _pack_pvec(inputs)
    ident = np.eye(P, dtype=np.float32)
    onehot = np.zeros((E, D), np.float32)
    for e in range(E):
        onehot[e, e * P:(e + 1) * P] = 1.0

    def swz(w):
        w = np.asarray(w, np.float32)
        K, M = w.shape
        return np.ascontiguousarray(
            w.reshape(K // P, P, M // P, P).transpose(2, 1, 0, 3))

    def swz_e(w):
        w = np.asarray(w, np.float32)
        _, K, M = w.shape
        return np.ascontiguousarray(
            w.reshape(E, K // P, P, M // P, P).transpose(0, 3, 2, 1, 4))

    shared = dict(
        Wk=swz(inputs["L0_attn_Wk"]),
        Wv=swz(inputs["L0_attn_Wv"]),
        Wr=swz(inputs["L0_attn_Wr"]),
        Wo=swz(inputs["L0_attn_Wo"]),
        router_w_s=np.ascontiguousarray(
            np.asarray(inputs["L0_attn_router_w"], np.float32)
            .reshape(NG, P, E).transpose(1, 0, 2).reshape(P, NG * E)),
        eWk=swz_e(inputs["L0_attn_eWk"]),
        eWv=swz_e(inputs["L0_attn_eWv"]),
        eWr=swz_e(inputs["L0_attn_eWr"]),
        ffk=np.ascontiguousarray(inputs["L0_ffn_kernel"], np.float32),
        ffa_s=np.ascontiguousarray(
            np.asarray(inputs["L0_ffn_a"], np.float32)
            .reshape(NG, P, 16).transpose(1, 0, 2).reshape(P, NG * 16)),
        ffb=np.ascontiguousarray(inputs["L0_ffn_b"], np.float32),
        pvec=pvec, cident=ident,
        conescol=np.ones((P, 1), np.float32),
        conesrow=np.ones((1, P), np.float32),
        cinv1024=np.full((1, P), 1.0 / D, np.float32),
        conehot=onehot,
        czcol=np.zeros((P, 1), np.float32),
        ceps=np.full((P, 1), EPS, np.float32),
        czc16=np.zeros((P, 16), np.float32),
    )

    in_maps = []
    for c in range(8):
        b, q = c // 4, c % 4
        s0 = q * LC
        lo = s0 - HALO
        if lo < 0:
            xs = np.concatenate(
                [np.zeros((D, -lo), np.float32),
                 np.ascontiguousarray(x[b, 0:s0 + LC].T)], axis=1)
        else:
            xs = np.ascontiguousarray(x[b, lo:s0 + LC].T)
        first = 1.0 if q > 0 else 0.0
        m = dict(shared)
        m["xT"] = np.ascontiguousarray(xs)
        m["cme"] = np.full((P, 1), first, np.float32)
        m["cz01"] = np.full((P, 1), first, np.float32)
        in_maps.append(m)

    nc = build()
    res = run_bass_kernel_spmd(nc, in_maps, core_ids=list(range(8)))

    out = np.empty((B, S, D), np.float32)
    for c in range(8):
        b, q = c // 4, c % 4
        out[b, q * LC:(q + 1) * LC] = res.results[c]["outT"].T
    if DEBUG_TAPS:
        kernel.last_results = res
    return out
